# version 09
# baseline (speedup 1.0000x reference)
import math
from contextlib import ExitStack

import numpy as np

import concourse.bass as bass
import concourse.mybir as mybir
from concourse.bass_types import AP
from concourse.bass_utils import run_bass_kernel_spmd

F32 = mybir.dt.float32
BF16 = mybir.dt.bfloat16
AF = mybir.ActivationFunctionType
ALU = mybir.AluOpType
AX = mybir.AxisListType
ds = bass.ds

B = 2
N = 8192
D = 1024
H = 16
DH = 64
BM = 32
TR = N // BM
TOPK = 51
NCORES = 8
TOK = N * B // NCORES
SEQC = 4

FH, FD = 8, 64
FI = FH * FD
WIN = 512
HID = int(D * 4 * 2 // 3)
QK_SCALE = 8.0

NEG = -1.0e30
LN_EPS = 1e-5
DEN_EPS = 1e-6
QB = 127.0
BIT_EPS = 1e-5
RMAGIC = 12582912.0


def _bf16(x):
    import ml_dtypes
    return np.ascontiguousarray(np.asarray(x).astype(ml_dtypes.bfloat16))


DMA_STREAMS = {"dsync": "sync", "dgps": "gpsimd"}
ENGS = ("tensor", "vector", "scalar", "gpsimd", "sync")


class Op:
    __slots__ = ("stream", "fn", "reads", "writes", "idx", "deps",
                 "inc_needed", "is_dma", "tag", "same_raw")

    def __init__(self, stream, fn, reads, writes, idx, is_dma, tag):
        self.stream, self.fn = stream, fn
        self.reads, self.writes = reads, writes
        self.idx = idx
        self.deps = {}
        self.inc_needed = False
        self.is_dma = is_dma
        self.tag = tag
        self.same_raw = -1


class Prog:
    def __init__(self, name):
        self.nc = bass.Bass("TRN2", target_bir_lowering=False, debug=False)
        self.es = ExitStack()
        self.name = name

    def dram_in(self, name, shape, dt=F32):
        return self.nc.dram_tensor(name, list(shape), dt, kind="ExternalInput").ap()

    def dram_out(self, name, shape, dt=F32):
        return self.nc.dram_tensor(name, list(shape), dt, kind="ExternalOutput").ap()

    def sb(self, name, shape, dt=F32):
        return self.es.enter_context(self.nc.sbuf_tensor(name, list(shape), dt))

    def ps(self, name, shape):
        return self.es.enter_context(self.nc.psum_tensor(name, list(shape), F32))

    def sem(self, name):
        return self.es.enter_context(self.nc.semaphore(name))


class Sched:
    def __init__(self, prog):
        self.p = prog
        self.ops = {s: [] for s in list(ENGS) + list(DMA_STREAMS)}
        self.last_w = {}
        self.readers = {}

    @staticmethod
    def _names(lst):
        out = []
        for x in lst or ():
            if x is None:
                continue
            out.append(x.tensor.name if isinstance(x, AP) else
                       (x.name if hasattr(x, "name") else str(x)))
        return out

    def add(self, stream, fn, reads=(), writes=(), tag=""):
        idx = len(self.ops[stream])
        op = Op(stream, fn, self._names(reads), self._names(writes), idx,
                stream in DMA_STREAMS, tag)
        deps = {}

        def _dep(s, i, raw=False):
            if s == stream and s not in DMA_STREAMS:
                if raw:
                    op.same_raw = max(op.same_raw, i)
                return
            key = (s, i % Sched.DMA_POOL) if s in DMA_STREAMS else (s, None)
            if deps.get(key, -1) < i:
                deps[key] = i

        for b in op.reads:
            for (s2, _sl), i in self.last_w.get(b, {}).items():
                _dep(s2, i, raw=True)
        for b in op.writes:
            for (s2, _sl), i in self.last_w.get(b, {}).items():
                _dep(s2, i, raw=True)
            for r in self.readers.get(b, ()):
                _dep(r[0], r[1], raw=True)
        op.deps = deps
        for (s2, _slot), i in deps.items():
            self.ops[s2][i].inc_needed = True
        key = ((stream, idx % Sched.DMA_POOL) if op.is_dma
               else (stream, None))
        for b in op.writes:
            old = self.last_w.get(b, {})
            kept = {k2: i2 for k2, i2 in old.items()
                    if k2[0] == stream and k2 != key}
            kept[key] = idx
            self.last_w[b] = kept
            self.readers[b] = []
        for b in op.reads:
            self.readers.setdefault(b, []).append((stream, idx))
        self.ops[stream].append(op)
        return op

    def mm(self, psum, lhsT, rhs, start, stop, extra_reads=(), tag="mm"):
        self.add("tensor",
                 lambda t: t.matmul(psum, lhsT, rhs, start=start, stop=stop),
                 reads=[lhsT, rhs] + list(extra_reads), writes=[psum], tag=tag)

    def transpose(self, psum, in_, ident, tag="tr"):
        self.add("tensor", lambda t: t.transpose(psum, in_, ident),
                 reads=[in_, ident], writes=[psum], tag=tag)

    def dve(self, meth, out, *ins, extra_reads=(), extra_writes=(), tag="", **kw):
        self.add("vector", lambda v: getattr(v, meth)(out, *ins, **kw),
                 reads=list(ins) + list(extra_reads),
                 writes=[out] + list(extra_writes), tag=tag or meth)

    def act(self, out, in_, func, bias=0.0, scale=1.0, accum=None, tag="act"):
        extra = [b for b in (bias, scale) if isinstance(b, AP)]

        def fn(s):
            kw = dict(bias=bias, scale=scale)
            if accum is not None:
                kw["accum_out"] = accum
            return s.activation(out, in_, func, **kw)
        self.add("scalar", fn, reads=[in_] + extra,
                 writes=[out] + ([accum] if accum is not None else []), tag=tag)

    def scopy(self, out, in_, tag="scopy"):
        self.add("scalar", lambda s: s.copy(out, in_), reads=[in_],
                 writes=[out], tag=tag)

    def dma(self, out, in_, q="dsync", tag="dma", slow=False):
        if slow:
            self.add(q, lambda e: e.dma_start(
                out=out, in_=in_, allow_slow_non_contiguous=True),
                reads=[in_], writes=[out], tag=tag)
        else:
            self.add(q, lambda e: e.dma_start(out=out, in_=in_),
                     reads=[in_], writes=[out], tag=tag)

    DMA_POOL = 8

    def emit(self):
        p = self.p
        nc = p.nc
        P = self.DMA_POOL
        live = {s: lst for s, lst in self.ops.items() if lst}
        sems = {}
        for s in live:
            if s in DMA_STREAMS:
                sems[s] = [p.sem(f"S_{s}_{j}") for j in range(P)]
            else:
                sems[s] = p.sem("S_" + s)

        prefix = {}
        for s, lst in live.items():
            if s in DMA_STREAMS:
                continue
            acc, pl = 0, []
            for op in lst:
                if op.inc_needed or op is lst[-1]:
                    acc += 1
                pl.append(acc)
            prefix[s] = pl

        def dep_wait(eng, s2, i, satisfied):
            if s2 in DMA_STREAMS:
                key = (s2, i % P)
                need = 16 * (i // P + 1)
                sem = sems[s2][i % P]
            else:
                key = s2
                need = prefix[s2][i]
                assert (self.ops[s2][i].inc_needed or
                        self.ops[s2][i] is live[s2][-1])
                sem = sems[s2]
            if satisfied.get(key, 0) < need:
                eng.wait_ge(sem, need)
                satisfied[key] = need

        with nc.Block() as blk:
            for s, lst in live.items():
                eng_name = DMA_STREAMS.get(s, s)
                deco = getattr(blk, eng_name)

                def make(s=s, lst=lst, eng_name=eng_name):
                    def body(eng):
                        satisfied = {}
                        dve_like = eng_name in ("vector", "scalar", "gpsimd")
                        is_dma_stream = s in DMA_STREAMS
                        flushed = 0
                        for op in lst:
                            for key in sorted(op.deps, key=str):
                                dep_wait(eng, key[0], op.deps[key], satisfied)
                            if is_dma_stream and op.idx >= P:
                                j = op.idx % P
                                eng.wait_ge(sems[s][j], 16 * (op.idx // P))
                            if dve_like and op.same_raw >= flushed:
                                eng.drain()
                                flushed = op.idx
                            inst = op.fn(eng)
                            if is_dma_stream:
                                inst.then_inc(sems[s][op.idx % P], 16)
                            else:
                                if op.inc_needed or op is lst[-1]:
                                    inst.then_inc(sems[s], 1)
                    return body

                deco(make())

            @blk.sync
            def _(sy):
                for s, lst in sorted(live.items()):
                    n = len(lst)
                    if s in DMA_STREAMS:
                        for j in range(min(n, P)):
                            cnt = (n - 1 - j) // P + 1
                            sy.wait_ge(sems[s][j], 16 * cnt)
                    else:
                        sy.wait_ge(sems[s], prefix[s][-1])

        return nc


def fm_view(dram_ap):
    return dram_ap.rearrange("(c p) t -> p c t", p=128)


def emit_split(s, hi, lo, src):
    s.dve("tensor_copy", hi, src, tag="split_hi")
    s.dve("tensor_sub", lo, src, hi, tag="split_lo")


def emit_proj3(s, psum, whi, wlo, xhi, xlo, nk, start=True, stop=True,
               tag="proj"):
    first = start
    for c in range(nk):
        s.mm(psum, whi(c), xhi(c), first, False, tag=tag)
        first = False
        s.mm(psum, whi(c), xlo(c), False, False, tag=tag)
        s.mm(psum, wlo(c), xhi(c), False, stop and (c == nk - 1), tag=tag)


def build_A():
    p = Prog("A")
    s = Sched(p)
    x0 = p.dram_in("x0T", [D, TOK])
    w_in = [p.dram_in(n, [D, D]) for n in ("wq", "wk", "wv")]
    qp = p.dram_out("qp", [128, H, TOK], BF16)
    kp = p.dram_out("kp", [128, H, TOK], BF16)
    vfull = p.dram_out("vfull", [TOK, H, DH])
    xpo = p.dram_out("xp", [D, TOK // BM])

    NT = 512
    ntt = TOK // NT
    xsc = [p.sb(f"xsc{i}", [128, TOK]) for i in range(2)]
    xhi = p.sb("xhi", [128, 8, TOK], BF16)
    xlo = p.sb("xlo", [128, 8, TOK], BF16)
    xpt = p.sb("xpt", [128, 8, TOK // BM])
    wbuf = [p.sb(f"wbuf{i}", [128, 8, 512]) for i in range(2)]
    whi = p.sb("whi", [128, 8, 1024], BF16)
    wlo = p.sb("wlo", [128, 8, 1024], BF16)
    evq = [p.sb(f"evq{i}", [128, NT]) for i in range(2)]
    evk = [p.sb(f"evk{i}", [128, NT]) for i in range(2)]
    evv = [p.sb(f"evv{i}", [128, NT]) for i in range(2)]
    hiq = [p.sb(f"hiq{i}", [128, NT], BF16) for i in range(2)]
    loq = [p.sb(f"loq{i}", [128, NT], BF16) for i in range(2)]
    hik = [p.sb(f"hik{i}", [128, NT], BF16) for i in range(2)]
    lok = [p.sb(f"lok{i}", [128, NT], BF16) for i in range(2)]
    hiv = [p.sb(f"hiv{i}", [128, NT], BF16) for i in range(2)]
    lov = [p.sb(f"lov{i}", [128, NT], BF16) for i in range(2)]
    pq = [p.ps(f"pq{i}", [128, NT]) for i in range(2)]
    pk = [p.ps(f"pk{i}", [128, NT]) for i in range(2)]
    pv = [p.ps(f"pv{i}", [128, NT]) for i in range(2)]

    for c in range(8):
        xb = xsc[c % 2]
        s.dma(xb[:], fm_view(x0)[:, c, :], tag=f"ldx{c}")
        emit_split(s, xhi[:, c, :], xlo[:, c, :], xb[:])
        s.dve("reduce_sum", xpt[:, c, :],
              xb[:].rearrange("p (r w) -> p r w", w=BM), axis=AX.X, tag="pool")
    s.dve("tensor_scalar_mul", xpt[:], xpt[:], 1.0 / BM, tag="poolscale")
    s.dma(fm_view(xpo), xpt[:], tag="stxp")

    for wi, wsrc in enumerate(w_in):
        for hf in range(2):
            wb = wbuf[hf]
            cols = slice(hf * 512, (hf + 1) * 512)
            s.dma(wb[:], wsrc[:, cols].rearrange("(c p) o -> p c o", p=128),
                  q="dgps", tag=f"ldw{wi}_{hf}")
            emit_split(s, whi[:, :, cols], wlo[:, :, cols], wb[:])

        if wi < 2:
            out_d = (qp, kp)
            ev = (evq, evk)[wi]
            hi_t = (hiq, hik)[wi]
            lo_t = (loq, lok)[wi]
            pp = (pq, pk)[wi]
            for oc in range(8):
                for tt in range(ntt):
                    bi = (oc * ntt + tt) % 2
                    sl = slice(tt * NT, (tt + 1) * NT)
                    ocs = slice(oc * 128, (oc + 1) * 128)
                    emit_proj3(
                        s, pp[bi][:],
                        lambda c: whi[:, c, ocs], lambda c: wlo[:, c, ocs],
                        lambda c: xhi[:, c, sl], lambda c: xlo[:, c, sl], 8,
                        tag=f"proj{wi}")
                    s.scopy(ev[bi][:], pp[bi][:], tag="evac")
                    emit_split(s, hi_t[bi][:], lo_t[bi][:], ev[bi][:])
                    for hh in range(2):
                        h_ = 2 * oc + hh
                        rows = slice(hh * 64, (hh + 1) * 64)
                        s.dma(out_d[wi][0:64, h_, sl], hi_t[bi][rows, :],
                              q="dgps", tag="stqk_hi")
                        s.dma(out_d[wi][64:128, h_, sl], lo_t[bi][rows, :],
                              q="dgps", tag="stqk_lo")
        else:
            for tt in range(TOK // 128):
                for half in range(2):
                    bi = (tt * 2 + half) % 2
                    ocs = slice(half * 512, (half + 1) * 512)
                    tsl = slice(tt * 128, (tt + 1) * 128)
                    first = True
                    for c in range(8):
                        s.mm(pv[bi][:], xhi[:, c, tsl], whi[:, c, ocs],
                             first, False, tag="projV")
                        first = False
                        s.mm(pv[bi][:], xhi[:, c, tsl], wlo[:, c, ocs],
                             False, False, tag="projV")
                        s.mm(pv[bi][:], xlo[:, c, tsl], whi[:, c, ocs],
                             False, c == 7, tag="projV")
                    s.scopy(evv[bi][:], pv[bi][:], tag="evacV")
                    h0 = half * 8
                    s.dma(vfull[tsl, h0:h0 + 8, :],
                          evv[bi][:].rearrange("t (h d) -> t h d", d=DH),
                          q="dgps", tag="stv")

    nc = s.emit()
    p.es.close()
    return nc


XT = 3584
MARG = 1536
KV0 = 512
Q0 = 1024
NQW = 5
HID_P = 2816
NHC = HID_P // 128


def build_C(block_ln, dbg=False):
    p = Prog("C")
    s = Sched(p)
    nc = p.nc
    xE = p.dram_in("xET", [D, XT])
    x1 = p.dram_in("x1T", [D, TOK])
    wqkv = p.dram_in("wqkv", [D, 3 * FI])
    wo = p.dram_in("wo", [FI, D])
    w1p = p.dram_in("w1p", [D, 2, HID_P])
    w2p = p.dram_in("w2p", [HID_P, D])
    qs128 = p.dram_in("qs128", [128, 1])
    ks128 = p.dram_in("ks128", [128, 1])
    lg128 = p.dram_in("lg128", [128, 8])
    lb128 = p.dram_in("lb128", [128, 8])
    b1a128 = p.dram_in("b1a128", [128, NHC])
    b1g128 = p.dram_in("b1g128", [128, NHC])
    b2c = p.dram_in("b2c", [128, 8])
    tabs = {n: p.dram_in(n, [128, XT]) for n in ("csq", "snq", "csk", "snk")}
    edgeneg = p.dram_in("edgeneg", [1, 1])
    edgezero = p.dram_in("edgezero", [128, 1])
    trimask = p.dram_in("trimask", [128, 128], BF16)
    expand2 = p.dram_in("expand2", [2, 128])
    ones2h = p.dram_in("ones2h", [128, 2])
    rotk = p.dram_in("rotk", [128, 128])
    idb = p.dram_in("idb", [128, 512], BF16)
    zid = p.dram_in("zid", [128, 640], BF16)
    tri2m = p.dram_in("tri2m", [128, 128], BF16)
    xo = p.dram_out("xoT", [D, TOK])

    q2D = nc.dram_tensor("q2D", [FH, 128, XT - Q0], BF16, kind="Internal").ap()
    qlD = nc.dram_tensor("qlD", [FH, 64, XT - Q0], BF16, kind="Internal").ap()
    kD = nc.dram_tensor("kD", [FH, 128, XT - KV0], BF16, kind="Internal").ap()
    vD = nc.dram_tensor("vD", [FH, 2, XT - KV0, FD], BF16, kind="Internal").ap()
    attD = nc.dram_tensor("attD", [128, FI // 128, 2, XT - Q0], BF16,
                          kind="Internal").ap()
    xmD = nc.dram_tensor("xmD", [D, TOK + 1], F32, kind="Internal").ap()
    gluD = nc.dram_tensor("gluD", [NHC, 2, 128, TOK], BF16,
                          kind="Internal").ap()
    lnhD = nc.dram_tensor("lnhD", [2, D, TOK], BF16, kind="Internal").ap()

    xsp = [p.sb(f"xsp{i}", [128, 513]) for i in range(2)]
    hhi = [p.sb(f"hhi{i}", [128, 8, 512], BF16) for i in range(2)]
    hlo = [p.sb(f"hlo{i}", [128, 8, 512], BF16) for i in range(2)]
    wfp = p.sb("wfp", [128, 8, 512])
    whi = p.sb("whi", [128, 8, 512], BF16)
    wlo = p.sb("wlo", [128, 8, 512], BF16)
    qs_s = p.sb("qs_s", [128, 1])
    ks_s = p.sb("ks_s", [128, 1])
    lg_s = p.sb("lg_s", [128, 8])
    lb_s = p.sb("lb_s", [128, 8])
    b1a_s = p.sb("b1a_s", [128, NHC])
    b1g_s = p.sb("b1g_s", [128, NHC])
    b2_s = p.sb("b2_s", [128, 8])
    tri_s = p.sb("tri_s", [128, 128], BF16)
    ex2_s = p.sb("ex2_s", [2, 128])
    on2_s = p.sb("on2_s", [128, 2])
    rot_s = p.sb("rot_s", [128, 128])
    idb_s = p.sb("idb_s", [128, 512], BF16)
    zid_s = p.sb("zid_s", [128, 640], BF16)
    tri2_s = p.sb("tri2_s", [128, 128], BF16)
    en_s = p.sb("en_s", [1, 1])
    ez_s = p.sb("ez_s", [128, 1])
    tab_s = {n: [p.sb(f"{n}{i}", [128, 512]) for i in range(2)]
             for n in ("csq", "snq", "csk", "snk")}
    ones_c = p.sb("ones_c", [128, 1])
    ones_r = p.sb("ones_r", [1, 128])
    ev = [p.sb(f"ev{i}", [128, 512]) for i in range(2)]
    ev2 = [p.sb(f"ev2{i}", [128, 512]) for i in range(2)]
    evh = [p.sb(f"evh{i}", [128, 512], BF16) for i in range(2)]
    evl = [p.sb(f"evl{i}", [128, 512], BF16) for i in range(2)]
    sqv = p.sb("sqv", [128, 512])
    n2 = p.sb("n2", [2, 512])
    kpx = [p.sb(f"kpx{i}", [128, XT - KV0], BF16) for i in range(2)]
    vstg = p.sb("vstg", [128, (XT - KV0) // 128, 2, 65], BF16)
    q2t = p.sb("q2t", [128, 512], BF16)
    qlt = p.sb("qlt", [64, 512], BF16)
    pex = [p.sb(f"pex{i}", [128, 512]) for i in range(2)]
    phh = [p.sb(f"phh{i}", [128, 512], BF16) for i in range(2)]
    pll = [p.sb(f"pll{i}", [128, 512], BF16) for i in range(2)]
    rden = p.sb("rden", [1, 512])
    rdex = p.sb("rdex", [64, 512])
    ahat = p.sb("ahat", [64, 512])
    ah_h = p.sb("ah_h", [64, 512], BF16)
    ah_l = p.sb("ah_l", [64, 512], BF16)
    ahld = [p.sb(f"ahld{i}", [128, 2, 512], BF16) for i in range(2)]
    lnhh = [p.sb(f"lnhh{i}", [128, 8, 512], BF16) for i in range(2)]
    lnhl = [p.sb(f"lnhl{i}", [128, 8, 512], BF16) for i in range(2)]
    stats = p.sb("stats", [1, 2, TOK])
    glh = p.sb("glh", [128, 512], BF16)
    gll = p.sb("gll", [128, 512], BF16)
    gel = p.sb("gel", [128, 512])
    asb = p.sb("asb", [128, 512])

    ps1024 = [p.ps(f"ps1024_{i}", [128, 1024]) for i in range(2)]
    ps512 = [p.ps(f"ps512_{i}", [128, 512]) for i in range(4)]

    for src_, dst in ((qs128, qs_s), (ks128, ks_s), (lg128, lg_s),
                      (lb128, lb_s), (b1a128, b1a_s), (b1g128, b1g_s),
                      (b2c, b2_s), (trimask, tri_s), (expand2, ex2_s),
                      (ones2h, on2_s), (edgeneg, en_s), (edgezero, ez_s),
                      (rotk, rot_s), (idb, idb_s), (zid, zid_s),
                      (tri2m, tri2_s)):
        s.dma(dst[:], src_[tuple([slice(None)] * len(src_.shape))], tag="ldc")
    s.dve("memset", ones_c[:], 1.0, tag="onesc")
    s.dve("memset", ones_r[:], 1.0, tag="onesr")
    s.dve("memset", vstg[:, :, 0, 64:65], 1.0, tag="vst1")
    s.dve("memset", vstg[:, :, 1, 64:65], 0.0, tag="vst0")

    NTT = (XT - KV0) // 512

    def load_x_tile(tt, tb):
        t0 = KV0 + tt * 512
        for c in range(8):
            s.dma(xsp[tb][:, 0:513],
                  xE.rearrange("(c p) t -> p c t", p=128)[:, c,
                                                          t0 - 1:t0 + 512],
                  tag="ldxc")
            off = 1 if c < 4 else 0
            emit_split(s, hhi[tb][:, c, :], hlo[tb][:, c, :],
                       xsp[tb][:, off:off + 512])

    p1iters = []
    for kind_ in range(3):
        for tt_ in range(NTT):
            if kind_ == 0 and KV0 + tt_ * 512 < Q0:
                continue
            p1iters.append((kind_, tt_))
    cur_kind = [-1]
    for it_i, (kind, tt) in enumerate(p1iters):
        t0 = KV0 + tt * 512
        tb = tt % 2
        if kind != cur_kind[0]:
            cols = slice(kind * 512, (kind + 1) * 512)
            s.dma(wfp[:], wqkv[:, cols].rearrange("(c p) o -> p c o", p=128),
                  q="dgps", tag=f"ldwqkv{kind}")
            emit_split(s, whi[:], wlo[:], wfp[:])
            cur_kind[0] = kind
        if it_i == 0:
            load_x_tile(tt, tb)
        if it_i + 1 < len(p1iters):
            ntt_ = p1iters[it_i + 1][1]
            load_x_tile(ntt_, ntt_ % 2)
        if True:
            if kind < 2:
                for n_ in (("csq", "snq") if kind == 0 else ("csk", "snk")):
                    s.dma(tab_s[n_][tb][:], tabs[n_][:, t0:t0 + 512],
                          q="dgps", tag="ldtab")
                def qk_chain(oc, kind, tb, t0):
                    pp = ps512[oc % 2][:, 0:512]
                    eb = ev[oc % 2]
                    sc_ap = (qs_s if kind == 0 else ks_s)[:]
                    s.act(eb[:], pp, AF.Square, tag="sq")
                    pn = ps1024[0][0:2, 0:512]
                    s.mm(pn, on2_s[:], eb[:], True, True, tag="n2mm")
                    s.act(n2[:], pn, AF.Sqrt, tag="nsqrt")
                    s.dve("tensor_scalar_max", n2[:], n2[:], 1e-12,
                          tag="nclip")
                    s.dve("reciprocal", n2[:], n2[:], tag="nrec")
                    pn2 = ps1024[0][0:128, 0:512]
                    s.mm(pn2, ex2_s[:], n2[:], True, True, tag="nexp")
                    s.dve("tensor_scalar_mul", eb[:], pp, sc_ap,
                          tag="qscale")
                    s.dve("tensor_mul", eb[:], eb[:], pn2, tag="qnorm")
                    prot = ps512[2 + oc % 2][:, 0:512]
                    s.mm(prot, rot_s[:], eb[:], True, True, tag="rotmm")
                    cs_t = tab_s["csq" if kind == 0 else "csk"][tb]
                    sn_t = tab_s["snq" if kind == 0 else "snk"][tb]
                    e2 = ev2[oc % 2]
                    s.dve("tensor_mul", e2[:], eb[:], cs_t[:], tag="qcs")
                    s.dve("tensor_mul", eb[:], prot, sn_t[:], tag="qsn")
                    s.dve("tensor_add", e2[:], e2[:], eb[:], tag="qfin")
                    emit_split(s, evh[oc % 2][:], evl[oc % 2][:], e2[:])
                    for hh_ in range(2):
                        h_ = 2 * oc + hh_
                        rows = slice(hh_ * 64, (hh_ + 1) * 64)
                        if kind == 0:
                            dsl = slice(t0 - Q0, t0 - Q0 + 512)
                            s.dma(q2D[h_, 0:64, dsl], evh[oc % 2][rows, :],
                                  q="dgps", tag="stq2a")
                            s.dma(q2D[h_, 64:128, dsl], evh[oc % 2][rows, :],
                                  q="dgps", tag="stq2b")
                            s.dma(qlD[h_, :, dsl], evl[oc % 2][rows, :],
                                  q="dgps", tag="stql")
                        else:
                            dsl = slice(t0 - KV0, t0 - KV0 + 512)
                            s.dma(kD[h_, 0:64, dsl], evh[oc % 2][rows, :],
                                  q="dgps", tag="stk_h")
                            s.dma(kD[h_, 64:128, dsl], evl[oc % 2][rows, :],
                                  q="dgps", tag="stk_l")

                pchain = [None]
                for oc in range(4):
                    pp = ps512[oc % 2][:, 0:512]
                    ocs = slice(oc * 128, (oc + 1) * 128)
                    emit_proj3(s, pp,
                               lambda c: whi[:, c, ocs],
                               lambda c: wlo[:, c, ocs],
                               lambda c: hhi[tb][:, c, :],
                               lambda c: hlo[tb][:, c, :], 8, tag="qkproj")
                    if pchain[0] is not None:
                        qk_chain(*pchain[0])
                    pchain[0] = (oc, kind, tb, t0)
                qk_chain(*pchain[0])
            else:
                for sub in range(4):
                    pp = ps512[sub % 2][:, 0:512]
                    tsl2 = slice(sub * 128, (sub + 1) * 128)
                    first = True
                    for c in range(8):
                        s.mm(pp, hhi[tb][:, c, tsl2], whi[:, c, :],
                             first, False, tag="vproj")
                        first = False
                        s.mm(pp, hhi[tb][:, c, tsl2], wlo[:, c, :],
                             False, False, tag="vproj")
                        s.mm(pp, hlo[tb][:, c, tsl2], whi[:, c, :],
                             False, c == 7, tag="vproj")
                    s.scopy(ev[sub % 2][:], pp, tag="vev")
                    emit_split(s, evh[sub % 2][:], evl[sub % 2][:],
                               ev[sub % 2][:])
                    vt0 = t0 - KV0 + sub * 128
                    for h_ in range(FH):
                        hc_ = slice(h_ * FD, (h_ + 1) * FD)
                        s.dma(vD[h_, 0, vt0:vt0 + 128, :],
                              evh[sub % 2][:, hc_], q="dgps", tag="stv_h")
                        s.dma(vD[h_, 1, vt0:vt0 + 128, :],
                              evl[sub % 2][:, hc_], q="dgps", tag="stv_l")

    if dbg:
        for nm, t_, sh in (("dbg_q2", q2D, [FH, 128, XT - Q0]),
                           ("dbg_k", kD, [FH, 128, XT - KV0]),
                           ("dbg_v", vD, [FH, 2, XT - KV0, FD])):
            o_ = p.dram_out(nm, sh, BF16)
            s.dma(o_[tuple(slice(None) for _ in sh)],
                  t_[tuple(slice(None) for _ in sh)], q="dgps", tag="dbg")
    NVC = (XT - KV0) // 128
    for h in range(FH):
        hb = h % 2
        s.dma(kpx[hb][:], kD[h, :, :], tag="ldkp")
        s.dma(vstg[:, :, 0, 0:64],
              vD[h, 0, :, :].rearrange("(kc pp) d -> pp kc d", pp=128),
              tag="ldvh")
        s.dma(vstg[:, :, 1, 0:64],
              vD[h, 1, :, :].rearrange("(kc pp) d -> pp kc d", pp=128),
              tag="ldvl")
        for w in range(2, 7):
            qb0 = w * 512
            qdsl = slice(qb0 - Q0, qb0 - Q0 + 512)
            s.dma(q2t[:], q2D[h, :, qdsl], q="dgps", tag="ldq2")
            s.dma(qlt[:], qlD[h, :, qdsl], q="dgps", tag="ldql2")
            pvp = ps512[0][0:65, 0:512]
            kb0 = (w - 1) * 512 - KV0
            order = [3, 2, 1, 0, 4, 5, 6, 7]
            prevs = []
            for oi, kci in enumerate(order):
                kb = oi % 2
                stp = ps1024[kb][:, 0:512]
                if kci < 4:
                    vld = 0
                    vend = kci * 128 + 128
                    z0 = 512 - kci * 128
                    s.mm(stp[:, 0:vend], tri2_s[:], zid_s[:, z0:z0 + vend],
                         True, False, tag="tri2mm")
                    if w == 3:
                        s.mm(stp[:, 0:vend], ones_r[:],
                             AP(en_s[:].tensor, 0, [[1, 1], [0, vend]]),
                             False, False, extra_reads=[en_s[:]],
                             tag="edgemm")
                else:
                    vld = (kci - 4) * 128
                    vend = 512
                    s.mm(stp[:, vld:512], tri_s[:], idb_s[:, 0:512 - vld],
                         True, False, tag="trimm")
                vsl = slice(vld, vend)
                ksl = slice(kb0 + kci * 128, kb0 + kci * 128 + 128)
                s.mm(stp[:, vsl], kpx[hb][:, ksl], q2t[:, vsl],
                     False, False, tag="fqkA")
                s.mm(stp[:, vsl], kpx[hb][0:64, ksl], qlt[:, vsl],
                     False, True, tag="fqkB")
                s.act(pex[kb][:, vsl], stp[:, vsl], AF.Exp, scale=QK_SCALE,
                      tag="fexp")
                s.act(phh[kb][:, vsl], stp[:, vsl], AF.Exp, scale=QK_SCALE,
                      tag="fexph")
                s.dve("tensor_sub", pll[kb][:, vsl], pex[kb][:, vsl],
                      phh[kb][:, vsl], tag="fplo")
                kcv = kb0 // 128 + kci
                prevs.append((oi, kb, vsl, kcv))
                emits = []
                if oi > 0:
                    emits.append(prevs.pop(0))
                if oi == 7:
                    emits.append(prevs.pop(0))
                for poi, pb_, pvsl, pkcv in emits:
                    s.mm(pvp[:, pvsl], vstg[:, pkcv, 0, :],
                         phh[pb_][:, pvsl], poi == 0, False, tag="fpvA")
                    s.mm(pvp[:, pvsl], vstg[:, pkcv, 0, :],
                         pll[pb_][:, pvsl], False, False, tag="fpvB")
                    s.mm(pvp[:, pvsl], vstg[:, pkcv, 1, :],
                         phh[pb_][:, pvsl], False, poi == 7, tag="fpvC")
            s.dve("reciprocal", rden[:], pvp[64:65, :], tag="frecip")
            pr = ps512[2][0:64, 0:512]
            s.mm(pr, ones_r[0:1, 0:64], rden[:], True, True, tag="frdex")
            s.scopy(rdex[:], pr, tag="frdev")
            s.dve("tensor_mul", ahat[:], pvp[0:64, :], rdex[:], tag="fahat")
            emit_split(s, ah_h[:], ah_l[:], ahat[:])
            hp = h // 2
            r0 = (h % 2) * 64
            s.dma(attD[r0:r0 + 64, hp, 0, qdsl], ah_h[:], q="dgps", tag="fsah")
            s.dma(attD[r0:r0 + 64, hp, 1, qdsl], ah_l[:], q="dgps", tag="fsal")

    if dbg:
        o_ = p.dram_out("dbg_att", [128, FI // 128, 2, XT - Q0], BF16)
        s.dma(o_[:, :, :, :], attD[:, :, :, :], q="dgps", tag="dbga")
    whf = whi[:].rearrange("p a b -> p (a b)")
    wlf = wlo[:].rearrange("p a b -> p (a b)")

    def wo_hi(ic, oc):
        return whf[:, ic * 1024 + oc * 128:ic * 1024 + (oc + 1) * 128]

    def wo_lo(ic, oc):
        return wlf[:, ic * 1024 + oc * 128:ic * 1024 + (oc + 1) * 128]

    wfp_f = wfp[:].rearrange("p a b -> p (a b)")
    for ic in range(4):
        s.dma(wfp_f[:, 0:1024], wo[ic * 128:(ic + 1) * 128, :], q="dgps",
              tag="ldwo")
        emit_split(s, whf[:, ic * 1024:(ic + 1) * 1024],
                   wlf[:, ic * 1024:(ic + 1) * 1024], wfp_f[:, 0:1024])

    def emit_wo(c0, length, tb):
        qsl = slice(c0 - Q0, c0 - Q0 + length)
        for ic in range(4):
            ab_ = ahld[ic % 2]
            s.dma(ab_[:, :, 0:length], attD[:, ic, :, qsl], tag="ldatt",
                  slow=(length == 1))
            for oc in range(8):
                pb = (ps512[oc][:, 0:length] if oc < 4 else
                      ps1024[(oc - 4) // 2][:, ((oc - 4) % 2) * 512:
                                            ((oc - 4) % 2) * 512 + length])
                s.mm(pb, wo_hi(ic, oc), ab_[:, 0, 0:length], ic == 0, False,
                     tag="woA")
                s.mm(pb, wo_hi(ic, oc), ab_[:, 1, 0:length], False, False,
                     tag="woB")
                s.mm(pb, wo_lo(ic, oc), ab_[:, 0, 0:length], False, ic == 3,
                     tag="woC")
        for oc in range(8):
            pb = (ps512[oc][:, 0:length] if oc < 4 else
                  ps1024[(oc - 4) // 2][:, ((oc - 4) % 2) * 512:
                                        ((oc - 4) % 2) * 512 + length])
            s.dma(ev[tb][:, 0:length],
                  xE.rearrange("(c p) t -> p c t", p=128)[:, oc,
                                                          c0:c0 + length],
                  tag="ldxr", slow=(length == 1))
            s.dve("tensor_add", ev2[tb][:, 0:length], pb, ev[tb][:, 0:length],
                  tag="xmadd")
            if c0 == 1535:
                s.dve("tensor_scalar_mul", ev2[tb][:, 0:length],
                      ev2[tb][:, 0:length], ez_s[:], tag="edgez")
            s.dma(xmD.rearrange("(c p) t -> p c t", p=128)[
                :, oc, c0 - 1535:c0 - 1535 + length], ev2[tb][:, 0:length],
                q="dgps", tag="stxm", slow=(length == 1))

    emit_wo(1535, 1, 0)
    for tt in range(4):
        emit_wo(1536 + tt * 512, 512, tt % 2)

    if dbg:
        o_ = p.dram_out("dbg_xm", [D, TOK + 1])
        s.dma(o_[:, :], xmD[:, :], q="dgps", tag="dbgxm")
    for tt in range(4):
        tb = tt % 2
        tsl = slice(tt * 512, (tt + 1) * 512)
        psum_s = ps512[2][0:1, 0:512]
        psum_q = ps512[2][32:33, 0:512]
        for oc in range(8):
            off = 1 if oc < 4 else 0
            s.dma(ev[tb][:], xmD.rearrange("(c p) t -> p c t", p=128)[
                :, oc, tt * 512 + off:tt * 512 + off + 512], tag="ldh2")
            s.act(sqv[:], ev[tb][:], AF.Square, tag="h2sq")
            s.mm(psum_s, ones_c[:], ev[tb][:], oc == 0, oc == 7, tag="h2sum")
            s.mm(psum_q, ones_c[:], sqv[:], oc == 0, oc == 7, tag="h2sq2")
        s.scopy(stats[:, 0, tsl], psum_s, tag="h2ev_s")
        s.scopy(stats[:, 1, tsl], psum_q, tag="h2ev_q")
    s.act(stats[:, 0, :], stats[:, 0, :], AF.Copy, scale=1.0 / D, tag="fmu")
    s.act(stats[:, 1, :], stats[:, 1, :], AF.Copy, scale=1.0 / D, tag="fexx")
    for hf in range(4):
        hsl = slice(hf * 512, (hf + 1) * 512)
        s.dve("tensor_mul", rden[:], stats[:, 0, hsl], stats[:, 0, hsl],
              tag="fmusq")
        s.dve("tensor_sub", stats[:, 1, hsl], stats[:, 1, hsl], rden[:],
              tag="fvar")
        s.dve("tensor_scalar_add", stats[:, 1, hsl], stats[:, 1, hsl],
              LN_EPS, tag="fvare")
        s.dve("reciprocal", rden[:], stats[:, 1, hsl], tag="fvrec")
        s.act(stats[:, 1, hsl], rden[:], AF.Sqrt, tag="frsq")
        s.dve("tensor_mul", stats[:, 0, hsl], stats[:, 0, hsl],
              stats[:, 1, hsl], tag="fmurs")
        s.dve("tensor_scalar_mul", stats[:, 0, hsl], stats[:, 0, hsl], -1.0,
              tag="fnegm")
    for tt in range(4):
        tb = tt % 2
        tsl = slice(tt * 512, (tt + 1) * 512)
        pa = ps512[0][:, 0:512]
        pbv = ps512[1][:, 0:512]
        s.mm(pa, ones_r[:], stats[:, 1, tsl], True, True, tag="faexp")
        s.mm(pbv, ones_r[:], stats[:, 0, tsl], True, True, tag="fbexp")
        for oc in range(8):
            off = 1 if oc < 4 else 0
            s.dma(ev[tb][:], xmD.rearrange("(c p) t -> p c t", p=128)[
                :, oc, tt * 512 + off:tt * 512 + off + 512], tag="ldh2b")
            s.dve("tensor_mul", ev2[tb][:], ev[tb][:], pa, tag="flnm")
            s.dve("tensor_add", ev2[tb][:], ev2[tb][:], pbv, tag="flna")
            s.dve("tensor_scalar", ev2[tb][:], ev2[tb][:], lg_s[:, oc:oc + 1],
                  scalar2=lb_s[:, oc:oc + 1], op0=ALU.mult, op1=ALU.add,
                  tag="flngb")
            emit_split(s, evh[tb][:], evl[tb][:], ev2[tb][:])
            s.dma(lnhD[0, oc * 128:(oc + 1) * 128, tsl], evh[tb][:],
                  q="dgps", tag="stlnh")
            s.dma(lnhD[1, oc * 128:(oc + 1) * 128, tsl], evl[tb][:],
                  q="dgps", tag="stlnl")

    if dbg:
        o_ = p.dram_out("dbg_lnh", [2, D, TOK], BF16)
        s.dma(o_[:, :, :], lnhD[:, :, :], q="dgps", tag="dbgln")
    for tthalf in range(2):
        for t2 in range(2):
            tt = tthalf * 2 + t2
            tsl = slice(tt * 512, (tt + 1) * 512)
            s.dma(hhi[t2][:],
                  lnhD[0, :, tsl].rearrange("(c p) t -> p c t", p=128),
                  tag="ldlh")
            s.dma(hlo[t2][:],
                  lnhD[1, :, tsl].rearrange("(c p) t -> p c t", p=128),
                  tag="ldll")
        for hc in range(NHC):
            wfa = wfp_f[:, 0:1024].rearrange("p (c o) -> p c o", o=128)
            wfg = wfp_f[:, 1024:2048].rearrange("p (c o) -> p c o", o=128)
            s.dma(wfa, w1p[:, 0, hc * 128:(hc + 1) * 128].rearrange(
                "(c p) o -> p c o", p=128), q="dgps", tag="ldw1a")
            s.dma(wfg, w1p[:, 1, hc * 128:(hc + 1) * 128].rearrange(
                "(c p) o -> p c o", p=128), q="dgps", tag="ldw1g")
            wha = whf[:, 0:1024].rearrange("p (c o) -> p c o", o=128)
            wla = wlf[:, 0:1024].rearrange("p (c o) -> p c o", o=128)
            whg = whf[:, 1024:2048].rearrange("p (c o) -> p c o", o=128)
            wlg = wlf[:, 1024:2048].rearrange("p (c o) -> p c o", o=128)
            emit_split(s, wha, wla, wfa)
            emit_split(s, whg, wlg, wfg)
            for t2 in range(2):
                tt = tthalf * 2 + t2
                tsl = slice(tt * 512, (tt + 1) * 512)
                pa_ = ps512[t2][:, 0:512]
                pg_ = ps512[2 + t2][:, 0:512]
                emit_proj3(s, pa_, lambda c: wha[:, c, :],
                           lambda c: wla[:, c, :],
                           lambda c: hhi[t2][:, c, :],
                           lambda c: hlo[t2][:, c, :], 8, tag="w1a")
                emit_proj3(s, pg_, lambda c: whg[:, c, :],
                           lambda c: wlg[:, c, :],
                           lambda c: hhi[t2][:, c, :],
                           lambda c: hlo[t2][:, c, :], 8, tag="w1g")
                s.act(gel[:], pg_, AF.Gelu, bias=b1g_s[:, hc:hc + 1],
                      tag="gelu")
                s.dve("tensor_scalar_add", asb[:], pa_,
                      b1a_s[:, hc:hc + 1], tag="aev")
                s.dve("tensor_mul", asb[:], asb[:], gel[:], tag="glu")
                emit_split(s, glh[:], gll[:], asb[:])
                s.dma(gluD[hc, 0, :, tsl], glh[:], q="dgps", tag="stgh")
                s.dma(gluD[hc, 1, :, tsl], gll[:], q="dgps", tag="stgl")

    gl2 = [p.sb(f"gl2_{i}", [128, 2, 512], BF16) for i in range(2)]
    for tt in range(4):
        tb = tt % 2
        tsl = slice(tt * 512, (tt + 1) * 512)
        for hc in range(NHC):
            w2h = whf[:, 2048:3072]
            w2l = wlf[:, 2048:3072]
            s.dma(wfp_f[:, 0:1024], w2p[hc * 128:(hc + 1) * 128, :],
                  q="dgps", tag="ldw2")
            emit_split(s, w2h, w2l, wfp_f[:, 0:1024])
            gb = gl2[hc % 2]
            s.dma(gb[:], gluD[hc].rearrange("hl p t -> p hl t")[:, :, tsl],
                  tag="ldglu")
            for oc in range(8):
                pb = (ps512[oc][:, 0:512] if oc < 4 else
                      ps1024[(oc - 4) // 2][:, ((oc - 4) % 2) * 512:
                                            ((oc - 4) % 2) * 512 + 512])
                ocs = slice(oc * 128, (oc + 1) * 128)
                s.mm(pb, w2h[:, ocs], gb[:, 0, :], hc == 0, False, tag="w2A")
                s.mm(pb, w2h[:, ocs], gb[:, 1, :], False, False, tag="w2B")
                s.mm(pb, w2l[:, ocs], gb[:, 0, :], False, hc == NHC - 1,
                     tag="w2C")
        for oc in range(8):
            pb = (ps512[oc][:, 0:512] if oc < 4 else
                  ps1024[(oc - 4) // 2][:, ((oc - 4) % 2) * 512:
                                        ((oc - 4) % 2) * 512 + 512])
            s.dve("tensor_scalar_add", ev[oc % 2][:], pb,
                  b2_s[:, oc:oc + 1], tag="b2ev")
            s.dma(ev2[oc % 2][:], xmD.rearrange("(c p) t -> p c t", p=128)[
                :, oc, tt * 512 + 1:tt * 512 + 513], tag="ldxm2")
            s.dve("tensor_add", ev[oc % 2][:], ev[oc % 2][:], ev2[oc % 2][:],
                  tag="ffnres")
            if block_ln:
                s.dma(ev2[oc % 2][:],
                      fm_view(x1)[:, oc, tsl], tag="ldx1r")
                s.dve("tensor_add", ev[oc % 2][:], ev[oc % 2][:],
                      ev2[oc % 2][:], tag="blkres")
                s.dma(xmD.rearrange("(c p) t -> p c t", p=128)[
                    :, oc, tt * 512 + 1:tt * 512 + 513] if False else
                    fm_view(xo)[:, oc, tsl], ev[oc % 2][:], q="dgps",
                    tag="stpre")
            else:
                s.dma(fm_view(xo)[:, oc, tsl], ev[oc % 2][:], q="dgps",
                      tag="stxo")

    if block_ln:
        for tt in range(4):
            tb = tt % 2
            tsl = slice(tt * 512, (tt + 1) * 512)
            psum_s = ps512[2][0:1, 0:512]
            psum_q = ps512[2][32:33, 0:512]
            for oc in range(8):
                s.dma(ev[tb][:], fm_view(xo)[:, oc, tsl], tag="ldpre")
                s.act(sqv[:], ev[tb][:], AF.Square, tag="bsq")
                s.mm(psum_s, ones_c[:], ev[tb][:], oc == 0, oc == 7,
                     tag="bsum")
                s.mm(psum_q, ones_c[:], sqv[:], oc == 0, oc == 7, tag="bsq2")
            s.scopy(stats[:, 0, tsl], psum_s, tag="bev_s")
            s.scopy(stats[:, 1, tsl], psum_q, tag="bev_q")
        s.act(stats[:, 0, :], stats[:, 0, :], AF.Copy, scale=1.0 / D,
              tag="bmu")
        s.act(stats[:, 1, :], stats[:, 1, :], AF.Copy, scale=1.0 / D,
              tag="bexx")
        for hf in range(4):
            hsl = slice(hf * 512, (hf + 1) * 512)
            s.dve("tensor_mul", rden[:], stats[:, 0, hsl], stats[:, 0, hsl],
                  tag="bmusq")
            s.dve("tensor_sub", stats[:, 1, hsl], stats[:, 1, hsl], rden[:],
                  tag="bvar")
            s.dve("tensor_scalar_add", stats[:, 1, hsl], stats[:, 1, hsl],
                  LN_EPS, tag="bvare")
            s.dve("reciprocal", rden[:], stats[:, 1, hsl], tag="bvrec")
            s.act(stats[:, 1, hsl], rden[:], AF.Sqrt, tag="brsq")
            s.dve("tensor_mul", stats[:, 0, hsl], stats[:, 0, hsl],
                  stats[:, 1, hsl], tag="bmurs")
            s.dve("tensor_scalar_mul", stats[:, 0, hsl], stats[:, 0, hsl],
                  -1.0, tag="bnegm")
        for tt in range(4):
            tb = tt % 2
            tsl = slice(tt * 512, (tt + 1) * 512)
            pa = ps512[0][:, 0:512]
            pbv = ps512[1][:, 0:512]
            s.mm(pa, ones_r[:], stats[:, 1, tsl], True, True, tag="baexp")
            s.mm(pbv, ones_r[:], stats[:, 0, tsl], True, True, tag="bbexp")
            for oc in range(8):
                s.dma(ev[tb][:], fm_view(xo)[:, oc, tsl], tag="ldpre2")
                s.dve("tensor_mul", ev2[tb][:], ev[tb][:], pa, tag="blnm")
                s.dve("tensor_add", ev2[tb][:], ev2[tb][:], pbv, tag="blna")
                s.dma(fm_view(xo)[:, oc, tsl], ev2[tb][:], q="dgps",
                      tag="stxo2")

    nc_ = s.emit()
    p.es.close()
    return nc_


def _np_split_pair(w):
    import ml_dtypes
    hi = _bf16(w)
    lo = _bf16(np.asarray(w, np.float32) - np.asarray(hi, np.float32))
    return hi, lo


def host_rotary_tables():
    half = np.arange(0, FD, 2, dtype=np.float32)
    inv = (1.0 / 10000.0 ** (half / FD)).astype(np.float32)
    t = np.arange(-MARG, N, dtype=np.float32)
    fr = t[:, None] * inv[None, :]
    fr = np.concatenate([fr, fr], -1).astype(np.float32)
    base = ((half + 0.4 * FD) / (1.4 * FD)).astype(np.float32)
    power = ((t - N // 2) / float(WIN // 2)).astype(np.float32)
    sc = (base[None, :] ** power[:, None]).astype(np.float32)
    sc = np.concatenate([sc, sc], -1).astype(np.float32)
    cos, sin = np.cos(fr).astype(np.float32), np.sin(fr).astype(np.float32)
    csq = (cos * sc).T
    snq = (sin * sc).T
    csk = (cos / sc).T
    snk = (sin / sc).T
    dup = lambda a: np.ascontiguousarray(np.concatenate([a, a], 0))
    return dup(csq), dup(snq), dup(csk), dup(snk)


def host_rotk():
    R = np.zeros((128, 128), np.float32)
    for blk in range(2):
        b0 = blk * 64
        for i in range(32):
            R[b0 + 32 + i, b0 + i] = -1.0
            R[b0 + i, b0 + 32 + i] = 1.0
    return R


def host_tri2T():
    t = np.full((128, 128), NEG, np.float32)
    t[np.triu_indices(128)] = 0.0
    return _bf16(t)


def host_trimaskT():
    t = np.full((128, 128), NEG, np.float32)
    t[np.tril_indices(128)] = 0.0
    return _bf16(t)


def make_C_inputs(x_layer_full, x1_core, wqkv, qs, ksc, wo, lg, lb, w1, b1,
                  w2, b2, core_q, tabs):
    s0 = core_q * TOK
    xe = np.zeros((XT, D), np.float32)
    lo = s0 - MARG
    src0 = max(0, lo)
    xe[src0 - lo:XT] = x_layer_full[src0:s0 + TOK]
    w1pad = np.zeros((D, 2, HID_P), np.float32)
    w1pad[:, 0, :HID] = w1[:, :HID]
    w1pad[:, 1, :HID] = w1[:, HID:]
    w2pad = np.zeros((HID_P, D), np.float32)
    w2pad[:HID] = w2
    b1a = np.zeros((NHC * 128,), np.float32)
    b1g = np.zeros((NHC * 128,), np.float32)
    b1a[:HID] = b1[:HID]
    b1g[:HID] = b1[HID:]
    csq, snq, csk, snk = tabs
    c0 = s0 - MARG + MARG
    tsl = slice(s0, s0 + XT)
    first = core_q == 0
    ins = {
        "xET": np.ascontiguousarray(xe.T),
        "x1T": (np.ascontiguousarray(x1_core.T) if x1_core is not None
                else np.zeros((D, TOK), np.float32)),
        "wqkv": np.ascontiguousarray(wqkv),
        "wo": np.ascontiguousarray(wo),
        "w1p": w1pad, "w2p": w2pad,
        "qs128": np.ascontiguousarray(np.tile(qs, 2)[:, None]),
        "ks128": np.ascontiguousarray(np.tile(ksc, 2)[:, None]),
        "lg128": np.ascontiguousarray(lg.reshape(8, 128).T),
        "lb128": np.ascontiguousarray(lb.reshape(8, 128).T),
        "b1a128": np.ascontiguousarray(b1a.reshape(NHC, 128).T),
        "b1g128": np.ascontiguousarray(b1g.reshape(NHC, 128).T),
        "b2c": np.ascontiguousarray(b2.reshape(8, 128).T),
        "csq": np.ascontiguousarray(csq[:, tsl]),
        "snq": np.ascontiguousarray(snq[:, tsl]),
        "csk": np.ascontiguousarray(csk[:, tsl]),
        "snk": np.ascontiguousarray(snk[:, tsl]),
        "edgeneg": np.full((1, 1), NEG if first else 0.0, np.float32),
        "edgezero": np.full((128, 1), 0.0 if first else 1.0, np.float32),
        "trimask": host_trimaskT(),
        "expand2": np.ascontiguousarray(
            np.kron(np.eye(2, dtype=np.float32), np.ones((1, 64), np.float32))),
        "ones2h": np.ascontiguousarray(
            np.kron(np.eye(2, dtype=np.float32), np.ones((64, 1), np.float32))),
        "rotk": host_rotk(),
        "idb": _bf16(np.concatenate(
            [np.eye(128, dtype=np.float32),
             np.zeros((128, 384), np.float32)], 1)),
        "zid": _bf16(np.concatenate(
            [np.zeros((128, 512), np.float32),
             np.eye(128, dtype=np.float32)], 1)),
        "tri2m": host_tri2T(),
    }
    return ins


def _np_split_pair(w):
    hi = _bf16(w)
    lo = _bf16(np.asarray(w, np.float32) - np.asarray(hi, np.float32))
    return hi, lo


def build_B(dbg=False):
    p = Prog("B")
    s = Sched(p)
    nc = p.nc
    x0 = p.dram_in("x0T", [D, TOK])
    qp2 = p.dram_in("qp2", [128, H, TOK], BF16)
    qlo = p.dram_in("qlo", [64, H, TOK], BF16)
    kpB = p.dram_in("kpB", [128, H, N], BF16)
    vfB = p.dram_in("vfB", [H, N, DH])
    xpg = p.dram_in("xpg", [B, D, TR])
    wq = p.dram_in("wq", [D, D])
    wk = p.dram_in("wk", [D, D])
    wo = p.dram_in("wo", [D, D])
    kron4 = p.dram_in("kron4", [4, 128], BF16)
    esel = p.dram_in("esel", [TR, TOK // BM], BF16)
    x1o = p.dram_out("x1T", [D, TOK])
    x1pre_d = nc.dram_tensor("x1pre", [D, TOK], F32, kind="Internal").ap()
    ah_d = nc.dram_tensor("ah_d", [128, 8, 2, TOK], BF16, kind="Internal").ap()

    MYROWS = TOK // BM
    QH = 1024
    NKC = N // 128

    wfp = p.sb("wfp", [128, 8, 512])
    whi = p.sb("whi", [128, 8, 1024], BF16)
    wlo = p.sb("wlo", [128, 8, 1024], BF16)
    xps = [p.sb(f"xps{i}", [128, B * TR]) for i in range(2)]
    xphi = p.sb("xphi", [128, 8, B * TR], BF16)
    xplo = p.sb("xplo", [128, 8, B * TR], BF16)
    pev = p.sb("pev", [128, B * TR])
    pevh = p.sb("pevh", [128, B * TR], BF16)
    pevl = p.sb("pevl", [128, B * TR], BF16)
    prq = [p.sb(f"prq{i}", [128, TR], BF16) for i in range(2)]
    prka = [p.sb(f"prka{i}", [128, TR], BF16) for i in range(2)]
    prkb = [p.sb(f"prkb{i}", [64, TR], BF16) for i in range(2)]
    avg = p.sb("avg", [128, 2, TR])
    m8 = p.sb("m8", [128, 8])
    amk = p.sb("amk", [128, 2, TR], BF16)
    kronsb = p.sb("kronsb", [4, 128], BF16)
    eselsb = p.sb("eselsb", [128, 2, MYROWS], BF16)
    btl = p.sb("btl", [4, MYROWS], BF16)
    binrep = p.sb("binrep", [128, NKC, MYROWS], BF16)
    kst = [p.sb(f"kst{i}", [128, N], BF16) for i in range(2)]
    vsth = p.sb("vsth", [128, NKC, 65], BF16)
    vstl = p.sb("vstl", [128, NKC, 65], BF16)
    qst = [p.sb(f"qst{i}", [128, QH], BF16) for i in range(2)]
    qlst = [p.sb(f"qlst{i}", [64, QH], BF16) for i in range(2)]
    pexp = [p.sb(f"pexp{i}", [128, QH]) for i in range(2)]
    phi = [p.sb(f"phi{i}", [128, QH], BF16) for i in range(2)]
    plo = [p.sb(f"plo{i}", [128, QH], BF16) for i in range(2)]
    rden = p.sb("rden", [1, QH])
    rdex = p.sb("rdex", [64, QH])
    ahat = p.sb("ahat", [64, QH])
    ah_h = p.sb("ah_h", [64, QH], BF16)
    ah_l = p.sb("ah_l", [64, QH], BF16)
    ahld = [p.sb(f"ahld{i}", [128, 2, 512], BF16) for i in range(2)]
    x0c = [p.sb(f"x0c{i}", [128, 512]) for i in range(2)]
    x1p = [p.sb(f"x1p{i}", [128, 512]) for i in range(2)]
    sq = p.sb("sq", [128, 512])
    stats = p.sb("stats", [1, 2, TOK])
    rden2 = None
    ones_c = p.sb("ones_c", [128, 1])
    ones_r = p.sb("ones_r", [1, 128])

    ps1024 = [p.ps(f"ps1024_{i}", [128, 1024]) for i in range(2)]
    ps512 = [p.ps(f"ps512_{i}", [128, 512]) for i in range(4)]

    s.dma(kronsb[:], kron4[:, :], tag="ldkron")
    s.dma(eselsb[:], esel.rearrange("(it p) m -> p it m", p=128), tag="ldesel")
    s.dve("memset", ones_c[:], 1.0, tag="onesc")
    s.dve("memset", ones_r[:], 1.0, tag="onesr")
    s.dve("memset", vsth[:, :, 64:65], 1.0, tag="vstones")
    s.dve("memset", vstl[:, :, 64:65], 0.0, tag="vstzer")

    for c in range(8):
        xb = xps[c % 2]
        s.dma(xb[:].rearrange("p (b r) -> p b r", b=B),
              xpg.rearrange("b (c p) r -> p c b r", p=128)[:, c, :, :],
              tag=f"ldxp{c}")
        emit_split(s, xphi[:, c, :], xplo[:, c, :], xb[:])
    avg_first = [True]
    for wi, wsrc in enumerate((wq, wk)):
        for hf in range(2):
            cols = slice(hf * 512, (hf + 1) * 512)
            s.dma(wfp[:], wsrc[:, cols].rearrange("(c p) o -> p c o", p=128),
                  q="dgps", tag=f"ldwqk{wi}{hf}")
            emit_split(s, whi[:, :, cols], wlo[:, :, cols], wfp[:])
        if wi == 0:
            qph_d = nc.dram_tensor("qph_d", [128, 8, B * TR], BF16,
                                   kind="Internal").ap()
            qpl_d = nc.dram_tensor("qpl_d", [128, 8, B * TR], BF16,
                                   kind="Internal").ap()
        for oc in range(8):
            pp = ps512[oc % 2][:, 0:B * TR]
            ocs = slice(oc * 128, (oc + 1) * 128)
            emit_proj3(s, pp,
                       lambda c: whi[:, c, ocs], lambda c: wlo[:, c, ocs],
                       lambda c: xphi[:, c, :], lambda c: xplo[:, c, :], 8,
                       tag="poolproj")
            s.scopy(pev[:], pp, tag="poolev")
            emit_split(s, pevh[:], pevl[:], pev[:])
            if wi == 0:
                s.dma(qph_d[:, oc, :], pevh[:], q="dgps", tag="stqp_h")
                s.dma(qpl_d[:, oc, :], pevl[:], q="dgps", tag="stqp_l")
            else:
                for hh in range(2):
                    h = 2 * oc + hh
                    ri = hh % 2
                    rs_ = slice(hh * 64, (hh + 1) * 64)
                    s.dma(prka[ri][0:64, :], pevh[rs_, 0:TR], q="dgps",
                          tag="prka1")
                    s.dma(prka[ri][64:128, :], pevh[rs_, 0:TR], q="dgps",
                          tag="prka2")
                    s.dma(prkb[ri][0:64, :], pevl[rs_, 0:TR], q="dgps",
                          tag="prkb")
                    s.dma(prq[ri][0:64, :], qph_d[rs_, oc, 0:TR], q="dgps",
                          tag="prq_h")
                    s.dma(prq[ri][64:128, :], qpl_d[rs_, oc, 0:TR], q="dgps",
                          tag="prq_l")
                    for it in range(2):
                        isl = slice(it * 128, (it + 1) * 128)
                        pp2 = ps1024[it][:, 0:TR]
                        s.mm(pp2, prq[ri][:, isl], prka[ri][:, :],
                             avg_first[0], False, tag="avgA")
                        s.mm(pp2, prq[ri][0:64, isl], prkb[ri][:, :],
                             False, False, tag="avgB")
                    avg_first[0] = False
                    bsl = slice(TR, 2 * TR)
                    s.dma(prka[ri][0:64, :], pevh[rs_, bsl], q="dgps",
                          tag="prka1b")
                    s.dma(prka[ri][64:128, :], pevh[rs_, bsl], q="dgps",
                          tag="prka2b")
                    s.dma(prkb[ri][0:64, :], pevl[rs_, bsl], q="dgps",
                          tag="prkbb")
                    s.dma(prq[ri][0:64, :], qph_d[rs_, oc, bsl], q="dgps",
                          tag="prq_hb")
                    s.dma(prq[ri][64:128, :], qpl_d[rs_, oc, bsl], q="dgps",
                          tag="prq_lb")
                    last = oc == 7 and hh == 1
                    for it in range(2):
                        isl = slice(it * 128, (it + 1) * 128)
                        pp2 = ps1024[it][:, 0:TR]
                        s.mm(pp2, prq[ri][:, isl], prka[ri][:, :],
                             False, False, tag="avgAb")
                        s.mm(pp2, prq[ri][0:64, isl], prkb[ri][:, :],
                             False, last, tag="avgBb")
    for it in range(2):
        s.act(avg[:, it, :], ps1024[it][:, 0:TR], AF.Copy,
              scale=1.0 / (B * H), tag="avgev")
    for it in range(2):
        w2 = avg[:, it, :]
        for r in range(7):
            s.dve("max", m8[:], w2, tag="max8")
            if r == 6:
                s.dve("memset", m8[:, 3:8], -1e38, tag="m8pad")
            s.dve("match_replace", w2, m8[:], w2, imm_value=NEG, tag="mrep")
    s.dve("tensor_scalar", amk[:], avg[:], NEG * 0.5, scalar2=NEG,
          op0=ALU.is_gt, op1=ALU.mult, tag="is_sel")
    if dbg:
        damk = p.dram_out("dbg_amk", [128, 2, TR], BF16)
        s.dma(damk[:, :, :], amk[:], q="dgps", tag="dbgamk")
        davg = p.dram_out("dbg_avg", [128, 2, TR])
        s.dma(davg[:, :, :], avg[:], q="dgps", tag="dbgavg")
    for kc in range(NKC):
        pp = ps512[2][0:4, 0:MYROWS]
        jsl = slice(kc * 4, (kc + 1) * 4)
        for it in range(2):
            s.mm(pp, amk[:, it, jsl], eselsb[:, it, :], it == 0, it == 1,
                 tag="eselmm")
        s.scopy(btl[:], pp, tag="eselev")
        pp2 = ps512[3][:, 0:MYROWS]
        s.mm(pp2, kronsb[:], btl[:], True, True, tag="binexp")
        s.scopy(binrep[:, kc, :], pp2, tag="binev")

    for hf in range(2):
        cols = slice(hf * 512, (hf + 1) * 512)
        s.dma(wfp[:], wo[:, cols].rearrange("(c p) o -> p c o", p=128),
              q="dgps", tag=f"ldwo{hf}")
        emit_split(s, whi[:, :, cols], wlo[:, :, cols], wfp[:])

    SCALE = 1.0 / math.sqrt(DH)
    for h in range(H):
        hb = h % 2
        s.dma(kst[hb][:], kpB[:, h, :], tag="ldk")
        for vq in range(4):
            vqs = slice(vq * 16, (vq + 1) * 16)
            vstage = pexp[0][:].rearrange("p (kc d) -> p kc d", d=64)[
                :, 0:16, :]
            s.dma(vstage, vfB[h, vq * 2048:(vq + 1) * 2048, :].rearrange(
                "(kc pp) d -> pp kc d", pp=128), tag="ldvf")
            emit_split(s, vsth[:, vqs, 0:64], vstl[:, vqs, 0:64], vstage)
        for qh in range(2):
            qsl = slice(qh * QH, (qh + 1) * QH)
            qb = qh % 2
            s.dma(qst[qb][:], qp2[:, h, qsl], q="dgps", tag="ldq")
            s.dma(qlst[qb][:], qlo[:, h, qsl], q="dgps", tag="ldql")
            pvp = [ps512[0][0:65, :], ps512[1][0:65, :]]
            for kc in range(NKC):
                kb = kc % 2
                stp = ps1024[kb]
                ksl = slice(kc * 128, (kc + 1) * 128)
                for half in range(2):
                    psl = slice(half * 512, (half + 1) * 512)
                    s.mm(stp[:, psl], kst[hb][:, ksl], qst[qb][:, psl],
                         True, False, tag="qkA")
                    s.mm(stp[:, psl], kst[hb][0:64, ksl], qlst[qb][:, psl],
                         False, True, tag="qkB")
                mrow = NKC * MYROWS
                mv = AP(binrep[:].tensor, kc * MYROWS + qh * 32,
                        [[mrow, 128], [1, 32], [0, 32]])
                s.dve("tensor_tensor",
                      stp[:].rearrange("p (r j) -> p r j", j=32),
                      stp[:].rearrange("p (r j) -> p r j", j=32),
                      mv, op=ALU.add, extra_reads=[binrep[:]], tag="maskadd")
                s.act(pexp[kb][:], stp[:], AF.Exp, scale=SCALE, tag="exp")
                s.act(phi[kb][:], stp[:], AF.Exp, scale=SCALE, tag="exph")
                s.dve("tensor_sub", plo[kb][:], pexp[kb][:], phi[kb][:],
                      tag="plosub")
                for pkc in ([kc - 1] if kc > 0 else []) + (
                        [kc] if kc == NKC - 1 else []):
                    pb_ = pkc % 2
                    for half in range(2):
                        psl = slice(half * 512, (half + 1) * 512)
                        pv = pvp[half]
                        s.mm(pv, vsth[:, pkc, :], phi[pb_][:, psl],
                             pkc == 0, False, tag="pvA")
                        s.mm(pv, vsth[:, pkc, :], plo[pb_][:, psl], False,
                             False, tag="pvB")
                        s.mm(pv, vstl[:, pkc, :], phi[pb_][:, psl], False,
                             pkc == NKC - 1, tag="pvC")
            for half in range(2):
                psl = slice(half * 512, (half + 1) * 512)
                s.dve("reciprocal", rden[:, psl], pvp[half][64:65, :],
                      tag="recip")
            pp = ps1024[0][0:64, :]
            for half in range(2):
                psl = slice(half * 512, (half + 1) * 512)
                s.mm(pp[:, psl], ones_r[0:1, 0:64], rden[:, psl],
                     True, True, tag="rdex")
            s.scopy(rdex[:], pp, tag="rdexev")
            for half in range(2):
                psl = slice(half * 512, (half + 1) * 512)
                s.dve("tensor_mul", ahat[:, psl], pvp[half][0:64, :],
                      rdex[:, psl], tag="ahat")
            emit_split(s, ah_h[:], ah_l[:], ahat[:])
            hp = h // 2
            r0 = (h % 2) * 64
            s.dma(ah_d[r0:r0 + 64, hp, 0, qsl], ah_h[:], q="dgps", tag="stah")
            s.dma(ah_d[r0:r0 + 64, hp, 1, qsl], ah_l[:], q="dgps", tag="stal")

    if dbg:
        dah = p.dram_out("dbg_ah", [128, 8, 2, TOK], BF16)
        for hp_ in range(8):
            s.dma(dah[:, hp_, :, :], ah_d[:, hp_, :, :], q="dgps", tag="dbgah")
    def octile(oc):
        if oc < 4:
            return ps512[oc][:, 0:512]
        i = oc - 4
        return ps1024[i // 2][:, (i % 2) * 512:(i % 2) * 512 + 512]

    for tt in range(4):
        tb = tt % 2
        tsl = slice(tt * 512, (tt + 1) * 512)
        for hp in range(8):
            ab_ = ahld[hp % 2]
            s.dma(ab_[:], ah_d[:, hp, :, tsl], tag="ldah")
            for oc in range(8):
                pb = octile(oc)
                ocs = slice(oc * 128, (oc + 1) * 128)
                s.mm(pb, whi[:, hp, ocs], ab_[:, 0, :], hp == 0, False,
                     tag="opA")
                s.mm(pb, whi[:, hp, ocs], ab_[:, 1, :], False, False,
                     tag="opB")
                s.mm(pb, wlo[:, hp, ocs], ab_[:, 0, :], False, hp == 7,
                     tag="opC")
        psum_s = ps512[0][0:1, 0:512]
        psum_q = ps512[0][32:33, 0:512]
        for oc in range(8):
            pb = octile(oc)
            s.dma(x0c[tb][:], fm_view(x0)[:, oc, tsl], tag="ldx0")
            s.dve("tensor_add", x1p[oc % 2][:], pb, x0c[tb][:], tag="res")
            s.dma(fm_view(x1pre_d)[:, oc, tsl], x1p[oc % 2][:], q="dgps",
                  tag="stx1p")
            s.act(sq[:], x1p[oc % 2][:], AF.Square, tag="sqr")
            s.mm(psum_s, ones_c[:], x1p[oc % 2][:], oc == 0, oc == 7,
                 tag="ssum")
            s.mm(psum_q, ones_c[:], sq[:], oc == 0, oc == 7, tag="sqsum")
        s.scopy(stats[:, 0, tsl], psum_s, tag="stev_s")
        s.scopy(stats[:, 1, tsl], psum_q, tag="stev_q")

    s.act(stats[:, 0, :], stats[:, 0, :], AF.Copy, scale=1.0 / D, tag="mu")
    s.act(stats[:, 1, :], stats[:, 1, :], AF.Copy, scale=1.0 / D, tag="exx")
    for hf in range(2):
        hsl = slice(hf * QH, (hf + 1) * QH)
        s.dve("tensor_mul", rden[:], stats[:, 0, hsl], stats[:, 0, hsl],
              tag="musq")
        s.dve("tensor_sub", stats[:, 1, hsl], stats[:, 1, hsl], rden[:],
              tag="var")
    for hf in range(2):
        hsl = slice(hf * QH, (hf + 1) * QH)
        s.dve("tensor_scalar_add", stats[:, 1, hsl], stats[:, 1, hsl],
              LN_EPS, tag="vare")
        s.dve("reciprocal", rden[:], stats[:, 1, hsl], tag="vrec")
        s.act(stats[:, 1, hsl], rden[:], AF.Sqrt, tag="rsq")
    s.dve("tensor_mul", stats[:, 0, :], stats[:, 0, :], stats[:, 1, :],
          tag="murs")
    s.dve("tensor_scalar_mul", stats[:, 0, :], stats[:, 0, :], -1.0,
          tag="negm")
    for tt in range(4):
        tb = tt % 2
        tsl = slice(tt * 512, (tt + 1) * 512)
        pa = ps512[0][:, 0:512]
        pbv = ps512[1][:, 0:512]
        s.mm(pa, ones_r[:], stats[:, 1, tsl], True, True, tag="aexp")
        s.mm(pbv, ones_r[:], stats[:, 0, tsl], True, True, tag="bexp")
        for oc in range(8):
            s.dma(x1p[tb][:], fm_view(x1pre_d)[:, oc, tsl], tag="ldx1p")
            s.dve("tensor_mul", x0c[tb][:], x1p[tb][:], pa, tag="lnmul")
            s.dve("tensor_add", x0c[tb][:], x0c[tb][:], pbv, tag="lnadd")
            s.dma(fm_view(x1o)[:, oc, tsl], x0c[tb][:], q="dgps", tag="stx1")

    nc_ = s.emit()
    p.es.close()
    return nc_


TOKF = 2048
CH = 512
NCK = TOKF // CH
HD4 = 4 * D
W1N = float(HD4 * D)


def build_EF(is_F, dbg=False):
    p = Prog("F" if is_F else "E")
    s = Sched(p)
    nc = p.nc
    x2 = p.dram_in("x2T", [D, TOKF])
    wqkvg = p.dram_in("wqkvg", [D, 3 * D])
    wvp = p.dram_in("wv", [D, D])
    bqg = p.dram_in("bqg", [1, 3 * D])
    bvp = p.dram_in("bv", [1, D])
    lgA = p.dram_in("lgA", [128, 8])
    lbA = p.dram_in("lbA", [128, 8])
    idf = p.dram_in("idf", [128, 128])
    if is_F:
        preS = p.dram_in("preS", [3, D, D])
        preK = p.dram_in("preK", [3, 128, 8])
        wpj = p.dram_in("wpj", [D, D])
        bpj = p.dram_in("bpj", [1, D])
        t1m = p.dram_in("t1m", [128, 896])
        w1t = p.dram_in("w1t", [D, HD4])
        b1c = p.dram_in("b1c", [128, 32])
        w2t = p.dram_in("w2t", [HD4, D])
        b2c8 = p.dram_in("b2c8", [128, 8])
        xout = p.dram_out("xoT", [D, TOKF])
        x3D = nc.dram_tensor("x3D", [D, TOKF], F32, kind="Internal").ap()
        w1qD = nc.dram_tensor("w1qD", [D, HD4], BF16, kind="Internal").ap()
        w2qD = nc.dram_tensor("w2qD", [HD4, D], BF16, kind="Internal").ap()
        rawD = nc.dram_tensor("rawD", [2, 4, 512], F32, kind="Internal").ap()
    else:
        SD = p.dram_out("SD", [D, D])
        kprD = p.dram_out("kprD", [128, 8])

    x2c = p.sb("x2c", [128, 8, CH])
    xnh = p.sb("xnh", [128, 8, CH], BF16)
    xnl = p.sb("xnl", [128, 8, CH], BF16)
    kfm = p.sb("kfm", [128, 8, CH])
    gat = p.sb("gat", [128, 8, CH])
    vtm = p.sb("vtm", [128, 4, D])
    ktm = p.sb("ktm", [128, 4, D])
    Ssb = p.sb("Ssb", [128, 8, D])
    kpre = p.sb("kpre", [128, 8])
    kt8 = p.sb("kt8", [128, 8])
    wsh = [p.sb(f"wsh{i}", [128, 8, 128], BF16) for i in range(2)]
    wsl = [p.sb(f"wsl{i}", [128, 8, 128], BF16) for i in range(2)]
    wvh = p.sb("wvh", [128, 8, 256], BF16)
    wvl = p.sb("wvl", [128, 8, 256], BF16)
    ev = [p.sb(f"ev{i}", [128, CH]) for i in range(2)]
    ev2 = p.sb("ev2", [128, CH])
    sqv = p.sb("sqv", [128, CH])
    uA = p.sb("uA", [128, CH])
    strow = p.sb("strow", [1, 2, CH])
    r1 = p.sb("r1", [1, CH])
    r2 = p.sb("r2", [1, CH])
    rowq = p.sb("rowq", [1, CH])
    rowsc = p.sb("rowsc", [1, CH])
    brow = [p.sb(f"brow{i}", [1, 128]) for i in range(2)]
    bvrow = p.sb("bvrow", [1, 256])
    xst = p.sb("xst", [128, 4])
    xr4 = p.sb("xr4", [4, 128])
    thr = p.sb("thr", [128, 8])
    wsum = p.sb("wsum", [128, 1])
    acol = p.sb("acol", [128, 1])
    bneg = p.sb("bneg", [128, 1])
    lg_s = p.sb("lg_s", [128, 8])
    lb_s = p.sb("lb_s", [128, 8])
    idf_s = p.sb("idf_s", [128, 128])
    ones_c = p.sb("ones_c", [128, 1])
    ones_r = p.sb("ones_r", [1, 128])
    ones_t = p.sb("ones_t", [1, 512])
    if is_F:
        qfm = p.sb("qfm", [128, 8, CH])
        Pt = p.sb("Pt", [128, 4, CH])
        t1_s = p.sb("t1_s", [128, 896])
        b1_s = p.sb("b1_s", [128, 32])
        b2_s = p.sb("b2_s", [128, 8])
        sc2s = p.sb("sc2s", [128, CH])

    ps = [p.ps(f"ps{i}", [128, 512]) for i in range(8)]

    s.dma(lg_s[:], lgA[:, :], tag="ldc")
    s.dma(lb_s[:], lbA[:, :], tag="ldc")
    s.dma(idf_s[:], idf[:, :], tag="ldc")
    if is_F:
        s.dma(t1_s[:], t1m[:, :], tag="ldc")
        s.dma(b1_s[:], b1c[:, :], tag="ldc")
        s.dma(b2_s[:], b2c8[:, :], tag="ldc")
    s.dve("memset", ones_c[:], 1.0, tag="onesc")
    s.dve("memset", ones_r[:], 1.0, tag="onesr")
    s.dve("memset", bneg[:], -0.5, tag="bneg")
    s.dve("memset", ones_t[:], 1.0, tag="onest")

    Sv = SD.rearrange("(c p) v -> p c v", p=128) if not is_F else None
    if is_F:
        pv = preS.rearrange("s (c p) v -> s p c v", p=128)
        s.dma(Ssb[:], pv[0], tag="ldS0")
        for j in (1, 2):
            s.dma(vtm[:], pv[j][:, 0:4, :], tag="ldSa")
            s.dma(ktm[:], pv[j][:, 4:8, :], tag="ldSb")
            for c in range(4):
                s.dve("tensor_add", Ssb[:, c, :], Ssb[:, c, :], vtm[:, c, :],
                      tag="Sacc")
                s.dve("tensor_add", Ssb[:, 4 + c, :], Ssb[:, 4 + c, :],
                      ktm[:, c, :], tag="Sacc")
        s.dma(kpre[:], preK[0, :, :], tag="ldK0")
        for j in (1, 2):
            s.dma(kt8[:], preK[j, :, :], tag="ldKj")
            s.dve("tensor_add", kpre[:], kpre[:], kt8[:], tag="Kacc")
    else:
        s.dve("memset", Ssb[:], 0.0, tag="zS")
        s.dve("memset", kpre[:], 0.0, tag="zK")

    def emit_elu1(dst, src_psum, tmp_r, tmp_e):
        s.act(tmp_r, src_psum, AF.Relu, tag="elur")
        s.dve("tensor_sub", tmp_e, src_psum, tmp_r, tag="elus")
        s.act(tmp_e, tmp_e, AF.Exp, tag="elue")
        s.dve("tensor_add", dst, tmp_e, tmp_r, tag="elua")

    for ck in range(NCK):
        tsl = slice(ck * CH, (ck + 1) * CH)
        psum_s = ps[0][0:1, 0:512]
        psum_q = ps[0][32:33, 0:512]
        for oc in range(8):
            s.dma(x2c[:, oc, :], fm_view(x2)[:, oc, tsl], tag="ldx2")
            s.act(sqv[:], x2c[:, oc, :], AF.Square, tag="lnsq")
            s.mm(psum_s, ones_c[:], x2c[:, oc, :], oc == 0, oc == 7,
                 tag="lnsum")
            s.mm(psum_q, ones_c[:], sqv[:], oc == 0, oc == 7, tag="lnsq2")
        s.scopy(strow[:, 0, :], psum_s, tag="lnev_s")
        s.scopy(strow[:, 1, :], psum_q, tag="lnev_q")
        s.act(strow[:, 0, :], strow[:, 0, :], AF.Copy, scale=1.0 / D,
              tag="lnmu")
        s.act(strow[:, 1, :], strow[:, 1, :], AF.Copy, scale=1.0 / D,
              tag="lnex")
        s.dve("tensor_mul", r1[:], strow[:, 0, :], strow[:, 0, :], tag="lnm2")
        s.dve("tensor_sub", strow[:, 1, :], strow[:, 1, :], r1[:], tag="lnvr")
        s.dve("tensor_scalar_add", strow[:, 1, :], strow[:, 1, :], LN_EPS,
              tag="lnve")
        s.dve("reciprocal", r1[:], strow[:, 1, :], tag="lnrc")
        s.act(strow[:, 1, :], r1[:], AF.Sqrt, tag="lnrs")
        s.dve("tensor_mul", strow[:, 0, :], strow[:, 0, :], strow[:, 1, :],
              tag="lnms")
        s.dve("tensor_scalar_mul", strow[:, 0, :], strow[:, 0, :], -1.0,
              tag="lnng")
        pa = ps[1][:, 0:512]
        pb = ps[2][:, 0:512]
        s.mm(pa, ones_r[:], strow[:, 1, :], True, True, tag="lnae")
        s.mm(pb, ones_r[:], strow[:, 0, :], True, True, tag="lnbe")
        for oc in range(8):
            s.dve("tensor_mul", kfm[:, oc, :], x2c[:, oc, :], pa, tag="lnap")
        for oc in range(8):
            s.dve("tensor_add", kfm[:, oc, :], kfm[:, oc, :], pb, tag="lnad")
        for oc in range(8):
            s.dve("tensor_scalar", kfm[:, oc, :], kfm[:, oc, :],
                  lg_s[:, oc:oc + 1], scalar2=lb_s[:, oc:oc + 1],
                  op0=ALU.mult, op1=ALU.add, tag="lngb")
        for oc in range(8):
            s.dve("tensor_copy", xnh[:, oc, :], kfm[:, oc, :],
                  tag="split_hi")
        for oc in range(8):
            s.dve("tensor_sub", xnl[:, oc, :], kfm[:, oc, :], xnh[:, oc, :],
                  tag="split_lo")

        kinds = ([(0, "q"), (1, "g"), (2, "k")] if is_F
                 else [(1, "g"), (2, "k")])
        fiters = [(kd, knm, oc8) for kd, knm in kinds for oc8 in range(8)]

        def fm_ldw(i):
            kd, knm, oc8 = fiters[i]
            oc = kd * 8 + oc8
            bi = oc % 2
            wb = x2c[:, :, 256 + bi * 128:256 + (bi + 1) * 128]
            s.dma(wb, wqkvg[:, oc * 128:(oc + 1) * 128].rearrange(
                "(c p) o -> p c o", p=128), q="dgps", tag="ldwq")
            s.dma(brow[bi][:], bqg[:, oc * 128:(oc + 1) * 128],
                  q="dgps", tag="ldbq")
            emit_split(s, wsh[bi][:], wsl[bi][:], wb)

        def fm_evac(i):
            kd, knm, oc8 = fiters[i]
            bi = (kd * 8 + oc8) % 2
            pp = ps[3 + bi][:, 0:512]
            if knm == "q":
                emit_elu1(qfm[:, oc8, :], pp, sqv[:], ev[0][:])
            elif knm == "g":
                s.act(gat[:, oc8, :], pp, AF.Sigmoid, tag="sigm")
            else:
                s.dve("tensor_mul", ev[1][:], pp, gat[:, oc8, :],
                      tag="kgate")
                emit_elu1(kfm[:, oc8, :], ev[1][:], sqv[:], ev[0][:])

        for fi in range(len(fiters)):
            kd, knm, oc8 = fiters[fi]
            bi = (kd * 8 + oc8) % 2
            if fi == 0:
                fm_ldw(0)
            pp = ps[3 + bi][:, 0:512]
            emit_proj3(s, pp,
                       lambda c: wsh[bi][:, c, :],
                       lambda c: wsl[bi][:, c, :],
                       lambda c: xnh[:, c, :], lambda c: xnl[:, c, :],
                       8, start=True, stop=False, tag="prj" + knm)
            s.mm(pp, brow[bi][:], ones_t[:], False, True, tag="bfold")
            if fi + 1 < len(fiters):
                fm_ldw(fi + 1)
            if fi > 0:
                fm_evac(fi - 1)
        fm_evac(len(fiters) - 1)

        for dvq in range(4):
            vsl = slice(dvq * 256, (dvq + 1) * 256)
            s.dma(x2c[:, :, 0:256], wvp[:, vsl].rearrange(
                "(c p) o -> p c o", p=128), q="dgps", tag="ldwv")
            s.dma(bvrow[:], bvp[:, vsl], q="dgps", tag="ldbv")
            emit_split(s, wvh[:], wvl[:], x2c[:, :, 0:256])
            for jb in range(4):
                jsl = slice(jb * 128, (jb + 1) * 128)
                pp = ps[3 + jb % 2][:, 0:256]
                first = True
                for c in range(8):
                    s.mm(pp, xnh[:, c, jsl], wvh[:, c, :], first, False,
                         tag="vprj")
                    first = False
                    s.mm(pp, xnh[:, c, jsl], wvl[:, c, :], False, False,
                         tag="vprj")
                    s.mm(pp, xnl[:, c, jsl], wvh[:, c, :], False, False,
                         tag="vprj")
                s.mm(pp, ones_r[:], bvrow[:], False, True, tag="vbias")
                s.scopy(vtm[:, jb, vsl], pp, tag="vev")

        for c in range(8):
            for jb in range(4):
                pp = ps[5 + jb % 2][:, 0:128]
                s.transpose(pp, kfm[:, c, jb * 128:(jb + 1) * 128], idf_s[:])
                s.scopy(ktm[:, jb, c * 128:(c + 1) * 128], pp, tag="ktev")

        if is_F:
            for jb in range(4):
                pp = ps[3 + jb % 2][:, 0:512]
                for c in range(8):
                    s.mm(pp, kfm[:, c, jb * 128:(jb + 1) * 128],
                         qfm[:, c, :], c == 0, c == 7, tag="atmm")
                z0 = 384 - 128 * jb
                s.dve("tensor_mul", Pt[:, jb, :], pp,
                      t1_s[:, z0:z0 + 512], tag="atmask")
            pd = ps[7][0:1, 0:512]
            for jb in range(4):
                s.mm(pd, ones_c[:], Pt[:, jb, :], jb == 0, False, tag="denA")
            for c in range(8):
                s.mm(pd, kpre[:, c:c + 1], qfm[:, c, :], False, c == 7,
                     tag="denB")
            s.scopy(r1[:], pd, tag="denev")
            s.dve("tensor_scalar_add", r1[:], r1[:], DEN_EPS, tag="deneps")
            s.dve("reciprocal", r1[:], r1[:], tag="denrc")
            pdb = ps[7][:, 0:512]
            s.mm(pdb, ones_r[:], r1[:], True, True, tag="denbc")
            s.scopy(uA[:], pdb, tag="denbev")
            for dc in range(8):
                pn = ps[dc % 3][:, 0:512]
                dsl = slice(dc * 128, (dc + 1) * 128)
                for jb in range(4):
                    s.mm(pn, vtm[:, jb, dsl], Pt[:, jb, :], jb == 0, False,
                         tag="numA")
                for c in range(8):
                    s.mm(pn, Ssb[:, c, dsl], qfm[:, c, :], False, c == 7,
                         tag="numB")
                s.dve("tensor_mul", gat[:, dc, :], pn, uA[:], tag="numev")

        for c in range(8):
            for dvh in range(2):
                pp = ps[5 + dvh][:, 0:512]
                hsl = slice(dvh * 512, (dvh + 1) * 512)
                for jb in range(4):
                    s.mm(pp, ktm[:, jb, c * 128:(c + 1) * 128],
                         vtm[:, jb, hsl], jb == 0, jb == 3, tag="supd")
                s.dve("tensor_add", Ssb[:, c, hsl], Ssb[:, c, hsl], pp,
                      tag="sacc")
        s.dve("reduce_sum", kt8[:], kfm[:], axis=AX.X, tag="kinc")
        s.dve("tensor_add", kpre[:], kpre[:], kt8[:], tag="kprea")

        if is_F:
            emit_split(s, xnh[:], xnl[:], gat[:])
            psum_s2 = ps[0][0:1, 0:512]
            psum_q2 = ps[0][32:33, 0:512]
            pre3 = vtm[:].rearrange("p a b -> p (a b)")
            for oc in range(8):
                bi = oc % 2
                wb = x2c[:, :, 256 + bi * 128:256 + (bi + 1) * 128]
                s.dma(wb, wpj[:, oc * 128:(oc + 1) * 128].rearrange(
                    "(c p) o -> p c o", p=128), q="dgps", tag="ldwp")
                s.dma(brow[bi][:], bpj[:, oc * 128:(oc + 1) * 128],
                      q="dgps", tag="ldbp")
                emit_split(s, wsh[bi][:], wsl[bi][:], wb)
                pp = ps[3 + bi][:, 0:512]
                emit_proj3(s, pp,
                           lambda c: wsh[bi][:, c, :],
                           lambda c: wsl[bi][:, c, :],
                           lambda c: xnh[:, c, :], lambda c: xnl[:, c, :],
                           8, start=True, stop=False, tag="prjo")
                s.mm(pp, brow[bi][:], ones_t[:], False, True, tag="bfoldo")
                s.dma(ev2[:], fm_view(x2)[:, oc, tsl], tag="ldx2r")
                pslc = pre3[:, oc * 512:(oc + 1) * 512]
                s.dve("tensor_add", pslc, pp, ev2[:], tag="ores")
                s.act(sqv[:], pslc, AF.Square, tag="osq")
                s.mm(psum_s2, ones_c[:], pslc, oc == 0, oc == 7, tag="osum")
                s.mm(psum_q2, ones_c[:], sqv[:], oc == 0, oc == 7, tag="osq2")
            s.scopy(strow[:, 0, :], psum_s2, tag="oev_s")
            s.scopy(strow[:, 1, :], psum_q2, tag="oev_q")
            s.act(strow[:, 0, :], strow[:, 0, :], AF.Copy, scale=1.0 / D,
                  tag="omu")
            s.act(strow[:, 1, :], strow[:, 1, :], AF.Copy, scale=1.0 / D,
                  tag="oex")
            s.dve("tensor_mul", r1[:], strow[:, 0, :], strow[:, 0, :],
                  tag="om2")
            s.dve("tensor_sub", strow[:, 1, :], strow[:, 1, :], r1[:],
                  tag="ovr")
            s.dve("tensor_scalar_add", strow[:, 1, :], strow[:, 1, :],
                  LN_EPS, tag="ove")
            s.dve("reciprocal", r1[:], strow[:, 1, :], tag="orc")
            s.act(strow[:, 1, :], r1[:], AF.Sqrt, tag="ors")
            s.dve("tensor_mul", strow[:, 0, :], strow[:, 0, :],
                  strow[:, 1, :], tag="oms")
            s.dve("tensor_scalar_mul", strow[:, 0, :], strow[:, 0, :], -1.0,
                  tag="ong")
            pa2 = ps[1][:, 0:512]
            pb2 = ps[2][:, 0:512]
            s.mm(pa2, ones_r[:], strow[:, 1, :], True, True, tag="oae")
            s.mm(pb2, ones_r[:], strow[:, 0, :], True, True, tag="obe")
            for oc in range(8):
                s.dve("tensor_mul", qfm[:, oc, :],
                      pre3[:, oc * 512:(oc + 1) * 512], pa2, tag="olm")
            for oc in range(8):
                s.dve("tensor_add", qfm[:, oc, :], qfm[:, oc, :], pb2,
                      tag="ola")
            for oc in range(8):
                s.dma(fm_view(x3D)[:, oc, tsl], qfm[:, oc, :], q="dgps",
                      tag="stx3")

    if not is_F:
        for c in range(8):
            s.dma(Sv[:, c, :], Ssb[:, c, :], q="dgps", tag="stS")
        s.dma(kprD[:, :], kpre[:], q="dgps", tag="stkp")
        if dbg:
            pass
        nc_ = s.emit()
        p.es.close()
        return nc_

    if dbg:
        dx3 = p.dram_out("dbg_x3", [D, TOKF])
        for c in range(8):
            s.dma(fm_view(dx3)[:, c, :], fm_view(x3D)[:, c, :], q="dgps",
                  tag="dbgx3")

    x2f = x2c[:].rearrange("p a b -> p (a b)")
    kf_f = kfm[:].rearrange("p a b -> p (a b)")
    ga_f = gat[:].rearrange("p a b -> p (a b)")
    xh_f = xnh[:].rearrange("p a b -> p (a b)")
    Ss_f = Ssb[:].rearrange("p a b -> p (a b)")
    qf_f = qfm[:].rearrange("p a b -> p (a b)")
    w1v = w1t.rearrange("(c p) h -> p c h", p=128)
    w2v = w2t.rearrange("(c p) d -> p c d", p=128)
    w1qv = w1qD.rearrange("(c p) h -> p c h", p=128)
    w2qv = w2qD.rearrange("(c p) d -> p c d", p=128)

    for wi, (wview, nhg, wtot) in enumerate(((w1v, 8, W1N), (w2v, 32, W1N))):
        stage = (x2c[:] if wi == 0 else
                 x2f.rearrange("p (a b) -> p a b", a=4))
        s.dve("memset", wsum[:], 0.0, tag="zws")
        for g in range(8):
            if wi == 0:
                src = wview[:, :, g * 512:(g + 1) * 512]
            else:
                src = wview[:, g * 4:(g + 1) * 4, :]
            s.dma(stage, src, tag="ldwg")
            s.act(x2f[:, 0:4096], x2f[:, 0:4096], AF.Abs, accum=acol[:],
                  tag="wabs")
            s.dve("tensor_add", wsum[:], wsum[:], acol[:], tag="wsacc")
        pt = ps[0][0:1, 0:1]
        s.mm(pt, ones_c[:], wsum[:], True, True, tag="wtot")
        s.scopy(r1[0:1, 0:1], pt, tag="wtev")
        s.act(r1[0:1, 0:1], r1[0:1, 0:1], AF.Copy, scale=1.0 / wtot,
              tag="wmean")
        s.dve("tensor_scalar_max", r1[0:1, 0:1], r1[0:1, 0:1], BIT_EPS,
              tag="wclip")
        pb = ps[1][:, 0:1]
        s.mm(pb, ones_r[:], r1[0:1, 0:1], True, True, tag="wbc")
        c0 = wi * 3
        s.dve("tensor_scalar_mul", thr[:, c0:c0 + 1], pb, 0.5, tag="thrp")
        s.dve("tensor_scalar_mul", thr[:, c0 + 1:c0 + 2], pb, -0.5,
              tag="thrn")
        s.dve("tensor_scalar_mul", thr[:, c0 + 2:c0 + 3], pb, 1.0 / QB,
              tag="thrq")
        qstage = (xnh[:] if wi == 0 else
                  xh_f.rearrange("p (a b) -> p a b", a=4))
        for g in range(8):
            if wi == 0:
                src = wview[:, :, g * 512:(g + 1) * 512]
                dst = w1qv[:, :, g * 512:(g + 1) * 512]
            else:
                src = wview[:, g * 4:(g + 1) * 4, :]
                dst = w2qv[:, g * 4:(g + 1) * 4, :]
            s.dma(stage, src, tag="ldwg2")
            s.dve("tensor_scalar", kf_f[:, 0:4096], x2f[:, 0:4096],
                  thr[:, c0:c0 + 1], scalar2=None, op0=ALU.is_ge, tag="tern1")
            s.dve("tensor_scalar", ga_f[:, 0:4096], x2f[:, 0:4096],
                  thr[:, c0 + 1:c0 + 2], scalar2=None, op0=ALU.is_le,
                  tag="tern2")
            s.dve("tensor_sub", xh_f[:, 0:4096], kf_f[:, 0:4096],
                  ga_f[:, 0:4096], tag="tern3")
            s.dma(dst, qstage, q="dgps", tag="stwq")

    RCONST = [128.0, 64.0, 32.0, 16.0, 8.0, 4.0, 2.0, 1.0]

    def emit_round_group(srcs, qpsum, dsts, ybufs, ubufs, sqb):
        n = len(srcs)
        for g in range(n):
            s.dve("tensor_mul", ybufs[g], srcs[g], qpsum, tag="rny")
        for g in range(n):
            s.dve("tensor_mul", ubufs[g], srcs[g], qpsum, tag="rnu")
        for ckst in RCONST:
            for g in range(n):
                s.act(sqb[g % len(sqb)], ubufs[g], AF.Sign,
                      bias=bneg[:, 0:1], tag="rsgn")
                s.dve("scalar_tensor_tensor", ubufs[g], sqb[g % len(sqb)],
                      -ckst / 2.0, ubufs[g], op0=ALU.mult, op1=ALU.add,
                      tag="rupd")
        for g in range(n):
            s.dve("scalar_tensor_tensor", ybufs[g], ubufs[g], -1.0,
                  ybufs[g], op0=ALU.mult, op1=ALU.add, tag="rfin")
        for g in range(n):
            s.dve("tensor_scalar_add", dsts[g], ybufs[g], 0.5, tag="rcast")

    def hbuf(hc):
        if hc < 16:
            return Ss_f[:, hc * 512:(hc + 1) * 512]
        if hc < 24:
            return qf_f[:, (hc - 16) * 512:(hc - 15) * 512]
        return kf_f[:, (hc - 24) * 512:(hc - 23) * 512]

    def emit_rows(par, col0, thr_col):
        s.dve("tensor_scalar_max", xst[:], xst[:], BIT_EPS, tag="xsc")
        pr = ps[6][0:4, 0:128]
        s.mm(pr, xst[:], idf_s[:], True, True, tag="xsrow")
        s.dve("reciprocal", xr4[:], pr, tag="xsrc")
        s.dve("tensor_scalar_mul", xr4[:], xr4[:], QB, tag="xsq")
        s.dma(rawD[par, :, col0:col0 + 128], xr4[:], q="dgps", tag="strow")
        s.dma(rowq[:].rearrange("a (b c) -> a b c", b=4),
              rawD[par:par + 1, :, col0:col0 + 128], tag="ldrow")
        s.dve("tensor_scalar_mul", xr4[:], pr, thr_col, tag="xssc")
        s.dma(rawD[par, :, col0 + 128:col0 + 256], xr4[:], q="dgps",
              tag="strow2")
        s.dma(rowsc[:].rearrange("a (b c) -> a b c", b=4),
              rawD[par:par + 1, :, col0 + 128:col0 + 256], tag="ldrow2")

    w2pieces = [xnh[:, :, 0:256], xnh[:, :, 256:512], wvh[:], wvl[:]]
    for tt in range(4):
        par = tt % 2
        tsl = slice(tt * CH, (tt + 1) * CH)
        for oc in range(8):
            s.dma(x2c[:, oc, :], fm_view(x3D)[:, oc, tsl], tag="ldx3")
        s.dve("tensor_reduce", ev[0][:],
              x2c[:].rearrange("p c t -> p t c"), axis=AX.X, op=ALU.max,
              apply_absolute_value=True, tag="absr")
        for sub in range(4):
            pp = ps[5][:, 0:128]
            s.transpose(pp, ev[0][:, sub * 128:(sub + 1) * 128], idf_s[:])
            s.dve("reduce_max", xst[:, sub:sub + 1], pp, axis=AX.X,
                  tag="xsr")
        emit_rows(par, 0, thr[0:4, 2:3])
        pqb = ps[2][:, 0:512]
        psc = ps[1][:, 0:512]
        s.mm(pqb, ones_r[:], rowq[:], True, True, tag="qbc")
        s.mm(psc, ones_r[:], rowsc[:], True, True, tag="scbc")
        s.scopy(sc2s[:], psc, tag="sc1ev")
        sqb4 = [ev[0][:], ev[1][:], sqv[:], uA[:]]
        emit_round_group([x2c[:, c, :] for c in range(8)], pqb,
                         [xnh[:, c, :] for c in range(8)],
                         [Ss_f[:, g * 512:(g + 1) * 512] for g in range(8)],
                         [Ss_f[:, (8 + g) * 512:(9 + g) * 512]
                          for g in range(8)], sqb4)
        vt_f = vtm[:].rearrange("p a b -> p (a b)")
        kt_f = ktm[:].rearrange("p a b -> p (a b)")
        for hc in range(32):
            bi = hc % 2
            s.dma(wsh[bi][:], w1qv[:, :, hc * 128:(hc + 1) * 128],
                  tag="ldw1q")
            pp = ps[3 + bi][:, 0:512]
            for c in range(8):
                s.mm(pp, wsh[bi][:, c, :], xnh[:, c, :], c == 0, c == 7,
                     tag="bl1")
            s.dve("tensor_mul", ev[0][:], pp, sc2s[:], tag="dq1")
            s.act(hbuf(hc), ev[0][:], AF.Gelu, bias=b1_s[:, hc:hc + 1],
                  tag="gelu1")
        s.dve("tensor_reduce", ev[0][:],
              Ssb[:].rearrange("p c v -> p (c v)").rearrange(
                  "p (a t) -> p t a", t=512), axis=AX.X, op=ALU.max,
              apply_absolute_value=True, tag="absr2a")
        s.dve("tensor_reduce", ev2[:],
              qfm[:].rearrange("p c t -> p t c"), axis=AX.X, op=ALU.max,
              apply_absolute_value=True, tag="absr2b")
        s.dve("tensor_max", ev[0][:], ev[0][:], ev2[:], tag="absm1")
        s.dve("tensor_reduce", ev2[:],
              kfm[:].rearrange("p c t -> p t c"), axis=AX.X, op=ALU.max,
              apply_absolute_value=True, tag="absr2c")
        s.dve("tensor_max", ev[0][:], ev[0][:], ev2[:], tag="absm2")
        for sub in range(4):
            pp = ps[5][:, 0:128]
            s.transpose(pp, ev[0][:, sub * 128:(sub + 1) * 128], idf_s[:])
            s.dve("reduce_max", xst[:, sub:sub + 1], pp, axis=AX.X,
                  tag="xsr2")
        emit_rows(par, 256, thr[0:4, 5:6])
        s.mm(pqb, ones_r[:], rowq[:], True, True, tag="qbc2")
        s.mm(psc, ones_r[:], rowsc[:], True, True, tag="scbc2")
        s.scopy(sc2s[:], psc, tag="sc2ev")
        qb2s = gat[:, 0, :]
        s.scopy(qb2s, pqb, tag="qb2ev")
        for G in range(8):
            gb0 = (G % 2) * 4
            hqb = xnl[:, gb0:gb0 + 4, :]
            for pc in range(4):
                s.dma(w2pieces[pc][:, 0:4, :],
                      w2qv[:, G * 4:(G + 1) * 4, pc * 256:(pc + 1) * 256],
                      q="dgps", tag="ldw2q")
            emit_round_group([hbuf(G * 4 + g) for g in range(4)], qb2s,
                             [hqb[:, g, :] for g in range(4)],
                             [vt_f[:, g * 512:(g + 1) * 512]
                              for g in range(4)],
                             [kt_f[:, g * 512:(g + 1) * 512]
                              for g in range(4)], sqb4)
            for oc in range(8):
                pp = ps[oc][:, 0:512]
                rem = (oc % 2) * 128
                for hc8 in range(4):
                    wap = w2pieces[oc // 2][:, hc8, rem:rem + 128]
                    s.mm(pp, wap, hqb[:, hc8, :], G == 0 and hc8 == 0,
                         G == 7 and hc8 == 3, tag="bl2")
        pre4 = ktm[:].rearrange("p a b -> p (a b)")
        for oc in range(8):
            s.dve("tensor_mul", vt_f[:, oc * 512:(oc + 1) * 512],
                  ps[oc][:, 0:512], sc2s[:], tag="dq2")
        for oc in range(8):
            s.dve("tensor_scalar_add", vt_f[:, oc * 512:(oc + 1) * 512],
                  vt_f[:, oc * 512:(oc + 1) * 512], b2_s[:, oc:oc + 1],
                  tag="b2a")
        for oc in range(8):
            s.dve("tensor_add", pre4[:, oc * 512:(oc + 1) * 512],
                  vt_f[:, oc * 512:(oc + 1) * 512], x2c[:, oc, :],
                  tag="fres")
        psum_s3 = ps[0][0:1, 0:512]
        psum_q3 = ps[0][32:33, 0:512]
        for oc in range(8):
            pslc = pre4[:, oc * 512:(oc + 1) * 512]
            s.act(ev2[:], pslc, AF.Square, tag="fsq")
            s.mm(psum_s3, ones_c[:], pslc, oc == 0, oc == 7, tag="fsum")
            s.mm(psum_q3, ones_c[:], ev2[:], oc == 0, oc == 7, tag="fsq2")
        s.scopy(strow[:, 0, :], psum_s3, tag="fev_s")
        s.scopy(strow[:, 1, :], psum_q3, tag="fev_q")
        s.act(strow[:, 0, :], strow[:, 0, :], AF.Copy, scale=1.0 / D,
              tag="fmu2")
        s.act(strow[:, 1, :], strow[:, 1, :], AF.Copy, scale=1.0 / D,
              tag="fex2")
        s.dve("tensor_mul", r1[:], strow[:, 0, :], strow[:, 0, :], tag="fm22")
        s.dve("tensor_sub", strow[:, 1, :], strow[:, 1, :], r1[:], tag="fvr2")
        s.dve("tensor_scalar_add", strow[:, 1, :], strow[:, 1, :], LN_EPS,
              tag="fve2")
        s.dve("reciprocal", r1[:], strow[:, 1, :], tag="frc2")
        s.act(strow[:, 1, :], r1[:], AF.Sqrt, tag="frs2")
        s.dve("tensor_mul", strow[:, 0, :], strow[:, 0, :], strow[:, 1, :],
              tag="fms2")
        s.dve("tensor_scalar_mul", strow[:, 0, :], strow[:, 0, :], -1.0,
              tag="fng2")
        pa3 = ps[1][:, 0:512]
        pb3 = ps[2][:, 0:512]
        s.mm(pa3, ones_r[:], strow[:, 1, :], True, True, tag="fae")
        s.mm(pb3, ones_r[:], strow[:, 0, :], True, True, tag="fbe")
        for oc in range(8):
            s.dve("tensor_mul", vt_f[:, oc * 512:(oc + 1) * 512],
                  pre4[:, oc * 512:(oc + 1) * 512], pa3, tag="flm2")
        for oc in range(8):
            s.dve("tensor_add", vt_f[:, oc * 512:(oc + 1) * 512],
                  vt_f[:, oc * 512:(oc + 1) * 512], pb3, tag="fla2")
        for oc in range(8):
            s.dma(fm_view(xout)[:, oc, tsl],
                  vt_f[:, oc * 512:(oc + 1) * 512], q="dgps", tag="stxf")

    nc_ = s.emit()
    p.es.close()
    return nc_


def host_t1m():
    pq = np.arange(128)[:, None]
    ii = np.arange(896)[None, :]
    return np.ascontiguousarray((ii >= pq + 384).astype(np.float32))


def _l1_common(x2_core, w_qkv, b_qkv, w_gate, b_gate, ln_g, ln_b):
    wqkvg = np.concatenate([w_qkv[:, 0:D], w_gate, w_qkv[:, D:2 * D]], 1)
    bqg = np.concatenate([b_qkv[0:D], b_gate, b_qkv[D:2 * D]])[None]
    return {
        "x2T": np.ascontiguousarray(np.asarray(x2_core, np.float32).T),
        "wqkvg": np.ascontiguousarray(wqkvg, dtype=np.float32),
        "wv": np.ascontiguousarray(w_qkv[:, 2 * D:3 * D], dtype=np.float32),
        "bqg": np.ascontiguousarray(bqg, dtype=np.float32),
        "bv": np.ascontiguousarray(b_qkv[2 * D:3 * D][None],
                                   dtype=np.float32),
        "lgA": np.ascontiguousarray(np.asarray(ln_g, np.float32
                                               ).reshape(8, 128).T),
        "lbA": np.ascontiguousarray(np.asarray(ln_b, np.float32
                                               ).reshape(8, 128).T),
        "idf": np.eye(128, dtype=np.float32),
    }


def make_E_inputs(x2_core, w_qkv, b_qkv, w_gate, b_gate, ln_g, ln_b):
    return _l1_common(x2_core, w_qkv, b_qkv, w_gate, b_gate, ln_g, ln_b)


def make_F_inputs(x2_core, w_qkv, b_qkv, w_gate, b_gate, ln_g, ln_b,
                  w_proj, b_proj, W1, b1, W2, b2, preS, preK):
    ins = _l1_common(x2_core, w_qkv, b_qkv, w_gate, b_gate, ln_g, ln_b)
    ins.update({
        "preS": np.ascontiguousarray(preS, dtype=np.float32),
        "preK": np.ascontiguousarray(preK, dtype=np.float32),
        "wpj": np.ascontiguousarray(w_proj, dtype=np.float32),
        "bpj": np.ascontiguousarray(np.asarray(b_proj, np.float32)[None]),
        "t1m": host_t1m(),
        "w1t": np.ascontiguousarray(np.asarray(W1, np.float32).T),
        "b1c": np.ascontiguousarray(np.asarray(b1, np.float32
                                               ).reshape(32, 128).T),
        "w2t": np.ascontiguousarray(np.asarray(W2, np.float32).T),
        "b2c8": np.ascontiguousarray(np.asarray(b2, np.float32
                                                ).reshape(8, 128).T),
    })
    return ins


_NC_CACHE = {}


def _get_nc(name):
    if name not in _NC_CACHE:
        if name == "A":
            _NC_CACHE[name] = build_A()
        elif name == "B":
            _NC_CACHE[name] = build_B()
        elif name == "C":
            _NC_CACHE[name] = build_C(block_ln=False)
        elif name == "D":
            _NC_CACHE[name] = build_C(block_ln=True)
        elif name == "E":
            _NC_CACHE[name] = build_EF(False)
        elif name == "F":
            _NC_CACHE[name] = build_EF(True)
    return _NC_CACHE[name]


def _launch(name, in_maps):
    res = run_bass_kernel_spmd(_get_nc(name), in_maps,
                               core_ids=list(range(NCORES)))
    return res.results


def kernel(**inputs):
    f32c = lambda a: np.ascontiguousarray(np.asarray(a, np.float32))
    x0 = f32c(inputs["L0_attn_x"])
    WQ = f32c(inputs["L0_attn_W_Q"])
    WK = f32c(inputs["L0_attn_W_K"])
    WV = f32c(inputs["L0_attn_W_V"])
    WO = f32c(inputs["L0_attn_W_O"])
    wqkv = f32c(inputs["L0_ffn_wqkv"])
    qsc = f32c(inputs["L0_ffn_q_scale"])
    ksc = f32c(inputs["L0_ffn_k_scale"])
    wo2 = f32c(inputs["L0_ffn_wo"])
    lng = f32c(inputs["L0_ffn_ln_g"])
    lnb = f32c(inputs["L0_ffn_ln_b"])
    w1 = f32c(inputs["L0_ffn_w1"])
    b1 = f32c(inputs["L0_ffn_b1"])
    w2 = f32c(inputs["L0_ffn_w2"])
    b2 = f32c(inputs["L0_ffn_b2"])
    wqkv1 = f32c(inputs["L1_attn_w_qkv"])
    bqkv1 = f32c(inputs["L1_attn_b_qkv"])
    wg1 = f32c(inputs["L1_attn_w_gate"])
    bg1 = f32c(inputs["L1_attn_b_gate"])
    wp1 = f32c(inputs["L1_attn_w_proj"])
    bp1 = f32c(inputs["L1_attn_b_proj"])
    lng1 = f32c(inputs["L1_attn_ln_g"])
    lnb1 = f32c(inputs["L1_attn_ln_b"])
    FW1 = f32c(inputs["L1_ffn_W1"])
    Fb1 = f32c(inputs["L1_ffn_b1"])
    FW2 = f32c(inputs["L1_ffn_W2"])
    Fb2 = f32c(inputs["L1_ffn_b2"])

    cores = [(c // SEQC, c % SEQC) for c in range(NCORES)]

    insA = [{"x0T": np.ascontiguousarray(x0[b, q * TOK:(q + 1) * TOK].T),
             "wq": WQ, "wk": WK, "wv": WV} for b, q in cores]
    outA = _launch("A", insA)

    kron4 = _bf16(np.kron(np.eye(4, dtype=np.float32),
                          np.ones((1, 32), np.float32)))
    kpB = {}
    vhiB = {}
    vloB = {}
    for b in range(B):
        kpB[b] = np.ascontiguousarray(np.concatenate(
            [np.asarray(outA[b * SEQC + q]["kp"]) for q in range(SEQC)], 2))
        vf = np.concatenate(
            [np.asarray(outA[b * SEQC + q]["vfull"]) for q in range(SEQC)],
            0)
        vhiB[b] = np.ascontiguousarray(vf.transpose(1, 0, 2))
    xpg = np.stack([np.concatenate(
        [np.asarray(outA[b * SEQC + q]["xp"]) for q in range(SEQC)], 1)
        for b in range(B)]).astype(np.float32)
    insB = []
    for b, q in cores:
        qp = np.asarray(outA[b * SEQC + q]["qp"])
        es = np.zeros((TR, TOK // BM), np.float32)
        for mcol in range(TOK // BM):
            es[q * (TOK // BM) + mcol, mcol] = 1.0
        insB.append({
            "x0T": np.ascontiguousarray(x0[b, q * TOK:(q + 1) * TOK].T),
            "qp2": np.ascontiguousarray(
                np.concatenate([qp[0:64], qp[0:64]], 0)),
            "qlo": np.ascontiguousarray(qp[64:128]),
            "kpB": kpB[b], "vfB": vhiB[b],
            "xpg": xpg, "wq": WQ, "wk": WK, "wo": WO,
            "kron4": kron4, "esel": _bf16(es),
        })
    outB = _launch("B", insB)

    tabs = host_rotary_tables()
    x1full = {b: np.concatenate(
        [np.asarray(outB[b * SEQC + q]["x1T"]).T for q in range(SEQC)], 0)
        for b in range(B)}
    insC = [make_C_inputs(x1full[b], None, wqkv[0], qsc[0], ksc[0], wo2[0],
                          lng[0], lnb[0], w1[0], b1[0], w2[0], b2[0], q,
                          tabs) for b, q in cores]
    outC = _launch("C", insC)

    yfull = {b: np.concatenate(
        [np.asarray(outC[b * SEQC + q]["xoT"]).T for q in range(SEQC)], 0)
        for b in range(B)}
    insD = [make_C_inputs(yfull[b], x1full[b][q * TOK:(q + 1) * TOK],
                          wqkv[1], qsc[1], ksc[1], wo2[1], lng[1], lnb[1],
                          w1[1], b1[1], w2[1], b2[1], q, tabs)
            for b, q in cores]
    outD = _launch("D", insD)

    x2core = {c: np.asarray(outD[c]["xoT"]).T for c in range(NCORES)}
    insE = [make_E_inputs(x2core[b * SEQC + q], wqkv1, bqkv1, wg1, bg1,
                          lng1, lnb1) for b, q in cores]
    outE = _launch("E", insE)

    insF = []
    for b, q in cores:
        preS = np.zeros((3, D, D), np.float32)
        preK = np.zeros((3, 128, 8), np.float32)
        for j in range(q):
            preS[j] = np.asarray(outE[b * SEQC + j]["SD"])
            preK[j] = np.asarray(outE[b * SEQC + j]["kprD"])
        insF.append(make_F_inputs(x2core[b * SEQC + q], wqkv1, bqkv1, wg1,
                                  bg1, lng1, lnb1, wp1, bp1, FW1, Fb1, FW2,
                                  Fb2, preS, preK))
    outF = _launch("F", insF)

    out = np.stack([np.concatenate(
        [np.asarray(outF[b * SEQC + q]["xoT"]).T for q in range(SEQC)], 0)
        for b in range(B)])
    return np.ascontiguousarray(out.astype(np.float32))


# version 10
# speedup vs baseline: 1.1039x; 1.1039x over previous
import math
from contextlib import ExitStack

import numpy as np

import concourse.bass as bass
import concourse.mybir as mybir
from concourse.bass_types import AP
from concourse.bass_utils import run_bass_kernel_spmd

F32 = mybir.dt.float32
BF16 = mybir.dt.bfloat16
AF = mybir.ActivationFunctionType
ALU = mybir.AluOpType
AX = mybir.AxisListType
ds = bass.ds

B = 2
N = 8192
D = 1024
H = 16
DH = 64
BM = 32
TR = N // BM
TOPK = 51
NCORES = 8
TOK = N * B // NCORES
SEQC = 4

FH, FD = 8, 64
FI = FH * FD
WIN = 512
HID = int(D * 4 * 2 // 3)
QK_SCALE = 8.0

NEG = -1.0e30
LN_EPS = 1e-5
DEN_EPS = 1e-6
QB = 127.0
BIT_EPS = 1e-5
RMAGIC = 12582912.0


def _bf16(x):
    import ml_dtypes
    return np.ascontiguousarray(np.asarray(x).astype(ml_dtypes.bfloat16))


DMA_STREAMS = {"dsync": "sync", "dgps": "gpsimd"}
ENGS = ("tensor", "vector", "scalar", "gpsimd", "sync")


class Op:
    __slots__ = ("stream", "fn", "reads", "writes", "idx", "deps",
                 "inc_needed", "is_dma", "tag", "same_raw")

    def __init__(self, stream, fn, reads, writes, idx, is_dma, tag):
        self.stream, self.fn = stream, fn
        self.reads, self.writes = reads, writes
        self.idx = idx
        self.deps = {}
        self.inc_needed = False
        self.is_dma = is_dma
        self.tag = tag
        self.same_raw = -1


class Prog:
    def __init__(self, name):
        self.nc = bass.Bass("TRN2", target_bir_lowering=False, debug=False)
        self.es = ExitStack()
        self.name = name

    def dram_in(self, name, shape, dt=F32):
        return self.nc.dram_tensor(name, list(shape), dt, kind="ExternalInput").ap()

    def dram_out(self, name, shape, dt=F32):
        return self.nc.dram_tensor(name, list(shape), dt, kind="ExternalOutput").ap()

    def sb(self, name, shape, dt=F32):
        return self.es.enter_context(self.nc.sbuf_tensor(name, list(shape), dt))

    def ps(self, name, shape):
        return self.es.enter_context(self.nc.psum_tensor(name, list(shape), F32))

    def sem(self, name):
        return self.es.enter_context(self.nc.semaphore(name))


class Sched:
    def __init__(self, prog):
        self.p = prog
        self.ops = {s: [] for s in list(ENGS) + list(DMA_STREAMS)}
        self.last_w = {}
        self.readers = {}

    @staticmethod
    def _names(lst):
        out = []
        for x in lst or ():
            if x is None:
                continue
            if isinstance(x, AP):
                try:
                    ext = x.offset + 1
                    ok = True
                    for st, cnt in x.ap:
                        if st < 0:
                            ok = False
                            break
                        ext += (cnt - 1) * st
                    out.append((x.tensor.name,
                                (x.offset, ext) if ok else None))
                except Exception:
                    out.append((x.tensor.name, None))
            else:
                out.append((x.name if hasattr(x, "name") else str(x), None))
        return out

    @staticmethod
    def _ovl(a, b):
        if a is None or b is None:
            return True
        return a[0] < b[1] and b[0] < a[1]

    @staticmethod
    def _covers(a, b):
        if a is None or b is None:
            return False
        return a[0] <= b[0] and b[1] <= a[1]

    def add(self, stream, fn, reads=(), writes=(), tag=""):
        idx = len(self.ops[stream])
        op = Op(stream, fn, self._names(reads), self._names(writes), idx,
                stream in DMA_STREAMS, tag)
        deps = {}

        def _dep(s, i, raw=False):
            if s == stream and s not in DMA_STREAMS:
                if raw:
                    op.same_raw = max(op.same_raw, i)
                return
            key = (s, i % Sched.DMA_POOL) if s in DMA_STREAMS else (s, None)
            if deps.get(key, -1) < i:
                deps[key] = i

        for b, ivl in op.reads:
            for (s2, _sl, i, wivl) in self.last_w.get(b, ()):
                if Sched._ovl(ivl, wivl):
                    _dep(s2, i, raw=True)
        for b, ivl in op.writes:
            for (s2, _sl, i, wivl) in self.last_w.get(b, ()):
                if Sched._ovl(ivl, wivl):
                    _dep(s2, i, raw=True)
            for (rs, ri, rivl) in self.readers.get(b, ()):
                if Sched._ovl(ivl, rivl):
                    _dep(rs, ri, raw=True)
        op.deps = deps
        for (s2, _slot), i in deps.items():
            self.ops[s2][i].inc_needed = True
        slot = idx % Sched.DMA_POOL if op.is_dma else None
        for b, ivl in op.writes:
            old = self.last_w.get(b, ())
            kept = []
            for e in old:
                s2, sl2, i2, ivl2 = e
                if Sched._covers(ivl, ivl2) and not (
                        s2 == stream and op.is_dma and sl2 != slot):
                    continue
                kept.append(e)
            kept.append((stream, slot, idx, ivl))
            self.last_w[b] = kept
            rd = self.readers.get(b)
            if rd:
                self.readers[b] = [r for r in rd
                                   if not Sched._covers(ivl, r[2])]
        for b, ivl in op.reads:
            self.readers.setdefault(b, []).append((stream, idx, ivl))
        self.ops[stream].append(op)
        return op

    def mm(self, psum, lhsT, rhs, start, stop, extra_reads=(), tag="mm"):
        self.add("tensor",
                 lambda t: t.matmul(psum, lhsT, rhs, start=start, stop=stop),
                 reads=[lhsT, rhs] + list(extra_reads), writes=[psum], tag=tag)

    def transpose(self, psum, in_, ident, tag="tr"):
        self.add("tensor", lambda t: t.transpose(psum, in_, ident),
                 reads=[in_, ident], writes=[psum], tag=tag)

    def dve(self, meth, out, *ins, extra_reads=(), extra_writes=(), tag="", **kw):
        self.add("vector", lambda v: getattr(v, meth)(out, *ins, **kw),
                 reads=list(ins) + list(extra_reads),
                 writes=[out] + list(extra_writes), tag=tag or meth)

    def act(self, out, in_, func, bias=0.0, scale=1.0, accum=None, tag="act"):
        extra = [b for b in (bias, scale) if isinstance(b, AP)]

        def fn(s):
            kw = dict(bias=bias, scale=scale)
            if accum is not None:
                kw["accum_out"] = accum
            return s.activation(out, in_, func, **kw)
        self.add("scalar", fn, reads=[in_] + extra,
                 writes=[out] + ([accum] if accum is not None else []), tag=tag)

    def scopy(self, out, in_, tag="scopy"):
        self.add("scalar", lambda s: s.copy(out, in_), reads=[in_],
                 writes=[out], tag=tag)

    def dma(self, out, in_, q="dsync", tag="dma", slow=False):
        if slow:
            self.add(q, lambda e: e.dma_start(
                out=out, in_=in_, allow_slow_non_contiguous=True),
                reads=[in_], writes=[out], tag=tag)
        else:
            self.add(q, lambda e: e.dma_start(out=out, in_=in_),
                     reads=[in_], writes=[out], tag=tag)

    DMA_POOL = 8

    def emit(self):
        p = self.p
        nc = p.nc
        P = self.DMA_POOL
        live = {s: lst for s, lst in self.ops.items() if lst}
        sems = {}
        for s in live:
            if s in DMA_STREAMS:
                sems[s] = [p.sem(f"S_{s}_{j}") for j in range(P)]
            else:
                sems[s] = p.sem("S_" + s)

        prefix = {}
        for s, lst in live.items():
            if s in DMA_STREAMS:
                continue
            acc, pl = 0, []
            for op in lst:
                if op.inc_needed or op is lst[-1]:
                    acc += 1
                pl.append(acc)
            prefix[s] = pl

        def dep_wait(eng, s2, i, satisfied):
            if s2 in DMA_STREAMS:
                key = (s2, i % P)
                need = 16 * (i // P + 1)
                sem = sems[s2][i % P]
            else:
                key = s2
                need = prefix[s2][i]
                assert (self.ops[s2][i].inc_needed or
                        self.ops[s2][i] is live[s2][-1])
                sem = sems[s2]
            if satisfied.get(key, 0) < need:
                eng.wait_ge(sem, need)
                satisfied[key] = need

        with nc.Block() as blk:
            for s, lst in live.items():
                eng_name = DMA_STREAMS.get(s, s)
                deco = getattr(blk, eng_name)

                def make(s=s, lst=lst, eng_name=eng_name):
                    def body(eng):
                        satisfied = {}
                        dve_like = eng_name in ("vector", "scalar", "gpsimd")
                        is_dma_stream = s in DMA_STREAMS
                        flushed = 0
                        for op in lst:
                            for key in sorted(op.deps, key=str):
                                dep_wait(eng, key[0], op.deps[key], satisfied)
                            if is_dma_stream and op.idx >= P:
                                j = op.idx % P
                                eng.wait_ge(sems[s][j], 16 * (op.idx // P))
                            if dve_like and op.same_raw >= flushed:
                                eng.drain()
                                flushed = op.idx
                            inst = op.fn(eng)
                            if is_dma_stream:
                                inst.then_inc(sems[s][op.idx % P], 16)
                            else:
                                if op.inc_needed or op is lst[-1]:
                                    inst.then_inc(sems[s], 1)
                    return body

                deco(make())

            @blk.sync
            def _(sy):
                for s, lst in sorted(live.items()):
                    n = len(lst)
                    if s in DMA_STREAMS:
                        for j in range(min(n, P)):
                            cnt = (n - 1 - j) // P + 1
                            sy.wait_ge(sems[s][j], 16 * cnt)
                    else:
                        sy.wait_ge(sems[s], prefix[s][-1])

        return nc


def fm_view(dram_ap):
    return dram_ap.rearrange("(c p) t -> p c t", p=128)


def emit_split(s, hi, lo, src):
    s.dve("tensor_copy", hi, src, tag="split_hi")
    s.dve("tensor_sub", lo, src, hi, tag="split_lo")


def emit_proj3(s, psum, whi, wlo, xhi, xlo, nk, start=True, stop=True,
               tag="proj"):
    first = start
    for c in range(nk):
        s.mm(psum, whi(c), xhi(c), first, False, tag=tag)
        first = False
        s.mm(psum, whi(c), xlo(c), False, False, tag=tag)
        s.mm(psum, wlo(c), xhi(c), False, stop and (c == nk - 1), tag=tag)


def build_A():
    p = Prog("A")
    s = Sched(p)
    x0 = p.dram_in("x0T", [D, TOK])
    w_in = [p.dram_in(n, [D, D]) for n in ("wq", "wk", "wv")]
    qp = p.dram_out("qp", [128, H, TOK], BF16)
    kp = p.dram_out("kp", [128, H, TOK], BF16)
    vfull = p.dram_out("vfull", [TOK, H, DH])
    xpo = p.dram_out("xp", [D, TOK // BM])

    NT = 512
    ntt = TOK // NT
    xsc = [p.sb(f"xsc{i}", [128, TOK]) for i in range(2)]
    xhi = p.sb("xhi", [128, 8, TOK], BF16)
    xlo = p.sb("xlo", [128, 8, TOK], BF16)
    xpt = p.sb("xpt", [128, 8, TOK // BM])
    wbuf = [p.sb(f"wbuf{i}", [128, 8, 512]) for i in range(2)]
    whi = p.sb("whi", [128, 8, 1024], BF16)
    wlo = p.sb("wlo", [128, 8, 1024], BF16)
    evq = [p.sb(f"evq{i}", [128, NT]) for i in range(2)]
    evk = [p.sb(f"evk{i}", [128, NT]) for i in range(2)]
    evv = [p.sb(f"evv{i}", [128, NT]) for i in range(2)]
    hiq = [p.sb(f"hiq{i}", [128, NT], BF16) for i in range(2)]
    loq = [p.sb(f"loq{i}", [128, NT], BF16) for i in range(2)]
    hik = [p.sb(f"hik{i}", [128, NT], BF16) for i in range(2)]
    lok = [p.sb(f"lok{i}", [128, NT], BF16) for i in range(2)]
    hiv = [p.sb(f"hiv{i}", [128, NT], BF16) for i in range(2)]
    lov = [p.sb(f"lov{i}", [128, NT], BF16) for i in range(2)]
    pq = [p.ps(f"pq{i}", [128, NT]) for i in range(2)]
    pk = [p.ps(f"pk{i}", [128, NT]) for i in range(2)]
    pv = [p.ps(f"pv{i}", [128, NT]) for i in range(2)]

    for c in range(8):
        xb = xsc[c % 2]
        s.dma(xb[:], fm_view(x0)[:, c, :], tag=f"ldx{c}")
        emit_split(s, xhi[:, c, :], xlo[:, c, :], xb[:])
        s.dve("reduce_sum", xpt[:, c, :],
              xb[:].rearrange("p (r w) -> p r w", w=BM), axis=AX.X, tag="pool")
    s.dve("tensor_scalar_mul", xpt[:], xpt[:], 1.0 / BM, tag="poolscale")
    s.dma(fm_view(xpo), xpt[:], tag="stxp")

    for wi, wsrc in enumerate(w_in):
        for hf in range(2):
            wb = wbuf[hf]
            cols = slice(hf * 512, (hf + 1) * 512)
            s.dma(wb[:], wsrc[:, cols].rearrange("(c p) o -> p c o", p=128),
                  q="dgps", tag=f"ldw{wi}_{hf}")
            emit_split(s, whi[:, :, cols], wlo[:, :, cols], wb[:])

        if wi < 2:
            out_d = (qp, kp)
            ev = (evq, evk)[wi]
            hi_t = (hiq, hik)[wi]
            lo_t = (loq, lok)[wi]
            pp = (pq, pk)[wi]
            for oc in range(8):
                for tt in range(ntt):
                    bi = (oc * ntt + tt) % 2
                    sl = slice(tt * NT, (tt + 1) * NT)
                    ocs = slice(oc * 128, (oc + 1) * 128)
                    emit_proj3(
                        s, pp[bi][:],
                        lambda c: whi[:, c, ocs], lambda c: wlo[:, c, ocs],
                        lambda c: xhi[:, c, sl], lambda c: xlo[:, c, sl], 8,
                        tag=f"proj{wi}")
                    s.scopy(ev[bi][:], pp[bi][:], tag="evac")
                    emit_split(s, hi_t[bi][:], lo_t[bi][:], ev[bi][:])
                    for hh in range(2):
                        h_ = 2 * oc + hh
                        rows = slice(hh * 64, (hh + 1) * 64)
                        s.dma(out_d[wi][0:64, h_, sl], hi_t[bi][rows, :],
                              q="dgps", tag="stqk_hi")
                        s.dma(out_d[wi][64:128, h_, sl], lo_t[bi][rows, :],
                              q="dgps", tag="stqk_lo")
        else:
            for tt in range(TOK // 128):
                for half in range(2):
                    bi = (tt * 2 + half) % 2
                    ocs = slice(half * 512, (half + 1) * 512)
                    tsl = slice(tt * 128, (tt + 1) * 128)
                    first = True
                    for c in range(8):
                        s.mm(pv[bi][:], xhi[:, c, tsl], whi[:, c, ocs],
                             first, False, tag="projV")
                        first = False
                        s.mm(pv[bi][:], xhi[:, c, tsl], wlo[:, c, ocs],
                             False, False, tag="projV")
                        s.mm(pv[bi][:], xlo[:, c, tsl], whi[:, c, ocs],
                             False, c == 7, tag="projV")
                    s.scopy(evv[bi][:], pv[bi][:], tag="evacV")
                    h0 = half * 8
                    s.dma(vfull[tsl, h0:h0 + 8, :],
                          evv[bi][:].rearrange("t (h d) -> t h d", d=DH),
                          q="dgps", tag="stv")

    nc = s.emit()
    p.es.close()
    return nc


XT = 3584
MARG = 1536
KV0 = 512
Q0 = 1024
NQW = 5
HID_P = 2816
NHC = HID_P // 128


def build_C(block_ln, dbg=False):
    p = Prog("C")
    s = Sched(p)
    nc = p.nc
    xE = p.dram_in("xET", [D, XT])
    x1 = p.dram_in("x1T", [D, TOK])
    wqkv = p.dram_in("wqkv", [D, 3 * FI])
    wo = p.dram_in("wo", [FI, D])
    w1p = p.dram_in("w1p", [D, 2, HID_P])
    w2p = p.dram_in("w2p", [HID_P, D])
    qs128 = p.dram_in("qs128", [128, 1])
    ks128 = p.dram_in("ks128", [128, 1])
    lg128 = p.dram_in("lg128", [128, 8])
    lb128 = p.dram_in("lb128", [128, 8])
    b1a128 = p.dram_in("b1a128", [128, NHC])
    b1g128 = p.dram_in("b1g128", [128, NHC])
    b2c = p.dram_in("b2c", [128, 8])
    tabs = {n: p.dram_in(n, [128, XT]) for n in ("csq", "snq", "csk", "snk")}
    edgeneg = p.dram_in("edgeneg", [1, 1])
    edgezero = p.dram_in("edgezero", [128, 1])
    trimask = p.dram_in("trimask", [128, 128], BF16)
    expand2 = p.dram_in("expand2", [2, 128])
    ones2h = p.dram_in("ones2h", [128, 2])
    rotk = p.dram_in("rotk", [128, 128])
    idb = p.dram_in("idb", [128, 512], BF16)
    zid = p.dram_in("zid", [128, 640], BF16)
    tri2m = p.dram_in("tri2m", [128, 128], BF16)
    xo = p.dram_out("xoT", [D, TOK])

    q2D = nc.dram_tensor("q2D", [FH, 128, XT - Q0], BF16, kind="Internal").ap()
    qlD = nc.dram_tensor("qlD", [FH, 64, XT - Q0], BF16, kind="Internal").ap()
    kD = nc.dram_tensor("kD", [FH, 128, XT - KV0], BF16, kind="Internal").ap()
    vD = nc.dram_tensor("vD", [FH, 2, XT - KV0, FD], BF16, kind="Internal").ap()
    attD = nc.dram_tensor("attD", [128, FI // 128, 2, XT - Q0], BF16,
                          kind="Internal").ap()
    xmD = nc.dram_tensor("xmD", [D, TOK + 1], F32, kind="Internal").ap()
    gluD = nc.dram_tensor("gluD", [NHC, 2, 128, TOK], BF16,
                          kind="Internal").ap()
    lnhD = nc.dram_tensor("lnhD", [2, D, TOK], BF16, kind="Internal").ap()

    xsp = [p.sb(f"xsp{i}", [128, 513]) for i in range(2)]
    hhi = [p.sb(f"hhi{i}", [128, 8, 512], BF16) for i in range(2)]
    hlo = [p.sb(f"hlo{i}", [128, 8, 512], BF16) for i in range(2)]
    wfp = p.sb("wfp", [128, 8, 512])
    whi = p.sb("whi", [128, 8, 512], BF16)
    wlo = p.sb("wlo", [128, 8, 512], BF16)
    qs_s = p.sb("qs_s", [128, 1])
    ks_s = p.sb("ks_s", [128, 1])
    lg_s = p.sb("lg_s", [128, 8])
    lb_s = p.sb("lb_s", [128, 8])
    b1a_s = p.sb("b1a_s", [128, NHC])
    b1g_s = p.sb("b1g_s", [128, NHC])
    b2_s = p.sb("b2_s", [128, 8])
    tri_s = p.sb("tri_s", [128, 128], BF16)
    ex2_s = p.sb("ex2_s", [2, 128])
    on2_s = p.sb("on2_s", [128, 2])
    rot_s = p.sb("rot_s", [128, 128])
    idb_s = p.sb("idb_s", [128, 512], BF16)
    zid_s = p.sb("zid_s", [128, 640], BF16)
    tri2_s = p.sb("tri2_s", [128, 128], BF16)
    en_s = p.sb("en_s", [1, 1])
    ez_s = p.sb("ez_s", [128, 1])
    tab_s = {n: [p.sb(f"{n}{i}", [128, 512]) for i in range(2)]
             for n in ("csq", "snq", "csk", "snk")}
    ones_c = p.sb("ones_c", [128, 1])
    ones_r = p.sb("ones_r", [1, 128])
    ev = [p.sb(f"ev{i}", [128, 512]) for i in range(2)]
    ev2 = [p.sb(f"ev2{i}", [128, 512]) for i in range(2)]
    evh = [p.sb(f"evh{i}", [128, 512], BF16) for i in range(2)]
    evl = [p.sb(f"evl{i}", [128, 512], BF16) for i in range(2)]
    sqv = p.sb("sqv", [128, 512])
    n2 = p.sb("n2", [2, 512])
    kpx = [p.sb(f"kpx{i}", [128, XT - KV0], BF16) for i in range(2)]
    vstg = p.sb("vstg", [128, (XT - KV0) // 128, 2, 65], BF16)
    q2t = p.sb("q2t", [128, 512], BF16)
    qlt = p.sb("qlt", [64, 512], BF16)
    pex = [p.sb(f"pex{i}", [128, 512]) for i in range(2)]
    phh = [p.sb(f"phh{i}", [128, 512], BF16) for i in range(2)]
    pll = [p.sb(f"pll{i}", [128, 512], BF16) for i in range(2)]
    rden = p.sb("rden", [1, 512])
    rdex = p.sb("rdex", [64, 512])
    ahat = p.sb("ahat", [64, 512])
    ah_h = p.sb("ah_h", [64, 512], BF16)
    ah_l = p.sb("ah_l", [64, 512], BF16)
    ahld = [p.sb(f"ahld{i}", [128, 2, 512], BF16) for i in range(2)]
    lnhh = [p.sb(f"lnhh{i}", [128, 8, 512], BF16) for i in range(2)]
    lnhl = [p.sb(f"lnhl{i}", [128, 8, 512], BF16) for i in range(2)]
    stats = p.sb("stats", [1, 2, TOK])
    glh = p.sb("glh", [128, 512], BF16)
    gll = p.sb("gll", [128, 512], BF16)
    gel = p.sb("gel", [128, 512])
    asb = p.sb("asb", [128, 512])

    ps1024 = [p.ps(f"ps1024_{i}", [128, 1024]) for i in range(2)]
    ps512 = [p.ps(f"ps512_{i}", [128, 512]) for i in range(4)]

    for src_, dst in ((qs128, qs_s), (ks128, ks_s), (lg128, lg_s),
                      (lb128, lb_s), (b1a128, b1a_s), (b1g128, b1g_s),
                      (b2c, b2_s), (trimask, tri_s), (expand2, ex2_s),
                      (ones2h, on2_s), (edgeneg, en_s), (edgezero, ez_s),
                      (rotk, rot_s), (idb, idb_s), (zid, zid_s),
                      (tri2m, tri2_s)):
        s.dma(dst[:], src_[tuple([slice(None)] * len(src_.shape))], tag="ldc")
    s.dve("memset", ones_c[:], 1.0, tag="onesc")
    s.dve("memset", ones_r[:], 1.0, tag="onesr")
    s.dve("memset", vstg[:, :, 0, 64:65], 1.0, tag="vst1")
    s.dve("memset", vstg[:, :, 1, 64:65], 0.0, tag="vst0")

    NTT = (XT - KV0) // 512

    def load_x_tile(tt, tb):
        t0 = KV0 + tt * 512
        for c in range(8):
            s.dma(xsp[tb][:, 0:513],
                  xE.rearrange("(c p) t -> p c t", p=128)[:, c,
                                                          t0 - 1:t0 + 512],
                  tag="ldxc")
            off = 1 if c < 4 else 0
            emit_split(s, hhi[tb][:, c, :], hlo[tb][:, c, :],
                       xsp[tb][:, off:off + 512])

    p1iters = []
    for kind_ in range(3):
        for tt_ in range(NTT):
            if kind_ == 0 and KV0 + tt_ * 512 < Q0:
                continue
            p1iters.append((kind_, tt_))
    cur_kind = [-1]
    for it_i, (kind, tt) in enumerate(p1iters):
        t0 = KV0 + tt * 512
        tb = tt % 2
        if kind != cur_kind[0]:
            cols = slice(kind * 512, (kind + 1) * 512)
            s.dma(wfp[:], wqkv[:, cols].rearrange("(c p) o -> p c o", p=128),
                  q="dgps", tag=f"ldwqkv{kind}")
            emit_split(s, whi[:], wlo[:], wfp[:])
            cur_kind[0] = kind
        if it_i == 0:
            load_x_tile(tt, tb)
        if it_i + 1 < len(p1iters):
            ntt_ = p1iters[it_i + 1][1]
            load_x_tile(ntt_, ntt_ % 2)
        if True:
            if kind < 2:
                for n_ in (("csq", "snq") if kind == 0 else ("csk", "snk")):
                    s.dma(tab_s[n_][tb][:], tabs[n_][:, t0:t0 + 512],
                          q="dgps", tag="ldtab")
                def qk_chain(oc, kind, tb, t0):
                    pp = ps512[oc % 2][:, 0:512]
                    eb = ev[oc % 2]
                    sc_ap = (qs_s if kind == 0 else ks_s)[:]
                    s.act(eb[:], pp, AF.Square, tag="sq")
                    pn = ps1024[0][0:2, 0:512]
                    s.mm(pn, on2_s[:], eb[:], True, True, tag="n2mm")
                    s.act(n2[:], pn, AF.Sqrt, tag="nsqrt")
                    s.dve("tensor_scalar_max", n2[:], n2[:], 1e-12,
                          tag="nclip")
                    s.dve("reciprocal", n2[:], n2[:], tag="nrec")
                    pn2 = ps1024[0][0:128, 0:512]
                    s.mm(pn2, ex2_s[:], n2[:], True, True, tag="nexp")
                    s.dve("tensor_scalar_mul", eb[:], pp, sc_ap,
                          tag="qscale")
                    s.dve("tensor_mul", eb[:], eb[:], pn2, tag="qnorm")
                    prot = ps512[2 + oc % 2][:, 0:512]
                    s.mm(prot, rot_s[:], eb[:], True, True, tag="rotmm")
                    cs_t = tab_s["csq" if kind == 0 else "csk"][tb]
                    sn_t = tab_s["snq" if kind == 0 else "snk"][tb]
                    e2 = ev2[oc % 2]
                    s.dve("tensor_mul", e2[:], eb[:], cs_t[:], tag="qcs")
                    s.dve("tensor_mul", eb[:], prot, sn_t[:], tag="qsn")
                    s.dve("tensor_add", e2[:], e2[:], eb[:], tag="qfin")
                    emit_split(s, evh[oc % 2][:], evl[oc % 2][:], e2[:])
                    for hh_ in range(2):
                        h_ = 2 * oc + hh_
                        rows = slice(hh_ * 64, (hh_ + 1) * 64)
                        if kind == 0:
                            dsl = slice(t0 - Q0, t0 - Q0 + 512)
                            s.dma(q2D[h_, 0:64, dsl], evh[oc % 2][rows, :],
                                  q="dgps", tag="stq2a")
                            s.dma(q2D[h_, 64:128, dsl], evh[oc % 2][rows, :],
                                  q="dgps", tag="stq2b")
                            s.dma(qlD[h_, :, dsl], evl[oc % 2][rows, :],
                                  q="dgps", tag="stql")
                        else:
                            dsl = slice(t0 - KV0, t0 - KV0 + 512)
                            s.dma(kD[h_, 0:64, dsl], evh[oc % 2][rows, :],
                                  q="dgps", tag="stk_h")
                            s.dma(kD[h_, 64:128, dsl], evl[oc % 2][rows, :],
                                  q="dgps", tag="stk_l")

                pchain = [None]
                for oc in range(4):
                    pp = ps512[oc % 2][:, 0:512]
                    ocs = slice(oc * 128, (oc + 1) * 128)
                    emit_proj3(s, pp,
                               lambda c: whi[:, c, ocs],
                               lambda c: wlo[:, c, ocs],
                               lambda c: hhi[tb][:, c, :],
                               lambda c: hlo[tb][:, c, :], 8, tag="qkproj")
                    if pchain[0] is not None:
                        qk_chain(*pchain[0])
                    pchain[0] = (oc, kind, tb, t0)
                qk_chain(*pchain[0])
            else:
                for sub in range(4):
                    pp = ps512[sub % 2][:, 0:512]
                    tsl2 = slice(sub * 128, (sub + 1) * 128)
                    first = True
                    for c in range(8):
                        s.mm(pp, hhi[tb][:, c, tsl2], whi[:, c, :],
                             first, False, tag="vproj")
                        first = False
                        s.mm(pp, hhi[tb][:, c, tsl2], wlo[:, c, :],
                             False, False, tag="vproj")
                        s.mm(pp, hlo[tb][:, c, tsl2], whi[:, c, :],
                             False, c == 7, tag="vproj")
                    s.scopy(ev[sub % 2][:], pp, tag="vev")
                    emit_split(s, evh[sub % 2][:], evl[sub % 2][:],
                               ev[sub % 2][:])
                    vt0 = t0 - KV0 + sub * 128
                    for h_ in range(FH):
                        hc_ = slice(h_ * FD, (h_ + 1) * FD)
                        s.dma(vD[h_, 0, vt0:vt0 + 128, :],
                              evh[sub % 2][:, hc_], q="dgps", tag="stv_h")
                        s.dma(vD[h_, 1, vt0:vt0 + 128, :],
                              evl[sub % 2][:, hc_], q="dgps", tag="stv_l")

    if dbg:
        for nm, t_, sh in (("dbg_q2", q2D, [FH, 128, XT - Q0]),
                           ("dbg_k", kD, [FH, 128, XT - KV0]),
                           ("dbg_v", vD, [FH, 2, XT - KV0, FD])):
            o_ = p.dram_out(nm, sh, BF16)
            s.dma(o_[tuple(slice(None) for _ in sh)],
                  t_[tuple(slice(None) for _ in sh)], q="dgps", tag="dbg")
    NVC = (XT - KV0) // 128
    for h in range(FH):
        hb = h % 2
        s.dma(kpx[hb][:], kD[h, :, :], tag="ldkp")
        s.dma(vstg[:, :, 0, 0:64],
              vD[h, 0, :, :].rearrange("(kc pp) d -> pp kc d", pp=128),
              tag="ldvh")
        s.dma(vstg[:, :, 1, 0:64],
              vD[h, 1, :, :].rearrange("(kc pp) d -> pp kc d", pp=128),
              tag="ldvl")
        for w in range(2, 7):
            qb0 = w * 512
            qdsl = slice(qb0 - Q0, qb0 - Q0 + 512)
            s.dma(q2t[:], q2D[h, :, qdsl], q="dgps", tag="ldq2")
            s.dma(qlt[:], qlD[h, :, qdsl], q="dgps", tag="ldql2")
            pvp = ps512[0][0:65, 0:512]
            kb0 = (w - 1) * 512 - KV0
            order = [3, 2, 1, 0, 4, 5, 6, 7]
            prevs = []
            for oi, kci in enumerate(order):
                kb = oi % 2
                stp = ps1024[kb][:, 0:512]
                if kci < 4:
                    vld = 0
                    vend = kci * 128 + 128
                    z0 = 512 - kci * 128
                    s.mm(stp[:, 0:vend], tri2_s[:], zid_s[:, z0:z0 + vend],
                         True, False, tag="tri2mm")
                    if w == 3:
                        s.mm(stp[:, 0:vend], ones_r[:],
                             AP(en_s[:].tensor, 0, [[1, 1], [0, vend]]),
                             False, False, extra_reads=[en_s[:]],
                             tag="edgemm")
                else:
                    vld = (kci - 4) * 128
                    vend = 512
                    s.mm(stp[:, vld:512], tri_s[:], idb_s[:, 0:512 - vld],
                         True, False, tag="trimm")
                vsl = slice(vld, vend)
                ksl = slice(kb0 + kci * 128, kb0 + kci * 128 + 128)
                s.mm(stp[:, vsl], kpx[hb][:, ksl], q2t[:, vsl],
                     False, False, tag="fqkA")
                s.mm(stp[:, vsl], kpx[hb][0:64, ksl], qlt[:, vsl],
                     False, True, tag="fqkB")
                s.act(pex[kb][:, vsl], stp[:, vsl], AF.Exp, scale=QK_SCALE,
                      tag="fexp")
                s.act(phh[kb][:, vsl], stp[:, vsl], AF.Exp, scale=QK_SCALE,
                      tag="fexph")
                s.dve("tensor_sub", pll[kb][:, vsl], pex[kb][:, vsl],
                      phh[kb][:, vsl], tag="fplo")
                kcv = kb0 // 128 + kci
                prevs.append((oi, kb, vsl, kcv))
                emits = []
                if oi > 0:
                    emits.append(prevs.pop(0))
                if oi == 7:
                    emits.append(prevs.pop(0))
                for poi, pb_, pvsl, pkcv in emits:
                    s.mm(pvp[:, pvsl], vstg[:, pkcv, 0, :],
                         phh[pb_][:, pvsl], poi == 0, False, tag="fpvA")
                    s.mm(pvp[:, pvsl], vstg[:, pkcv, 0, :],
                         pll[pb_][:, pvsl], False, False, tag="fpvB")
                    s.mm(pvp[:, pvsl], vstg[:, pkcv, 1, :],
                         phh[pb_][:, pvsl], False, poi == 7, tag="fpvC")
            s.dve("reciprocal", rden[:], pvp[64:65, :], tag="frecip")
            pr = ps512[2][0:64, 0:512]
            s.mm(pr, ones_r[0:1, 0:64], rden[:], True, True, tag="frdex")
            s.scopy(rdex[:], pr, tag="frdev")
            s.dve("tensor_mul", ahat[:], pvp[0:64, :], rdex[:], tag="fahat")
            emit_split(s, ah_h[:], ah_l[:], ahat[:])
            hp = h // 2
            r0 = (h % 2) * 64
            s.dma(attD[r0:r0 + 64, hp, 0, qdsl], ah_h[:], q="dgps", tag="fsah")
            s.dma(attD[r0:r0 + 64, hp, 1, qdsl], ah_l[:], q="dgps", tag="fsal")

    if dbg:
        o_ = p.dram_out("dbg_att", [128, FI // 128, 2, XT - Q0], BF16)
        s.dma(o_[:, :, :, :], attD[:, :, :, :], q="dgps", tag="dbga")
    whf = whi[:].rearrange("p a b -> p (a b)")
    wlf = wlo[:].rearrange("p a b -> p (a b)")

    def wo_hi(ic, oc):
        return whf[:, ic * 1024 + oc * 128:ic * 1024 + (oc + 1) * 128]

    def wo_lo(ic, oc):
        return wlf[:, ic * 1024 + oc * 128:ic * 1024 + (oc + 1) * 128]

    wfp_f = wfp[:].rearrange("p a b -> p (a b)")
    for ic in range(4):
        s.dma(wfp_f[:, 0:1024], wo[ic * 128:(ic + 1) * 128, :], q="dgps",
              tag="ldwo")
        emit_split(s, whf[:, ic * 1024:(ic + 1) * 1024],
                   wlf[:, ic * 1024:(ic + 1) * 1024], wfp_f[:, 0:1024])

    def emit_wo(c0, length, tb):
        qsl = slice(c0 - Q0, c0 - Q0 + length)
        for ic in range(4):
            ab_ = ahld[ic % 2]
            s.dma(ab_[:, :, 0:length], attD[:, ic, :, qsl], tag="ldatt",
                  slow=(length == 1))
            for oc in range(8):
                pb = (ps512[oc][:, 0:length] if oc < 4 else
                      ps1024[(oc - 4) // 2][:, ((oc - 4) % 2) * 512:
                                            ((oc - 4) % 2) * 512 + length])
                s.mm(pb, wo_hi(ic, oc), ab_[:, 0, 0:length], ic == 0, False,
                     tag="woA")
                s.mm(pb, wo_hi(ic, oc), ab_[:, 1, 0:length], False, False,
                     tag="woB")
                s.mm(pb, wo_lo(ic, oc), ab_[:, 0, 0:length], False, ic == 3,
                     tag="woC")
        for oc in range(8):
            pb = (ps512[oc][:, 0:length] if oc < 4 else
                  ps1024[(oc - 4) // 2][:, ((oc - 4) % 2) * 512:
                                        ((oc - 4) % 2) * 512 + length])
            s.dma(ev[tb][:, 0:length],
                  xE.rearrange("(c p) t -> p c t", p=128)[:, oc,
                                                          c0:c0 + length],
                  tag="ldxr", slow=(length == 1))
            s.dve("tensor_add", ev2[tb][:, 0:length], pb, ev[tb][:, 0:length],
                  tag="xmadd")
            if c0 == 1535:
                s.dve("tensor_scalar_mul", ev2[tb][:, 0:length],
                      ev2[tb][:, 0:length], ez_s[:], tag="edgez")
            s.dma(xmD.rearrange("(c p) t -> p c t", p=128)[
                :, oc, c0 - 1535:c0 - 1535 + length], ev2[tb][:, 0:length],
                q="dgps", tag="stxm", slow=(length == 1))

    emit_wo(1535, 1, 0)
    for tt in range(4):
        emit_wo(1536 + tt * 512, 512, tt % 2)

    if dbg:
        o_ = p.dram_out("dbg_xm", [D, TOK + 1])
        s.dma(o_[:, :], xmD[:, :], q="dgps", tag="dbgxm")
    for tt in range(4):
        tb = tt % 2
        tsl = slice(tt * 512, (tt + 1) * 512)
        psum_s = ps512[2][0:1, 0:512]
        psum_q = ps512[2][32:33, 0:512]
        for oc in range(8):
            off = 1 if oc < 4 else 0
            s.dma(ev[tb][:], xmD.rearrange("(c p) t -> p c t", p=128)[
                :, oc, tt * 512 + off:tt * 512 + off + 512], tag="ldh2")
            s.act(sqv[:], ev[tb][:], AF.Square, tag="h2sq")
            s.mm(psum_s, ones_c[:], ev[tb][:], oc == 0, oc == 7, tag="h2sum")
            s.mm(psum_q, ones_c[:], sqv[:], oc == 0, oc == 7, tag="h2sq2")
        s.scopy(stats[:, 0, tsl], psum_s, tag="h2ev_s")
        s.scopy(stats[:, 1, tsl], psum_q, tag="h2ev_q")
    s.act(stats[:, 0, :], stats[:, 0, :], AF.Copy, scale=1.0 / D, tag="fmu")
    s.act(stats[:, 1, :], stats[:, 1, :], AF.Copy, scale=1.0 / D, tag="fexx")
    for hf in range(4):
        hsl = slice(hf * 512, (hf + 1) * 512)
        s.dve("tensor_mul", rden[:], stats[:, 0, hsl], stats[:, 0, hsl],
              tag="fmusq")
        s.dve("tensor_sub", stats[:, 1, hsl], stats[:, 1, hsl], rden[:],
              tag="fvar")
        s.dve("tensor_scalar_add", stats[:, 1, hsl], stats[:, 1, hsl],
              LN_EPS, tag="fvare")
        s.dve("reciprocal", rden[:], stats[:, 1, hsl], tag="fvrec")
        s.act(stats[:, 1, hsl], rden[:], AF.Sqrt, tag="frsq")
        s.dve("tensor_mul", stats[:, 0, hsl], stats[:, 0, hsl],
              stats[:, 1, hsl], tag="fmurs")
        s.dve("tensor_scalar_mul", stats[:, 0, hsl], stats[:, 0, hsl], -1.0,
              tag="fnegm")
    for tt in range(4):
        tb = tt % 2
        tsl = slice(tt * 512, (tt + 1) * 512)
        pa = ps512[0][:, 0:512]
        pbv = ps512[1][:, 0:512]
        s.mm(pa, ones_r[:], stats[:, 1, tsl], True, True, tag="faexp")
        s.mm(pbv, ones_r[:], stats[:, 0, tsl], True, True, tag="fbexp")
        for oc in range(8):
            off = 1 if oc < 4 else 0
            s.dma(ev[tb][:], xmD.rearrange("(c p) t -> p c t", p=128)[
                :, oc, tt * 512 + off:tt * 512 + off + 512], tag="ldh2b")
            s.dve("tensor_mul", ev2[tb][:], ev[tb][:], pa, tag="flnm")
            s.dve("tensor_add", ev2[tb][:], ev2[tb][:], pbv, tag="flna")
            s.dve("tensor_scalar", ev2[tb][:], ev2[tb][:], lg_s[:, oc:oc + 1],
                  scalar2=lb_s[:, oc:oc + 1], op0=ALU.mult, op1=ALU.add,
                  tag="flngb")
            emit_split(s, evh[tb][:], evl[tb][:], ev2[tb][:])
            s.dma(lnhD[0, oc * 128:(oc + 1) * 128, tsl], evh[tb][:],
                  q="dgps", tag="stlnh")
            s.dma(lnhD[1, oc * 128:(oc + 1) * 128, tsl], evl[tb][:],
                  q="dgps", tag="stlnl")

    if dbg:
        o_ = p.dram_out("dbg_lnh", [2, D, TOK], BF16)
        s.dma(o_[:, :, :], lnhD[:, :, :], q="dgps", tag="dbgln")
    def ld_lnh(tt_):
        tb_ = tt_ % 2
        tsl_ = slice(tt_ * 512, (tt_ + 1) * 512)
        s.dma(lnhh[tb_][:],
              lnhD[0, :, tsl_].rearrange("(c p) t -> p c t", p=128),
              tag="ldlh")
        s.dma(lnhl[tb_][:],
              lnhD[1, :, tsl_].rearrange("(c p) t -> p c t", p=128),
              tag="ldll")

    it4b = [(hc_, tt_) for hc_ in range(NHC) for tt_ in range(4)]
    for it_i, (hc, tt) in enumerate(it4b):
        if tt == 0:
            wfa = wfp_f[:, 0:1024].rearrange("p (c o) -> p c o", o=128)
            wfg = wfp_f[:, 1024:2048].rearrange("p (c o) -> p c o", o=128)
            s.dma(wfa, w1p[:, 0, hc * 128:(hc + 1) * 128].rearrange(
                "(c p) o -> p c o", p=128), q="dgps", tag="ldw1a")
            s.dma(wfg, w1p[:, 1, hc * 128:(hc + 1) * 128].rearrange(
                "(c p) o -> p c o", p=128), q="dgps", tag="ldw1g")
            wha = whf[:, 0:1024].rearrange("p (c o) -> p c o", o=128)
            wla = wlf[:, 0:1024].rearrange("p (c o) -> p c o", o=128)
            whg = whf[:, 1024:2048].rearrange("p (c o) -> p c o", o=128)
            wlg = wlf[:, 1024:2048].rearrange("p (c o) -> p c o", o=128)
            emit_split(s, wha, wla, wfa)
            emit_split(s, whg, wlg, wfg)
        tb = tt % 2
        tsl = slice(tt * 512, (tt + 1) * 512)
        if it_i == 0:
            ld_lnh(0)
        if it_i + 1 < len(it4b):
            ld_lnh(it4b[it_i + 1][1])
        if True:
            pa_ = ps512[tb][:, 0:512]
            pg_ = ps512[2 + tb][:, 0:512]
            emit_proj3(s, pa_, lambda c: wha[:, c, :], lambda c: wla[:, c, :],
                       lambda c: lnhh[tb][:, c, :],
                       lambda c: lnhl[tb][:, c, :], 8, tag="w1a")
            emit_proj3(s, pg_, lambda c: whg[:, c, :], lambda c: wlg[:, c, :],
                       lambda c: lnhh[tb][:, c, :],
                       lambda c: lnhl[tb][:, c, :], 8, tag="w1g")
            s.act(gel[:], pg_, AF.Gelu, bias=b1g_s[:, hc:hc + 1], tag="gelu")
            s.dve("tensor_scalar_add", asb[:], pa_, b1a_s[:, hc:hc + 1],
                  tag="aev")
            s.dve("tensor_mul", asb[:], asb[:], gel[:], tag="glu")
            emit_split(s, glh[:], gll[:], asb[:])
            s.dma(gluD[hc, 0, :, tsl], glh[:], q="dgps", tag="stgh")
            s.dma(gluD[hc, 1, :, tsl], gll[:], q="dgps", tag="stgl")

    gl2 = [p.sb(f"gl2_{i}", [128, 2, 512], BF16) for i in range(2)]
    for tt in range(4):
        tb = tt % 2
        tsl = slice(tt * 512, (tt + 1) * 512)
        for hc in range(NHC):
            w2h = whf[:, 2048:3072]
            w2l = wlf[:, 2048:3072]
            s.dma(wfp_f[:, 0:1024], w2p[hc * 128:(hc + 1) * 128, :],
                  q="dgps", tag="ldw2")
            emit_split(s, w2h, w2l, wfp_f[:, 0:1024])
            gb = gl2[hc % 2]
            s.dma(gb[:], gluD[hc].rearrange("hl p t -> p hl t")[:, :, tsl],
                  tag="ldglu")
            for oc in range(8):
                pb = (ps512[oc][:, 0:512] if oc < 4 else
                      ps1024[(oc - 4) // 2][:, ((oc - 4) % 2) * 512:
                                            ((oc - 4) % 2) * 512 + 512])
                ocs = slice(oc * 128, (oc + 1) * 128)
                s.mm(pb, w2h[:, ocs], gb[:, 0, :], hc == 0, False, tag="w2A")
                s.mm(pb, w2h[:, ocs], gb[:, 1, :], False, False, tag="w2B")
                s.mm(pb, w2l[:, ocs], gb[:, 0, :], False, hc == NHC - 1,
                     tag="w2C")
        for oc in range(8):
            pb = (ps512[oc][:, 0:512] if oc < 4 else
                  ps1024[(oc - 4) // 2][:, ((oc - 4) % 2) * 512:
                                        ((oc - 4) % 2) * 512 + 512])
            s.dve("tensor_scalar_add", ev[oc % 2][:], pb,
                  b2_s[:, oc:oc + 1], tag="b2ev")
            s.dma(ev2[oc % 2][:], xmD.rearrange("(c p) t -> p c t", p=128)[
                :, oc, tt * 512 + 1:tt * 512 + 513], tag="ldxm2")
            s.dve("tensor_add", ev[oc % 2][:], ev[oc % 2][:], ev2[oc % 2][:],
                  tag="ffnres")
            if block_ln:
                s.dma(ev2[oc % 2][:],
                      fm_view(x1)[:, oc, tsl], tag="ldx1r")
                s.dve("tensor_add", ev[oc % 2][:], ev[oc % 2][:],
                      ev2[oc % 2][:], tag="blkres")
                s.dma(xmD.rearrange("(c p) t -> p c t", p=128)[
                    :, oc, tt * 512 + 1:tt * 512 + 513] if False else
                    fm_view(xo)[:, oc, tsl], ev[oc % 2][:], q="dgps",
                    tag="stpre")
            else:
                s.dma(fm_view(xo)[:, oc, tsl], ev[oc % 2][:], q="dgps",
                      tag="stxo")

    if block_ln:
        for tt in range(4):
            tb = tt % 2
            tsl = slice(tt * 512, (tt + 1) * 512)
            psum_s = ps512[2][0:1, 0:512]
            psum_q = ps512[2][32:33, 0:512]
            for oc in range(8):
                s.dma(ev[tb][:], fm_view(xo)[:, oc, tsl], tag="ldpre")
                s.act(sqv[:], ev[tb][:], AF.Square, tag="bsq")
                s.mm(psum_s, ones_c[:], ev[tb][:], oc == 0, oc == 7,
                     tag="bsum")
                s.mm(psum_q, ones_c[:], sqv[:], oc == 0, oc == 7, tag="bsq2")
            s.scopy(stats[:, 0, tsl], psum_s, tag="bev_s")
            s.scopy(stats[:, 1, tsl], psum_q, tag="bev_q")
        s.act(stats[:, 0, :], stats[:, 0, :], AF.Copy, scale=1.0 / D,
              tag="bmu")
        s.act(stats[:, 1, :], stats[:, 1, :], AF.Copy, scale=1.0 / D,
              tag="bexx")
        for hf in range(4):
            hsl = slice(hf * 512, (hf + 1) * 512)
            s.dve("tensor_mul", rden[:], stats[:, 0, hsl], stats[:, 0, hsl],
                  tag="bmusq")
            s.dve("tensor_sub", stats[:, 1, hsl], stats[:, 1, hsl], rden[:],
                  tag="bvar")
            s.dve("tensor_scalar_add", stats[:, 1, hsl], stats[:, 1, hsl],
                  LN_EPS, tag="bvare")
            s.dve("reciprocal", rden[:], stats[:, 1, hsl], tag="bvrec")
            s.act(stats[:, 1, hsl], rden[:], AF.Sqrt, tag="brsq")
            s.dve("tensor_mul", stats[:, 0, hsl], stats[:, 0, hsl],
                  stats[:, 1, hsl], tag="bmurs")
            s.dve("tensor_scalar_mul", stats[:, 0, hsl], stats[:, 0, hsl],
                  -1.0, tag="bnegm")
        for tt in range(4):
            tb = tt % 2
            tsl = slice(tt * 512, (tt + 1) * 512)
            pa = ps512[0][:, 0:512]
            pbv = ps512[1][:, 0:512]
            s.mm(pa, ones_r[:], stats[:, 1, tsl], True, True, tag="baexp")
            s.mm(pbv, ones_r[:], stats[:, 0, tsl], True, True, tag="bbexp")
            for oc in range(8):
                s.dma(ev[tb][:], fm_view(xo)[:, oc, tsl], tag="ldpre2")
                s.dve("tensor_mul", ev2[tb][:], ev[tb][:], pa, tag="blnm")
                s.dve("tensor_add", ev2[tb][:], ev2[tb][:], pbv, tag="blna")
                s.dma(fm_view(xo)[:, oc, tsl], ev2[tb][:], q="dgps",
                      tag="stxo2")

    nc_ = s.emit()
    p.es.close()
    return nc_


def _np_split_pair(w):
    import ml_dtypes
    hi = _bf16(w)
    lo = _bf16(np.asarray(w, np.float32) - np.asarray(hi, np.float32))
    return hi, lo


def host_rotary_tables():
    half = np.arange(0, FD, 2, dtype=np.float32)
    inv = (1.0 / 10000.0 ** (half / FD)).astype(np.float32)
    t = np.arange(-MARG, N, dtype=np.float32)
    fr = t[:, None] * inv[None, :]
    fr = np.concatenate([fr, fr], -1).astype(np.float32)
    base = ((half + 0.4 * FD) / (1.4 * FD)).astype(np.float32)
    power = ((t - N // 2) / float(WIN // 2)).astype(np.float32)
    sc = (base[None, :] ** power[:, None]).astype(np.float32)
    sc = np.concatenate([sc, sc], -1).astype(np.float32)
    cos, sin = np.cos(fr).astype(np.float32), np.sin(fr).astype(np.float32)
    csq = (cos * sc).T
    snq = (sin * sc).T
    csk = (cos / sc).T
    snk = (sin / sc).T
    dup = lambda a: np.ascontiguousarray(np.concatenate([a, a], 0))
    return dup(csq), dup(snq), dup(csk), dup(snk)


def host_rotk():
    R = np.zeros((128, 128), np.float32)
    for blk in range(2):
        b0 = blk * 64
        for i in range(32):
            R[b0 + 32 + i, b0 + i] = -1.0
            R[b0 + i, b0 + 32 + i] = 1.0
    return R


def host_tri2T():
    t = np.full((128, 128), NEG, np.float32)
    t[np.triu_indices(128)] = 0.0
    return _bf16(t)


def host_trimaskT():
    t = np.full((128, 128), NEG, np.float32)
    t[np.tril_indices(128)] = 0.0
    return _bf16(t)


def make_C_inputs(x_layer_full, x1_core, wqkv, qs, ksc, wo, lg, lb, w1, b1,
                  w2, b2, core_q, tabs):
    s0 = core_q * TOK
    xe = np.zeros((XT, D), np.float32)
    lo = s0 - MARG
    src0 = max(0, lo)
    xe[src0 - lo:XT] = x_layer_full[src0:s0 + TOK]
    w1pad = np.zeros((D, 2, HID_P), np.float32)
    w1pad[:, 0, :HID] = w1[:, :HID]
    w1pad[:, 1, :HID] = w1[:, HID:]
    w2pad = np.zeros((HID_P, D), np.float32)
    w2pad[:HID] = w2
    b1a = np.zeros((NHC * 128,), np.float32)
    b1g = np.zeros((NHC * 128,), np.float32)
    b1a[:HID] = b1[:HID]
    b1g[:HID] = b1[HID:]
    csq, snq, csk, snk = tabs
    c0 = s0 - MARG + MARG
    tsl = slice(s0, s0 + XT)
    first = core_q == 0
    ins = {
        "xET": np.ascontiguousarray(xe.T),
        "x1T": (np.ascontiguousarray(x1_core.T) if x1_core is not None
                else np.zeros((D, TOK), np.float32)),
        "wqkv": np.ascontiguousarray(wqkv),
        "wo": np.ascontiguousarray(wo),
        "w1p": w1pad, "w2p": w2pad,
        "qs128": np.ascontiguousarray(np.tile(qs, 2)[:, None]),
        "ks128": np.ascontiguousarray(np.tile(ksc, 2)[:, None]),
        "lg128": np.ascontiguousarray(lg.reshape(8, 128).T),
        "lb128": np.ascontiguousarray(lb.reshape(8, 128).T),
        "b1a128": np.ascontiguousarray(b1a.reshape(NHC, 128).T),
        "b1g128": np.ascontiguousarray(b1g.reshape(NHC, 128).T),
        "b2c": np.ascontiguousarray(b2.reshape(8, 128).T),
        "csq": np.ascontiguousarray(csq[:, tsl]),
        "snq": np.ascontiguousarray(snq[:, tsl]),
        "csk": np.ascontiguousarray(csk[:, tsl]),
        "snk": np.ascontiguousarray(snk[:, tsl]),
        "edgeneg": np.full((1, 1), NEG if first else 0.0, np.float32),
        "edgezero": np.full((128, 1), 0.0 if first else 1.0, np.float32),
        "trimask": host_trimaskT(),
        "expand2": np.ascontiguousarray(
            np.kron(np.eye(2, dtype=np.float32), np.ones((1, 64), np.float32))),
        "ones2h": np.ascontiguousarray(
            np.kron(np.eye(2, dtype=np.float32), np.ones((64, 1), np.float32))),
        "rotk": host_rotk(),
        "idb": _bf16(np.concatenate(
            [np.eye(128, dtype=np.float32),
             np.zeros((128, 384), np.float32)], 1)),
        "zid": _bf16(np.concatenate(
            [np.zeros((128, 512), np.float32),
             np.eye(128, dtype=np.float32)], 1)),
        "tri2m": host_tri2T(),
    }
    return ins


def _np_split_pair(w):
    hi = _bf16(w)
    lo = _bf16(np.asarray(w, np.float32) - np.asarray(hi, np.float32))
    return hi, lo


def build_B(dbg=False):
    p = Prog("B")
    s = Sched(p)
    nc = p.nc
    x0 = p.dram_in("x0T", [D, TOK])
    qp2 = p.dram_in("qp2", [128, H, TOK], BF16)
    qlo = p.dram_in("qlo", [64, H, TOK], BF16)
    kpB = p.dram_in("kpB", [128, H, N], BF16)
    vfB = p.dram_in("vfB", [H, N, DH])
    xpg = p.dram_in("xpg", [B, D, TR])
    wq = p.dram_in("wq", [D, D])
    wk = p.dram_in("wk", [D, D])
    wo = p.dram_in("wo", [D, D])
    kron4 = p.dram_in("kron4", [4, 128], BF16)
    esel = p.dram_in("esel", [TR, TOK // BM], BF16)
    x1o = p.dram_out("x1T", [D, TOK])
    x1pre_d = nc.dram_tensor("x1pre", [D, TOK], F32, kind="Internal").ap()
    ah_d = nc.dram_tensor("ah_d", [128, 8, 2, TOK], BF16, kind="Internal").ap()

    MYROWS = TOK // BM
    QH = 1024
    NKC = N // 128

    wfp = p.sb("wfp", [128, 8, 512])
    whi = p.sb("whi", [128, 8, 1024], BF16)
    wlo = p.sb("wlo", [128, 8, 1024], BF16)
    xps = [p.sb(f"xps{i}", [128, B * TR]) for i in range(2)]
    xphi = p.sb("xphi", [128, 8, B * TR], BF16)
    xplo = p.sb("xplo", [128, 8, B * TR], BF16)
    pev = p.sb("pev", [128, B * TR])
    pevh = p.sb("pevh", [128, B * TR], BF16)
    pevl = p.sb("pevl", [128, B * TR], BF16)
    prq = [p.sb(f"prq{i}", [128, TR], BF16) for i in range(2)]
    prka = [p.sb(f"prka{i}", [128, TR], BF16) for i in range(2)]
    prkb = [p.sb(f"prkb{i}", [64, TR], BF16) for i in range(2)]
    avg = p.sb("avg", [128, 2, TR])
    m8 = p.sb("m8", [128, 8])
    amk = p.sb("amk", [128, 2, TR], BF16)
    kronsb = p.sb("kronsb", [4, 128], BF16)
    eselsb = p.sb("eselsb", [128, 2, MYROWS], BF16)
    btl = p.sb("btl", [4, MYROWS], BF16)
    binrep = p.sb("binrep", [128, NKC, MYROWS], BF16)
    kst = [p.sb(f"kst{i}", [128, N], BF16) for i in range(2)]
    vsth = p.sb("vsth", [128, NKC, 65], BF16)
    vstl = p.sb("vstl", [128, NKC, 65], BF16)
    qst = [p.sb(f"qst{i}", [128, QH], BF16) for i in range(2)]
    qlst = [p.sb(f"qlst{i}", [64, QH], BF16) for i in range(2)]
    pexp = [p.sb(f"pexp{i}", [128, QH]) for i in range(2)]
    phi = [p.sb(f"phi{i}", [128, QH], BF16) for i in range(2)]
    plo = [p.sb(f"plo{i}", [128, QH], BF16) for i in range(2)]
    rden = p.sb("rden", [1, QH])
    rdex = p.sb("rdex", [64, QH])
    ahat = p.sb("ahat", [64, QH])
    ah_h = p.sb("ah_h", [64, QH], BF16)
    ah_l = p.sb("ah_l", [64, QH], BF16)
    ahld = [p.sb(f"ahld{i}", [128, 2, 512], BF16) for i in range(2)]
    x0c = [p.sb(f"x0c{i}", [128, 512]) for i in range(2)]
    x1p = [p.sb(f"x1p{i}", [128, 512]) for i in range(2)]
    sq = p.sb("sq", [128, 512])
    stats = p.sb("stats", [1, 2, TOK])
    rden2 = None
    ones_c = p.sb("ones_c", [128, 1])
    ones_r = p.sb("ones_r", [1, 128])

    ps1024 = [p.ps(f"ps1024_{i}", [128, 1024]) for i in range(2)]
    ps512 = [p.ps(f"ps512_{i}", [128, 512]) for i in range(4)]

    s.dma(kronsb[:], kron4[:, :], tag="ldkron")
    s.dma(eselsb[:], esel.rearrange("(it p) m -> p it m", p=128), tag="ldesel")
    s.dve("memset", ones_c[:], 1.0, tag="onesc")
    s.dve("memset", ones_r[:], 1.0, tag="onesr")
    s.dve("memset", vsth[:, :, 64:65], 1.0, tag="vstones")
    s.dve("memset", vstl[:, :, 64:65], 0.0, tag="vstzer")

    for c in range(8):
        xb = xps[c % 2]
        s.dma(xb[:].rearrange("p (b r) -> p b r", b=B),
              xpg.rearrange("b (c p) r -> p c b r", p=128)[:, c, :, :],
              tag=f"ldxp{c}")
        emit_split(s, xphi[:, c, :], xplo[:, c, :], xb[:])
    avg_first = [True]
    for wi, wsrc in enumerate((wq, wk)):
        for hf in range(2):
            cols = slice(hf * 512, (hf + 1) * 512)
            s.dma(wfp[:], wsrc[:, cols].rearrange("(c p) o -> p c o", p=128),
                  q="dgps", tag=f"ldwqk{wi}{hf}")
            emit_split(s, whi[:, :, cols], wlo[:, :, cols], wfp[:])
        if wi == 0:
            qph_d = nc.dram_tensor("qph_d", [128, 8, B * TR], BF16,
                                   kind="Internal").ap()
            qpl_d = nc.dram_tensor("qpl_d", [128, 8, B * TR], BF16,
                                   kind="Internal").ap()
        for oc in range(8):
            pp = ps512[oc % 2][:, 0:B * TR]
            ocs = slice(oc * 128, (oc + 1) * 128)
            emit_proj3(s, pp,
                       lambda c: whi[:, c, ocs], lambda c: wlo[:, c, ocs],
                       lambda c: xphi[:, c, :], lambda c: xplo[:, c, :], 8,
                       tag="poolproj")
            s.scopy(pev[:], pp, tag="poolev")
            emit_split(s, pevh[:], pevl[:], pev[:])
            if wi == 0:
                s.dma(qph_d[:, oc, :], pevh[:], q="dgps", tag="stqp_h")
                s.dma(qpl_d[:, oc, :], pevl[:], q="dgps", tag="stqp_l")
            else:
                for hh in range(2):
                    h = 2 * oc + hh
                    ri = hh % 2
                    rs_ = slice(hh * 64, (hh + 1) * 64)
                    s.dma(prka[ri][0:64, :], pevh[rs_, 0:TR], q="dgps",
                          tag="prka1")
                    s.dma(prka[ri][64:128, :], pevh[rs_, 0:TR], q="dgps",
                          tag="prka2")
                    s.dma(prkb[ri][0:64, :], pevl[rs_, 0:TR], q="dgps",
                          tag="prkb")
                    s.dma(prq[ri][0:64, :], qph_d[rs_, oc, 0:TR], q="dgps",
                          tag="prq_h")
                    s.dma(prq[ri][64:128, :], qpl_d[rs_, oc, 0:TR], q="dgps",
                          tag="prq_l")
                    for it in range(2):
                        isl = slice(it * 128, (it + 1) * 128)
                        pp2 = ps1024[it][:, 0:TR]
                        s.mm(pp2, prq[ri][:, isl], prka[ri][:, :],
                             avg_first[0], False, tag="avgA")
                        s.mm(pp2, prq[ri][0:64, isl], prkb[ri][:, :],
                             False, False, tag="avgB")
                    avg_first[0] = False
                    bsl = slice(TR, 2 * TR)
                    s.dma(prka[ri][0:64, :], pevh[rs_, bsl], q="dgps",
                          tag="prka1b")
                    s.dma(prka[ri][64:128, :], pevh[rs_, bsl], q="dgps",
                          tag="prka2b")
                    s.dma(prkb[ri][0:64, :], pevl[rs_, bsl], q="dgps",
                          tag="prkbb")
                    s.dma(prq[ri][0:64, :], qph_d[rs_, oc, bsl], q="dgps",
                          tag="prq_hb")
                    s.dma(prq[ri][64:128, :], qpl_d[rs_, oc, bsl], q="dgps",
                          tag="prq_lb")
                    last = oc == 7 and hh == 1
                    for it in range(2):
                        isl = slice(it * 128, (it + 1) * 128)
                        pp2 = ps1024[it][:, 0:TR]
                        s.mm(pp2, prq[ri][:, isl], prka[ri][:, :],
                             False, False, tag="avgAb")
                        s.mm(pp2, prq[ri][0:64, isl], prkb[ri][:, :],
                             False, last, tag="avgBb")
    for it in range(2):
        s.act(avg[:, it, :], ps1024[it][:, 0:TR], AF.Copy,
              scale=1.0 / (B * H), tag="avgev")
    for it in range(2):
        w2 = avg[:, it, :]
        for r in range(7):
            s.dve("max", m8[:], w2, tag="max8")
            if r == 6:
                s.dve("memset", m8[:, 3:8], -1e38, tag="m8pad")
            s.dve("match_replace", w2, m8[:], w2, imm_value=NEG, tag="mrep")
    s.dve("tensor_scalar", amk[:], avg[:], NEG * 0.5, scalar2=NEG,
          op0=ALU.is_gt, op1=ALU.mult, tag="is_sel")
    if dbg:
        damk = p.dram_out("dbg_amk", [128, 2, TR], BF16)
        s.dma(damk[:, :, :], amk[:], q="dgps", tag="dbgamk")
        davg = p.dram_out("dbg_avg", [128, 2, TR])
        s.dma(davg[:, :, :], avg[:], q="dgps", tag="dbgavg")
    for kc in range(NKC):
        pp = ps512[2][0:4, 0:MYROWS]
        jsl = slice(kc * 4, (kc + 1) * 4)
        for it in range(2):
            s.mm(pp, amk[:, it, jsl], eselsb[:, it, :], it == 0, it == 1,
                 tag="eselmm")
        s.scopy(btl[:], pp, tag="eselev")
        pp2 = ps512[3][:, 0:MYROWS]
        s.mm(pp2, kronsb[:], btl[:], True, True, tag="binexp")
        s.scopy(binrep[:, kc, :], pp2, tag="binev")

    for hf in range(2):
        cols = slice(hf * 512, (hf + 1) * 512)
        s.dma(wfp[:], wo[:, cols].rearrange("(c p) o -> p c o", p=128),
              q="dgps", tag=f"ldwo{hf}")
        emit_split(s, whi[:, :, cols], wlo[:, :, cols], wfp[:])

    SCALE = 1.0 / math.sqrt(DH)
    for h in range(H):
        hb = h % 2
        s.dma(kst[hb][:], kpB[:, h, :], tag="ldk")
        for vq in range(4):
            vqs = slice(vq * 16, (vq + 1) * 16)
            vstage = pexp[0][:].rearrange("p (kc d) -> p kc d", d=64)[
                :, 0:16, :]
            s.dma(vstage, vfB[h, vq * 2048:(vq + 1) * 2048, :].rearrange(
                "(kc pp) d -> pp kc d", pp=128), tag="ldvf")
            emit_split(s, vsth[:, vqs, 0:64], vstl[:, vqs, 0:64], vstage)
        for qh in range(2):
            qsl = slice(qh * QH, (qh + 1) * QH)
            qb = qh % 2
            s.dma(qst[qb][:], qp2[:, h, qsl], q="dgps", tag="ldq")
            s.dma(qlst[qb][:], qlo[:, h, qsl], q="dgps", tag="ldql")
            pvp = [ps512[0][0:65, :], ps512[1][0:65, :]]
            for kc in range(NKC):
                kb = kc % 2
                stp = ps1024[kb]
                ksl = slice(kc * 128, (kc + 1) * 128)
                for half in range(2):
                    psl = slice(half * 512, (half + 1) * 512)
                    s.mm(stp[:, psl], kst[hb][:, ksl], qst[qb][:, psl],
                         True, False, tag="qkA")
                    s.mm(stp[:, psl], kst[hb][0:64, ksl], qlst[qb][:, psl],
                         False, True, tag="qkB")
                mrow = NKC * MYROWS
                mv = AP(binrep[:].tensor, kc * MYROWS + qh * 32,
                        [[mrow, 128], [1, 32], [0, 32]])
                s.dve("tensor_tensor",
                      stp[:].rearrange("p (r j) -> p r j", j=32),
                      stp[:].rearrange("p (r j) -> p r j", j=32),
                      mv, op=ALU.add, extra_reads=[binrep[:]], tag="maskadd")
                s.act(pexp[kb][:], stp[:], AF.Exp, scale=SCALE, tag="exp")
                s.act(phi[kb][:], stp[:], AF.Exp, scale=SCALE, tag="exph")
                s.dve("tensor_sub", plo[kb][:], pexp[kb][:], phi[kb][:],
                      tag="plosub")
                for pkc in ([kc - 1] if kc > 0 else []) + (
                        [kc] if kc == NKC - 1 else []):
                    pb_ = pkc % 2
                    for half in range(2):
                        psl = slice(half * 512, (half + 1) * 512)
                        pv = pvp[half]
                        s.mm(pv, vsth[:, pkc, :], phi[pb_][:, psl],
                             pkc == 0, False, tag="pvA")
                        s.mm(pv, vsth[:, pkc, :], plo[pb_][:, psl], False,
                             False, tag="pvB")
                        s.mm(pv, vstl[:, pkc, :], phi[pb_][:, psl], False,
                             pkc == NKC - 1, tag="pvC")
            for half in range(2):
                psl = slice(half * 512, (half + 1) * 512)
                s.dve("reciprocal", rden[:, psl], pvp[half][64:65, :],
                      tag="recip")
            pp = ps1024[0][0:64, :]
            for half in range(2):
                psl = slice(half * 512, (half + 1) * 512)
                s.mm(pp[:, psl], ones_r[0:1, 0:64], rden[:, psl],
                     True, True, tag="rdex")
            s.scopy(rdex[:], pp, tag="rdexev")
            for half in range(2):
                psl = slice(half * 512, (half + 1) * 512)
                s.dve("tensor_mul", ahat[:, psl], pvp[half][0:64, :],
                      rdex[:, psl], tag="ahat")
            emit_split(s, ah_h[:], ah_l[:], ahat[:])
            hp = h // 2
            r0 = (h % 2) * 64
            s.dma(ah_d[r0:r0 + 64, hp, 0, qsl], ah_h[:], q="dgps", tag="stah")
            s.dma(ah_d[r0:r0 + 64, hp, 1, qsl], ah_l[:], q="dgps", tag="stal")

    if dbg:
        dah = p.dram_out("dbg_ah", [128, 8, 2, TOK], BF16)
        for hp_ in range(8):
            s.dma(dah[:, hp_, :, :], ah_d[:, hp_, :, :], q="dgps", tag="dbgah")
    def octile(oc):
        if oc < 4:
            return ps512[oc][:, 0:512]
        i = oc - 4
        return ps1024[i // 2][:, (i % 2) * 512:(i % 2) * 512 + 512]

    for tt in range(4):
        tb = tt % 2
        tsl = slice(tt * 512, (tt + 1) * 512)
        for hp in range(8):
            ab_ = ahld[hp % 2]
            s.dma(ab_[:], ah_d[:, hp, :, tsl], tag="ldah")
            for oc in range(8):
                pb = octile(oc)
                ocs = slice(oc * 128, (oc + 1) * 128)
                s.mm(pb, whi[:, hp, ocs], ab_[:, 0, :], hp == 0, False,
                     tag="opA")
                s.mm(pb, whi[:, hp, ocs], ab_[:, 1, :], False, False,
                     tag="opB")
                s.mm(pb, wlo[:, hp, ocs], ab_[:, 0, :], False, hp == 7,
                     tag="opC")
        psum_s = ps512[0][0:1, 0:512]
        psum_q = ps512[0][32:33, 0:512]
        for oc in range(8):
            pb = octile(oc)
            s.dma(x0c[tb][:], fm_view(x0)[:, oc, tsl], tag="ldx0")
            s.dve("tensor_add", x1p[oc % 2][:], pb, x0c[tb][:], tag="res")
            s.dma(fm_view(x1pre_d)[:, oc, tsl], x1p[oc % 2][:], q="dgps",
                  tag="stx1p")
            s.act(sq[:], x1p[oc % 2][:], AF.Square, tag="sqr")
            s.mm(psum_s, ones_c[:], x1p[oc % 2][:], oc == 0, oc == 7,
                 tag="ssum")
            s.mm(psum_q, ones_c[:], sq[:], oc == 0, oc == 7, tag="sqsum")
        s.scopy(stats[:, 0, tsl], psum_s, tag="stev_s")
        s.scopy(stats[:, 1, tsl], psum_q, tag="stev_q")

    s.act(stats[:, 0, :], stats[:, 0, :], AF.Copy, scale=1.0 / D, tag="mu")
    s.act(stats[:, 1, :], stats[:, 1, :], AF.Copy, scale=1.0 / D, tag="exx")
    for hf in range(2):
        hsl = slice(hf * QH, (hf + 1) * QH)
        s.dve("tensor_mul", rden[:], stats[:, 0, hsl], stats[:, 0, hsl],
              tag="musq")
        s.dve("tensor_sub", stats[:, 1, hsl], stats[:, 1, hsl], rden[:],
              tag="var")
    for hf in range(2):
        hsl = slice(hf * QH, (hf + 1) * QH)
        s.dve("tensor_scalar_add", stats[:, 1, hsl], stats[:, 1, hsl],
              LN_EPS, tag="vare")
        s.dve("reciprocal", rden[:], stats[:, 1, hsl], tag="vrec")
        s.act(stats[:, 1, hsl], rden[:], AF.Sqrt, tag="rsq")
    s.dve("tensor_mul", stats[:, 0, :], stats[:, 0, :], stats[:, 1, :],
          tag="murs")
    s.dve("tensor_scalar_mul", stats[:, 0, :], stats[:, 0, :], -1.0,
          tag="negm")
    for tt in range(4):
        tb = tt % 2
        tsl = slice(tt * 512, (tt + 1) * 512)
        pa = ps512[0][:, 0:512]
        pbv = ps512[1][:, 0:512]
        s.mm(pa, ones_r[:], stats[:, 1, tsl], True, True, tag="aexp")
        s.mm(pbv, ones_r[:], stats[:, 0, tsl], True, True, tag="bexp")
        for oc in range(8):
            s.dma(x1p[tb][:], fm_view(x1pre_d)[:, oc, tsl], tag="ldx1p")
            s.dve("tensor_mul", x0c[tb][:], x1p[tb][:], pa, tag="lnmul")
            s.dve("tensor_add", x0c[tb][:], x0c[tb][:], pbv, tag="lnadd")
            s.dma(fm_view(x1o)[:, oc, tsl], x0c[tb][:], q="dgps", tag="stx1")

    nc_ = s.emit()
    p.es.close()
    return nc_


TOKF = 2048
CH = 512
NCK = TOKF // CH
HD4 = 4 * D
W1N = float(HD4 * D)


def build_EF(is_F, dbg=False):
    p = Prog("F" if is_F else "E")
    s = Sched(p)
    nc = p.nc
    x2 = p.dram_in("x2T", [D, TOKF])
    wqkvg = p.dram_in("wqkvg", [D, 3 * D])
    wvp = p.dram_in("wv", [D, D])
    bqg = p.dram_in("bqg", [1, 3 * D])
    bvp = p.dram_in("bv", [1, D])
    lgA = p.dram_in("lgA", [128, 8])
    lbA = p.dram_in("lbA", [128, 8])
    idf = p.dram_in("idf", [128, 128])
    if is_F:
        preS = p.dram_in("preS", [3, D, D])
        preK = p.dram_in("preK", [3, 128, 8])
        wpj = p.dram_in("wpj", [D, D])
        bpj = p.dram_in("bpj", [1, D])
        t1m = p.dram_in("t1m", [128, 896])
        w1t = p.dram_in("w1t", [D, HD4])
        b1c = p.dram_in("b1c", [128, 32])
        w2t = p.dram_in("w2t", [HD4, D])
        b2c8 = p.dram_in("b2c8", [128, 8])
        xout = p.dram_out("xoT", [D, TOKF])
        x3D = nc.dram_tensor("x3D", [D, TOKF], F32, kind="Internal").ap()
        w1qD = nc.dram_tensor("w1qD", [D, HD4], BF16, kind="Internal").ap()
        w2qD = nc.dram_tensor("w2qD", [HD4, D], BF16, kind="Internal").ap()
        rawD = nc.dram_tensor("rawD", [2, 4, 512], F32, kind="Internal").ap()
    else:
        SD = p.dram_out("SD", [D, D])
        kprD = p.dram_out("kprD", [128, 8])

    x2c = p.sb("x2c", [128, 8, CH])
    xnh = p.sb("xnh", [128, 8, CH], BF16)
    xnl = p.sb("xnl", [128, 8, CH], BF16)
    kfm = p.sb("kfm", [128, 8, CH])
    gat = p.sb("gat", [128, 8, CH])
    vtm = p.sb("vtm", [128, 4, D])
    ktm = p.sb("ktm", [128, 4, D])
    Ssb = p.sb("Ssb", [128, 8, D])
    kpre = p.sb("kpre", [128, 8])
    kt8 = p.sb("kt8", [128, 8])
    wsh = [p.sb(f"wsh{i}", [128, 8, 128], BF16) for i in range(2)]
    wsl = [p.sb(f"wsl{i}", [128, 8, 128], BF16) for i in range(2)]
    wvh = p.sb("wvh", [128, 8, 256], BF16)
    wvl = p.sb("wvl", [128, 8, 256], BF16)
    ev = [p.sb(f"ev{i}", [128, CH]) for i in range(2)]
    ev2 = p.sb("ev2", [128, CH])
    sqv = p.sb("sqv", [128, CH])
    uA = p.sb("uA", [128, CH])
    strow = p.sb("strow", [1, 2, CH])
    r1 = p.sb("r1", [1, CH])
    r2 = p.sb("r2", [1, CH])
    rowq = p.sb("rowq", [1, CH])
    rowsc = p.sb("rowsc", [1, CH])
    brow = [p.sb(f"brow{i}", [1, 128]) for i in range(2)]
    bvrow = p.sb("bvrow", [1, 256])
    xst = p.sb("xst", [128, 4])
    xr4 = p.sb("xr4", [4, 128])
    thr = p.sb("thr", [128, 8])
    wsum = p.sb("wsum", [128, 1])
    acol = p.sb("acol", [128, 1])
    bneg = p.sb("bneg", [128, 1])
    lg_s = p.sb("lg_s", [128, 8])
    lb_s = p.sb("lb_s", [128, 8])
    idf_s = p.sb("idf_s", [128, 128])
    ones_c = p.sb("ones_c", [128, 1])
    ones_r = p.sb("ones_r", [1, 128])
    ones_t = p.sb("ones_t", [1, 512])
    if is_F:
        qfm = p.sb("qfm", [128, 8, CH])
        Pt = p.sb("Pt", [128, 4, CH])
        t1_s = p.sb("t1_s", [128, 896])
        b1_s = p.sb("b1_s", [128, 32])
        b2_s = p.sb("b2_s", [128, 8])
        sc2s = p.sb("sc2s", [128, CH])

    ps = [p.ps(f"ps{i}", [128, 512]) for i in range(8)]

    s.dma(lg_s[:], lgA[:, :], tag="ldc")
    s.dma(lb_s[:], lbA[:, :], tag="ldc")
    s.dma(idf_s[:], idf[:, :], tag="ldc")
    if is_F:
        s.dma(t1_s[:], t1m[:, :], tag="ldc")
        s.dma(b1_s[:], b1c[:, :], tag="ldc")
        s.dma(b2_s[:], b2c8[:, :], tag="ldc")
    s.dve("memset", ones_c[:], 1.0, tag="onesc")
    s.dve("memset", ones_r[:], 1.0, tag="onesr")
    s.dve("memset", bneg[:], -0.5, tag="bneg")
    s.dve("memset", ones_t[:], 1.0, tag="onest")

    Sv = SD.rearrange("(c p) v -> p c v", p=128) if not is_F else None
    if is_F:
        pv = preS.rearrange("s (c p) v -> s p c v", p=128)
        s.dma(Ssb[:], pv[0], tag="ldS0")
        for j in (1, 2):
            s.dma(vtm[:], pv[j][:, 0:4, :], tag="ldSa")
            s.dma(ktm[:], pv[j][:, 4:8, :], tag="ldSb")
            for c in range(4):
                s.dve("tensor_add", Ssb[:, c, :], Ssb[:, c, :], vtm[:, c, :],
                      tag="Sacc")
                s.dve("tensor_add", Ssb[:, 4 + c, :], Ssb[:, 4 + c, :],
                      ktm[:, c, :], tag="Sacc")
        s.dma(kpre[:], preK[0, :, :], tag="ldK0")
        for j in (1, 2):
            s.dma(kt8[:], preK[j, :, :], tag="ldKj")
            s.dve("tensor_add", kpre[:], kpre[:], kt8[:], tag="Kacc")
    else:
        s.dve("memset", Ssb[:], 0.0, tag="zS")
        s.dve("memset", kpre[:], 0.0, tag="zK")

    def emit_elu1(dst, src_psum, tmp_r, tmp_e):
        s.act(tmp_r, src_psum, AF.Relu, tag="elur")
        s.dve("tensor_sub", tmp_e, src_psum, tmp_r, tag="elus")
        s.act(tmp_e, tmp_e, AF.Exp, tag="elue")
        s.dve("tensor_add", dst, tmp_e, tmp_r, tag="elua")

    for ck in range(NCK):
        tsl = slice(ck * CH, (ck + 1) * CH)
        psum_s = ps[0][0:1, 0:512]
        psum_q = ps[0][32:33, 0:512]
        for oc in range(8):
            s.dma(x2c[:, oc, :], fm_view(x2)[:, oc, tsl], tag="ldx2")
            s.act(sqv[:], x2c[:, oc, :], AF.Square, tag="lnsq")
            s.mm(psum_s, ones_c[:], x2c[:, oc, :], oc == 0, oc == 7,
                 tag="lnsum")
            s.mm(psum_q, ones_c[:], sqv[:], oc == 0, oc == 7, tag="lnsq2")
        s.scopy(strow[:, 0, :], psum_s, tag="lnev_s")
        s.scopy(strow[:, 1, :], psum_q, tag="lnev_q")
        s.act(strow[:, 0, :], strow[:, 0, :], AF.Copy, scale=1.0 / D,
              tag="lnmu")
        s.act(strow[:, 1, :], strow[:, 1, :], AF.Copy, scale=1.0 / D,
              tag="lnex")
        s.dve("tensor_mul", r1[:], strow[:, 0, :], strow[:, 0, :], tag="lnm2")
        s.dve("tensor_sub", strow[:, 1, :], strow[:, 1, :], r1[:], tag="lnvr")
        s.dve("tensor_scalar_add", strow[:, 1, :], strow[:, 1, :], LN_EPS,
              tag="lnve")
        s.dve("reciprocal", r1[:], strow[:, 1, :], tag="lnrc")
        s.act(strow[:, 1, :], r1[:], AF.Sqrt, tag="lnrs")
        s.dve("tensor_mul", strow[:, 0, :], strow[:, 0, :], strow[:, 1, :],
              tag="lnms")
        s.dve("tensor_scalar_mul", strow[:, 0, :], strow[:, 0, :], -1.0,
              tag="lnng")
        pa = ps[1][:, 0:512]
        pb = ps[2][:, 0:512]
        s.mm(pa, ones_r[:], strow[:, 1, :], True, True, tag="lnae")
        s.mm(pb, ones_r[:], strow[:, 0, :], True, True, tag="lnbe")
        for oc in range(8):
            s.dve("tensor_mul", kfm[:, oc, :], x2c[:, oc, :], pa, tag="lnap")
        for oc in range(8):
            s.dve("tensor_add", kfm[:, oc, :], kfm[:, oc, :], pb, tag="lnad")
        for oc in range(8):
            s.dve("tensor_scalar", kfm[:, oc, :], kfm[:, oc, :],
                  lg_s[:, oc:oc + 1], scalar2=lb_s[:, oc:oc + 1],
                  op0=ALU.mult, op1=ALU.add, tag="lngb")
        for oc in range(8):
            s.dve("tensor_copy", xnh[:, oc, :], kfm[:, oc, :],
                  tag="split_hi")
        for oc in range(8):
            s.dve("tensor_sub", xnl[:, oc, :], kfm[:, oc, :], xnh[:, oc, :],
                  tag="split_lo")

        kinds = ([(0, "q"), (1, "g"), (2, "k")] if is_F
                 else [(1, "g"), (2, "k")])
        fiters = [(kd, knm, oc8) for kd, knm in kinds for oc8 in range(8)]

        def fm_ldw(i):
            kd, knm, oc8 = fiters[i]
            oc = kd * 8 + oc8
            bi = oc % 2
            wb = x2c[:, :, 256 + bi * 128:256 + (bi + 1) * 128]
            s.dma(wb, wqkvg[:, oc * 128:(oc + 1) * 128].rearrange(
                "(c p) o -> p c o", p=128), q="dgps", tag="ldwq")
            s.dma(brow[bi][:], bqg[:, oc * 128:(oc + 1) * 128],
                  q="dgps", tag="ldbq")
            emit_split(s, wsh[bi][:], wsl[bi][:], wb)

        def fm_evac(i):
            kd, knm, oc8 = fiters[i]
            bi = (kd * 8 + oc8) % 2
            pp = ps[3 + bi][:, 0:512]
            if knm == "q":
                emit_elu1(qfm[:, oc8, :], pp, sqv[:], ev[0][:])
            elif knm == "g":
                s.act(gat[:, oc8, :], pp, AF.Sigmoid, tag="sigm")
            else:
                s.dve("tensor_mul", ev[1][:], pp, gat[:, oc8, :],
                      tag="kgate")
                emit_elu1(kfm[:, oc8, :], ev[1][:], sqv[:], ev[0][:])

        for fi in range(len(fiters)):
            kd, knm, oc8 = fiters[fi]
            bi = (kd * 8 + oc8) % 2
            if fi == 0:
                fm_ldw(0)
            pp = ps[3 + bi][:, 0:512]
            emit_proj3(s, pp,
                       lambda c: wsh[bi][:, c, :],
                       lambda c: wsl[bi][:, c, :],
                       lambda c: xnh[:, c, :], lambda c: xnl[:, c, :],
                       8, start=True, stop=False, tag="prj" + knm)
            s.mm(pp, brow[bi][:], ones_t[:], False, True, tag="bfold")
            if fi + 1 < len(fiters):
                fm_ldw(fi + 1)
            if fi > 0:
                fm_evac(fi - 1)
        fm_evac(len(fiters) - 1)

        for dvq in range(4):
            vsl = slice(dvq * 256, (dvq + 1) * 256)
            s.dma(x2c[:, :, 0:256], wvp[:, vsl].rearrange(
                "(c p) o -> p c o", p=128), q="dgps", tag="ldwv")
            s.dma(bvrow[:], bvp[:, vsl], q="dgps", tag="ldbv")
            emit_split(s, wvh[:], wvl[:], x2c[:, :, 0:256])
            for jb in range(4):
                jsl = slice(jb * 128, (jb + 1) * 128)
                pp = ps[3 + jb % 2][:, 0:256]
                first = True
                for c in range(8):
                    s.mm(pp, xnh[:, c, jsl], wvh[:, c, :], first, False,
                         tag="vprj")
                    first = False
                    s.mm(pp, xnh[:, c, jsl], wvl[:, c, :], False, False,
                         tag="vprj")
                    s.mm(pp, xnl[:, c, jsl], wvh[:, c, :], False, False,
                         tag="vprj")
                s.mm(pp, ones_r[:], bvrow[:], False, True, tag="vbias")
                s.scopy(vtm[:, jb, vsl], pp, tag="vev")

        for c in range(8):
            for jb in range(4):
                pp = ps[5 + jb % 2][:, 0:128]
                s.transpose(pp, kfm[:, c, jb * 128:(jb + 1) * 128], idf_s[:])
                s.scopy(ktm[:, jb, c * 128:(c + 1) * 128], pp, tag="ktev")

        if is_F:
            for jb in range(4):
                pp = ps[3 + jb % 2][:, 0:512]
                for c in range(8):
                    s.mm(pp, kfm[:, c, jb * 128:(jb + 1) * 128],
                         qfm[:, c, :], c == 0, c == 7, tag="atmm")
                z0 = 384 - 128 * jb
                s.dve("tensor_mul", Pt[:, jb, :], pp,
                      t1_s[:, z0:z0 + 512], tag="atmask")
            pd = ps[7][0:1, 0:512]
            for jb in range(4):
                s.mm(pd, ones_c[:], Pt[:, jb, :], jb == 0, False, tag="denA")
            for c in range(8):
                s.mm(pd, kpre[:, c:c + 1], qfm[:, c, :], False, c == 7,
                     tag="denB")
            s.scopy(r1[:], pd, tag="denev")
            s.dve("tensor_scalar_add", r1[:], r1[:], DEN_EPS, tag="deneps")
            s.dve("reciprocal", r1[:], r1[:], tag="denrc")
            pdb = ps[7][:, 0:512]
            s.mm(pdb, ones_r[:], r1[:], True, True, tag="denbc")
            s.scopy(uA[:], pdb, tag="denbev")
            for dc in range(8):
                pn = ps[dc % 3][:, 0:512]
                dsl = slice(dc * 128, (dc + 1) * 128)
                for jb in range(4):
                    s.mm(pn, vtm[:, jb, dsl], Pt[:, jb, :], jb == 0, False,
                         tag="numA")
                for c in range(8):
                    s.mm(pn, Ssb[:, c, dsl], qfm[:, c, :], False, c == 7,
                         tag="numB")
                s.dve("tensor_mul", gat[:, dc, :], pn, uA[:], tag="numev")

        for c in range(8):
            for dvh in range(2):
                pp = ps[5 + dvh][:, 0:512]
                hsl = slice(dvh * 512, (dvh + 1) * 512)
                for jb in range(4):
                    s.mm(pp, ktm[:, jb, c * 128:(c + 1) * 128],
                         vtm[:, jb, hsl], jb == 0, jb == 3, tag="supd")
                s.dve("tensor_add", Ssb[:, c, hsl], Ssb[:, c, hsl], pp,
                      tag="sacc")
        s.dve("reduce_sum", kt8[:], kfm[:], axis=AX.X, tag="kinc")
        s.dve("tensor_add", kpre[:], kpre[:], kt8[:], tag="kprea")

        if is_F:
            emit_split(s, xnh[:], xnl[:], gat[:])
            psum_s2 = ps[0][0:1, 0:512]
            psum_q2 = ps[0][32:33, 0:512]
            pre3 = vtm[:].rearrange("p a b -> p (a b)")
            for oc in range(8):
                bi = oc % 2
                wb = x2c[:, :, 256 + bi * 128:256 + (bi + 1) * 128]
                s.dma(wb, wpj[:, oc * 128:(oc + 1) * 128].rearrange(
                    "(c p) o -> p c o", p=128), q="dgps", tag="ldwp")
                s.dma(brow[bi][:], bpj[:, oc * 128:(oc + 1) * 128],
                      q="dgps", tag="ldbp")
                emit_split(s, wsh[bi][:], wsl[bi][:], wb)
                pp = ps[3 + bi][:, 0:512]
                emit_proj3(s, pp,
                           lambda c: wsh[bi][:, c, :],
                           lambda c: wsl[bi][:, c, :],
                           lambda c: xnh[:, c, :], lambda c: xnl[:, c, :],
                           8, start=True, stop=False, tag="prjo")
                s.mm(pp, brow[bi][:], ones_t[:], False, True, tag="bfoldo")
                s.dma(ev2[:], fm_view(x2)[:, oc, tsl], tag="ldx2r")
                pslc = pre3[:, oc * 512:(oc + 1) * 512]
                s.dve("tensor_add", pslc, pp, ev2[:], tag="ores")
                s.act(sqv[:], pslc, AF.Square, tag="osq")
                s.mm(psum_s2, ones_c[:], pslc, oc == 0, oc == 7, tag="osum")
                s.mm(psum_q2, ones_c[:], sqv[:], oc == 0, oc == 7, tag="osq2")
            s.scopy(strow[:, 0, :], psum_s2, tag="oev_s")
            s.scopy(strow[:, 1, :], psum_q2, tag="oev_q")
            s.act(strow[:, 0, :], strow[:, 0, :], AF.Copy, scale=1.0 / D,
                  tag="omu")
            s.act(strow[:, 1, :], strow[:, 1, :], AF.Copy, scale=1.0 / D,
                  tag="oex")
            s.dve("tensor_mul", r1[:], strow[:, 0, :], strow[:, 0, :],
                  tag="om2")
            s.dve("tensor_sub", strow[:, 1, :], strow[:, 1, :], r1[:],
                  tag="ovr")
            s.dve("tensor_scalar_add", strow[:, 1, :], strow[:, 1, :],
                  LN_EPS, tag="ove")
            s.dve("reciprocal", r1[:], strow[:, 1, :], tag="orc")
            s.act(strow[:, 1, :], r1[:], AF.Sqrt, tag="ors")
            s.dve("tensor_mul", strow[:, 0, :], strow[:, 0, :],
                  strow[:, 1, :], tag="oms")
            s.dve("tensor_scalar_mul", strow[:, 0, :], strow[:, 0, :], -1.0,
                  tag="ong")
            pa2 = ps[1][:, 0:512]
            pb2 = ps[2][:, 0:512]
            s.mm(pa2, ones_r[:], strow[:, 1, :], True, True, tag="oae")
            s.mm(pb2, ones_r[:], strow[:, 0, :], True, True, tag="obe")
            for oc in range(8):
                s.dve("tensor_mul", qfm[:, oc, :],
                      pre3[:, oc * 512:(oc + 1) * 512], pa2, tag="olm")
            for oc in range(8):
                s.dve("tensor_add", qfm[:, oc, :], qfm[:, oc, :], pb2,
                      tag="ola")
            for oc in range(8):
                s.dma(fm_view(x3D)[:, oc, tsl], qfm[:, oc, :], q="dgps",
                      tag="stx3")

    if not is_F:
        for c in range(8):
            s.dma(Sv[:, c, :], Ssb[:, c, :], q="dgps", tag="stS")
        s.dma(kprD[:, :], kpre[:], q="dgps", tag="stkp")
        if dbg:
            pass
        nc_ = s.emit()
        p.es.close()
        return nc_

    if dbg:
        dx3 = p.dram_out("dbg_x3", [D, TOKF])
        for c in range(8):
            s.dma(fm_view(dx3)[:, c, :], fm_view(x3D)[:, c, :], q="dgps",
                  tag="dbgx3")

    x2f = x2c[:].rearrange("p a b -> p (a b)")
    kf_f = kfm[:].rearrange("p a b -> p (a b)")
    ga_f = gat[:].rearrange("p a b -> p (a b)")
    xh_f = xnh[:].rearrange("p a b -> p (a b)")
    Ss_f = Ssb[:].rearrange("p a b -> p (a b)")
    qf_f = qfm[:].rearrange("p a b -> p (a b)")
    w1v = w1t.rearrange("(c p) h -> p c h", p=128)
    w2v = w2t.rearrange("(c p) d -> p c d", p=128)
    w1qv = w1qD.rearrange("(c p) h -> p c h", p=128)
    w2qv = w2qD.rearrange("(c p) d -> p c d", p=128)

    for wi, (wview, nhg, wtot) in enumerate(((w1v, 8, W1N), (w2v, 32, W1N))):
        stage = (x2c[:] if wi == 0 else
                 x2f.rearrange("p (a b) -> p a b", a=4))
        s.dve("memset", wsum[:], 0.0, tag="zws")
        for g in range(8):
            if wi == 0:
                src = wview[:, :, g * 512:(g + 1) * 512]
            else:
                src = wview[:, g * 4:(g + 1) * 4, :]
            s.dma(stage, src, tag="ldwg")
            s.act(x2f[:, 0:4096], x2f[:, 0:4096], AF.Abs, accum=acol[:],
                  tag="wabs")
            s.dve("tensor_add", wsum[:], wsum[:], acol[:], tag="wsacc")
        pt = ps[0][0:1, 0:1]
        s.mm(pt, ones_c[:], wsum[:], True, True, tag="wtot")
        s.scopy(r1[0:1, 0:1], pt, tag="wtev")
        s.act(r1[0:1, 0:1], r1[0:1, 0:1], AF.Copy, scale=1.0 / wtot,
              tag="wmean")
        s.dve("tensor_scalar_max", r1[0:1, 0:1], r1[0:1, 0:1], BIT_EPS,
              tag="wclip")
        pb = ps[1][:, 0:1]
        s.mm(pb, ones_r[:], r1[0:1, 0:1], True, True, tag="wbc")
        c0 = wi * 3
        s.dve("tensor_scalar_mul", thr[:, c0:c0 + 1], pb, 0.5, tag="thrp")
        s.dve("tensor_scalar_mul", thr[:, c0 + 1:c0 + 2], pb, -0.5,
              tag="thrn")
        s.dve("tensor_scalar_mul", thr[:, c0 + 2:c0 + 3], pb, 1.0 / QB,
              tag="thrq")
        qstage = (xnh[:] if wi == 0 else
                  xh_f.rearrange("p (a b) -> p a b", a=4))
        for g in range(8):
            if wi == 0:
                src = wview[:, :, g * 512:(g + 1) * 512]
                dst = w1qv[:, :, g * 512:(g + 1) * 512]
            else:
                src = wview[:, g * 4:(g + 1) * 4, :]
                dst = w2qv[:, g * 4:(g + 1) * 4, :]
            s.dma(stage, src, tag="ldwg2")
            s.dve("tensor_scalar", kf_f[:, 0:4096], x2f[:, 0:4096],
                  thr[:, c0:c0 + 1], scalar2=None, op0=ALU.is_ge, tag="tern1")
            s.dve("tensor_scalar", ga_f[:, 0:4096], x2f[:, 0:4096],
                  thr[:, c0 + 1:c0 + 2], scalar2=None, op0=ALU.is_le,
                  tag="tern2")
            s.dve("tensor_sub", xh_f[:, 0:4096], kf_f[:, 0:4096],
                  ga_f[:, 0:4096], tag="tern3")
            s.dma(dst, qstage, q="dgps", tag="stwq")

    RCONST = [128.0, 64.0, 32.0, 16.0, 8.0, 4.0, 2.0, 1.0]

    def emit_round_group(srcs, qpsum, dsts, ybufs, ubufs, sqb):
        n = len(srcs)
        for g in range(n):
            s.dve("tensor_mul", ybufs[g], srcs[g], qpsum, tag="rny")
        for g in range(n):
            s.dve("tensor_mul", ubufs[g], srcs[g], qpsum, tag="rnu")
        for ckst in RCONST:
            for g in range(n):
                s.act(sqb[g % len(sqb)], ubufs[g], AF.Sign,
                      bias=bneg[:, 0:1], tag="rsgn")
                s.dve("scalar_tensor_tensor", ubufs[g], sqb[g % len(sqb)],
                      -ckst / 2.0, ubufs[g], op0=ALU.mult, op1=ALU.add,
                      tag="rupd")
        for g in range(n):
            s.dve("scalar_tensor_tensor", ybufs[g], ubufs[g], -1.0,
                  ybufs[g], op0=ALU.mult, op1=ALU.add, tag="rfin")
        for g in range(n):
            s.dve("tensor_scalar_add", dsts[g], ybufs[g], 0.5, tag="rcast")

    def hbuf(hc):
        if hc < 16:
            return Ss_f[:, hc * 512:(hc + 1) * 512]
        if hc < 24:
            return qf_f[:, (hc - 16) * 512:(hc - 15) * 512]
        return kf_f[:, (hc - 24) * 512:(hc - 23) * 512]

    def emit_rows(par, col0, thr_col):
        s.dve("tensor_scalar_max", xst[:], xst[:], BIT_EPS, tag="xsc")
        pr = ps[6][0:4, 0:128]
        s.mm(pr, xst[:], idf_s[:], True, True, tag="xsrow")
        s.dve("reciprocal", xr4[:], pr, tag="xsrc")
        s.dve("tensor_scalar_mul", xr4[:], xr4[:], QB, tag="xsq")
        s.dma(rawD[par, :, col0:col0 + 128], xr4[:], q="dgps", tag="strow")
        s.dma(rowq[:].rearrange("a (b c) -> a b c", b=4),
              rawD[par:par + 1, :, col0:col0 + 128], tag="ldrow")
        s.dve("tensor_scalar_mul", xr4[:], pr, thr_col, tag="xssc")
        s.dma(rawD[par, :, col0 + 128:col0 + 256], xr4[:], q="dgps",
              tag="strow2")
        s.dma(rowsc[:].rearrange("a (b c) -> a b c", b=4),
              rawD[par:par + 1, :, col0 + 128:col0 + 256], tag="ldrow2")

    w2pieces = [xnh[:, :, 0:256], xnh[:, :, 256:512], wvh[:], wvl[:]]
    for tt in range(4):
        par = tt % 2
        tsl = slice(tt * CH, (tt + 1) * CH)
        for oc in range(8):
            s.dma(x2c[:, oc, :], fm_view(x3D)[:, oc, tsl], tag="ldx3")
        s.dve("tensor_reduce", ev[0][:],
              x2c[:].rearrange("p c t -> p t c"), axis=AX.X, op=ALU.max,
              apply_absolute_value=True, tag="absr")
        for sub in range(4):
            pp = ps[5][:, 0:128]
            s.transpose(pp, ev[0][:, sub * 128:(sub + 1) * 128], idf_s[:])
            s.dve("reduce_max", xst[:, sub:sub + 1], pp, axis=AX.X,
                  tag="xsr")
        emit_rows(par, 0, thr[0:4, 2:3])
        pqb = ps[2][:, 0:512]
        psc = ps[1][:, 0:512]
        s.mm(pqb, ones_r[:], rowq[:], True, True, tag="qbc")
        s.mm(psc, ones_r[:], rowsc[:], True, True, tag="scbc")
        s.scopy(sc2s[:], psc, tag="sc1ev")
        sqb4 = [ev[0][:], ev[1][:], sqv[:], uA[:]]
        emit_round_group([x2c[:, c, :] for c in range(8)], pqb,
                         [xnh[:, c, :] for c in range(8)],
                         [Ss_f[:, g * 512:(g + 1) * 512] for g in range(8)],
                         [Ss_f[:, (8 + g) * 512:(9 + g) * 512]
                          for g in range(8)], sqb4)
        vt_f = vtm[:].rearrange("p a b -> p (a b)")
        kt_f = ktm[:].rearrange("p a b -> p (a b)")
        for hc in range(32):
            bi = hc % 2
            s.dma(wsh[bi][:], w1qv[:, :, hc * 128:(hc + 1) * 128],
                  tag="ldw1q")
            pp = ps[3 + bi][:, 0:512]
            for c in range(8):
                s.mm(pp, wsh[bi][:, c, :], xnh[:, c, :], c == 0, c == 7,
                     tag="bl1")
            s.dve("tensor_mul", ev[0][:], pp, sc2s[:], tag="dq1")
            s.act(hbuf(hc), ev[0][:], AF.Gelu, bias=b1_s[:, hc:hc + 1],
                  tag="gelu1")
        s.dve("tensor_reduce", ev[0][:],
              Ssb[:].rearrange("p c v -> p (c v)").rearrange(
                  "p (a t) -> p t a", t=512), axis=AX.X, op=ALU.max,
              apply_absolute_value=True, tag="absr2a")
        s.dve("tensor_reduce", ev2[:],
              qfm[:].rearrange("p c t -> p t c"), axis=AX.X, op=ALU.max,
              apply_absolute_value=True, tag="absr2b")
        s.dve("tensor_max", ev[0][:], ev[0][:], ev2[:], tag="absm1")
        s.dve("tensor_reduce", ev2[:],
              kfm[:].rearrange("p c t -> p t c"), axis=AX.X, op=ALU.max,
              apply_absolute_value=True, tag="absr2c")
        s.dve("tensor_max", ev[0][:], ev[0][:], ev2[:], tag="absm2")
        for sub in range(4):
            pp = ps[5][:, 0:128]
            s.transpose(pp, ev[0][:, sub * 128:(sub + 1) * 128], idf_s[:])
            s.dve("reduce_max", xst[:, sub:sub + 1], pp, axis=AX.X,
                  tag="xsr2")
        emit_rows(par, 256, thr[0:4, 5:6])
        s.mm(pqb, ones_r[:], rowq[:], True, True, tag="qbc2")
        s.mm(psc, ones_r[:], rowsc[:], True, True, tag="scbc2")
        s.scopy(sc2s[:], psc, tag="sc2ev")
        qb2s = gat[:, 0, :]
        s.scopy(qb2s, pqb, tag="qb2ev")
        for G in range(8):
            gb0 = (G % 2) * 4
            hqb = xnl[:, gb0:gb0 + 4, :]
            for pc in range(4):
                s.dma(w2pieces[pc][:, 0:4, :],
                      w2qv[:, G * 4:(G + 1) * 4, pc * 256:(pc + 1) * 256],
                      q="dgps", tag="ldw2q")
            emit_round_group([hbuf(G * 4 + g) for g in range(4)], qb2s,
                             [hqb[:, g, :] for g in range(4)],
                             [vt_f[:, g * 512:(g + 1) * 512]
                              for g in range(4)],
                             [kt_f[:, g * 512:(g + 1) * 512]
                              for g in range(4)], sqb4)
            for oc in range(8):
                pp = ps[oc][:, 0:512]
                rem = (oc % 2) * 128
                for hc8 in range(4):
                    wap = w2pieces[oc // 2][:, hc8, rem:rem + 128]
                    s.mm(pp, wap, hqb[:, hc8, :], G == 0 and hc8 == 0,
                         G == 7 and hc8 == 3, tag="bl2")
        pre4 = ktm[:].rearrange("p a b -> p (a b)")
        for oc in range(8):
            s.dve("tensor_mul", vt_f[:, oc * 512:(oc + 1) * 512],
                  ps[oc][:, 0:512], sc2s[:], tag="dq2")
        for oc in range(8):
            s.dve("tensor_scalar_add", vt_f[:, oc * 512:(oc + 1) * 512],
                  vt_f[:, oc * 512:(oc + 1) * 512], b2_s[:, oc:oc + 1],
                  tag="b2a")
        for oc in range(8):
            s.dve("tensor_add", pre4[:, oc * 512:(oc + 1) * 512],
                  vt_f[:, oc * 512:(oc + 1) * 512], x2c[:, oc, :],
                  tag="fres")
        psum_s3 = ps[0][0:1, 0:512]
        psum_q3 = ps[0][32:33, 0:512]
        for oc in range(8):
            pslc = pre4[:, oc * 512:(oc + 1) * 512]
            s.act(ev2[:], pslc, AF.Square, tag="fsq")
            s.mm(psum_s3, ones_c[:], pslc, oc == 0, oc == 7, tag="fsum")
            s.mm(psum_q3, ones_c[:], ev2[:], oc == 0, oc == 7, tag="fsq2")
        s.scopy(strow[:, 0, :], psum_s3, tag="fev_s")
        s.scopy(strow[:, 1, :], psum_q3, tag="fev_q")
        s.act(strow[:, 0, :], strow[:, 0, :], AF.Copy, scale=1.0 / D,
              tag="fmu2")
        s.act(strow[:, 1, :], strow[:, 1, :], AF.Copy, scale=1.0 / D,
              tag="fex2")
        s.dve("tensor_mul", r1[:], strow[:, 0, :], strow[:, 0, :], tag="fm22")
        s.dve("tensor_sub", strow[:, 1, :], strow[:, 1, :], r1[:], tag="fvr2")
        s.dve("tensor_scalar_add", strow[:, 1, :], strow[:, 1, :], LN_EPS,
              tag="fve2")
        s.dve("reciprocal", r1[:], strow[:, 1, :], tag="frc2")
        s.act(strow[:, 1, :], r1[:], AF.Sqrt, tag="frs2")
        s.dve("tensor_mul", strow[:, 0, :], strow[:, 0, :], strow[:, 1, :],
              tag="fms2")
        s.dve("tensor_scalar_mul", strow[:, 0, :], strow[:, 0, :], -1.0,
              tag="fng2")
        pa3 = ps[1][:, 0:512]
        pb3 = ps[2][:, 0:512]
        s.mm(pa3, ones_r[:], strow[:, 1, :], True, True, tag="fae")
        s.mm(pb3, ones_r[:], strow[:, 0, :], True, True, tag="fbe")
        for oc in range(8):
            s.dve("tensor_mul", vt_f[:, oc * 512:(oc + 1) * 512],
                  pre4[:, oc * 512:(oc + 1) * 512], pa3, tag="flm2")
        for oc in range(8):
            s.dve("tensor_add", vt_f[:, oc * 512:(oc + 1) * 512],
                  vt_f[:, oc * 512:(oc + 1) * 512], pb3, tag="fla2")
        for oc in range(8):
            s.dma(fm_view(xout)[:, oc, tsl],
                  vt_f[:, oc * 512:(oc + 1) * 512], q="dgps", tag="stxf")

    nc_ = s.emit()
    p.es.close()
    return nc_


def host_t1m():
    pq = np.arange(128)[:, None]
    ii = np.arange(896)[None, :]
    return np.ascontiguousarray((ii >= pq + 384).astype(np.float32))


def _l1_common(x2_core, w_qkv, b_qkv, w_gate, b_gate, ln_g, ln_b):
    wqkvg = np.concatenate([w_qkv[:, 0:D], w_gate, w_qkv[:, D:2 * D]], 1)
    bqg = np.concatenate([b_qkv[0:D], b_gate, b_qkv[D:2 * D]])[None]
    return {
        "x2T": np.ascontiguousarray(np.asarray(x2_core, np.float32).T),
        "wqkvg": np.ascontiguousarray(wqkvg, dtype=np.float32),
        "wv": np.ascontiguousarray(w_qkv[:, 2 * D:3 * D], dtype=np.float32),
        "bqg": np.ascontiguousarray(bqg, dtype=np.float32),
        "bv": np.ascontiguousarray(b_qkv[2 * D:3 * D][None],
                                   dtype=np.float32),
        "lgA": np.ascontiguousarray(np.asarray(ln_g, np.float32
                                               ).reshape(8, 128).T),
        "lbA": np.ascontiguousarray(np.asarray(ln_b, np.float32
                                               ).reshape(8, 128).T),
        "idf": np.eye(128, dtype=np.float32),
    }


def make_E_inputs(x2_core, w_qkv, b_qkv, w_gate, b_gate, ln_g, ln_b):
    return _l1_common(x2_core, w_qkv, b_qkv, w_gate, b_gate, ln_g, ln_b)


def make_F_inputs(x2_core, w_qkv, b_qkv, w_gate, b_gate, ln_g, ln_b,
                  w_proj, b_proj, W1, b1, W2, b2, preS, preK):
    ins = _l1_common(x2_core, w_qkv, b_qkv, w_gate, b_gate, ln_g, ln_b)
    ins.update({
        "preS": np.ascontiguousarray(preS, dtype=np.float32),
        "preK": np.ascontiguousarray(preK, dtype=np.float32),
        "wpj": np.ascontiguousarray(w_proj, dtype=np.float32),
        "bpj": np.ascontiguousarray(np.asarray(b_proj, np.float32)[None]),
        "t1m": host_t1m(),
        "w1t": np.ascontiguousarray(np.asarray(W1, np.float32).T),
        "b1c": np.ascontiguousarray(np.asarray(b1, np.float32
                                               ).reshape(32, 128).T),
        "w2t": np.ascontiguousarray(np.asarray(W2, np.float32).T),
        "b2c8": np.ascontiguousarray(np.asarray(b2, np.float32
                                                ).reshape(8, 128).T),
    })
    return ins


_NC_CACHE = {}


def _get_nc(name):
    if name not in _NC_CACHE:
        if name == "A":
            _NC_CACHE[name] = build_A()
        elif name == "B":
            _NC_CACHE[name] = build_B()
        elif name == "C":
            _NC_CACHE[name] = build_C(block_ln=False)
        elif name == "D":
            _NC_CACHE[name] = build_C(block_ln=True)
        elif name == "E":
            _NC_CACHE[name] = build_EF(False)
        elif name == "F":
            _NC_CACHE[name] = build_EF(True)
    return _NC_CACHE[name]


def _launch(name, in_maps):
    res = run_bass_kernel_spmd(_get_nc(name), in_maps,
                               core_ids=list(range(NCORES)))
    return res.results


def kernel(**inputs):
    f32c = lambda a: np.ascontiguousarray(np.asarray(a, np.float32))
    x0 = f32c(inputs["L0_attn_x"])
    WQ = f32c(inputs["L0_attn_W_Q"])
    WK = f32c(inputs["L0_attn_W_K"])
    WV = f32c(inputs["L0_attn_W_V"])
    WO = f32c(inputs["L0_attn_W_O"])
    wqkv = f32c(inputs["L0_ffn_wqkv"])
    qsc = f32c(inputs["L0_ffn_q_scale"])
    ksc = f32c(inputs["L0_ffn_k_scale"])
    wo2 = f32c(inputs["L0_ffn_wo"])
    lng = f32c(inputs["L0_ffn_ln_g"])
    lnb = f32c(inputs["L0_ffn_ln_b"])
    w1 = f32c(inputs["L0_ffn_w1"])
    b1 = f32c(inputs["L0_ffn_b1"])
    w2 = f32c(inputs["L0_ffn_w2"])
    b2 = f32c(inputs["L0_ffn_b2"])
    wqkv1 = f32c(inputs["L1_attn_w_qkv"])
    bqkv1 = f32c(inputs["L1_attn_b_qkv"])
    wg1 = f32c(inputs["L1_attn_w_gate"])
    bg1 = f32c(inputs["L1_attn_b_gate"])
    wp1 = f32c(inputs["L1_attn_w_proj"])
    bp1 = f32c(inputs["L1_attn_b_proj"])
    lng1 = f32c(inputs["L1_attn_ln_g"])
    lnb1 = f32c(inputs["L1_attn_ln_b"])
    FW1 = f32c(inputs["L1_ffn_W1"])
    Fb1 = f32c(inputs["L1_ffn_b1"])
    FW2 = f32c(inputs["L1_ffn_W2"])
    Fb2 = f32c(inputs["L1_ffn_b2"])

    cores = [(c // SEQC, c % SEQC) for c in range(NCORES)]

    insA = [{"x0T": np.ascontiguousarray(x0[b, q * TOK:(q + 1) * TOK].T),
             "wq": WQ, "wk": WK, "wv": WV} for b, q in cores]
    outA = _launch("A", insA)

    kron4 = _bf16(np.kron(np.eye(4, dtype=np.float32),
                          np.ones((1, 32), np.float32)))
    kpB = {}
    vhiB = {}
    vloB = {}
    for b in range(B):
        kpB[b] = np.ascontiguousarray(np.concatenate(
            [np.asarray(outA[b * SEQC + q]["kp"]) for q in range(SEQC)], 2))
        vf = np.concatenate(
            [np.asarray(outA[b * SEQC + q]["vfull"]) for q in range(SEQC)],
            0)
        vhiB[b] = np.ascontiguousarray(vf.transpose(1, 0, 2))
    xpg = np.stack([np.concatenate(
        [np.asarray(outA[b * SEQC + q]["xp"]) for q in range(SEQC)], 1)
        for b in range(B)]).astype(np.float32)
    insB = []
    for b, q in cores:
        qp = np.asarray(outA[b * SEQC + q]["qp"])
        es = np.zeros((TR, TOK // BM), np.float32)
        for mcol in range(TOK // BM):
            es[q * (TOK // BM) + mcol, mcol] = 1.0
        insB.append({
            "x0T": np.ascontiguousarray(x0[b, q * TOK:(q + 1) * TOK].T),
            "qp2": np.ascontiguousarray(
                np.concatenate([qp[0:64], qp[0:64]], 0)),
            "qlo": np.ascontiguousarray(qp[64:128]),
            "kpB": kpB[b], "vfB": vhiB[b],
            "xpg": xpg, "wq": WQ, "wk": WK, "wo": WO,
            "kron4": kron4, "esel": _bf16(es),
        })
    outB = _launch("B", insB)

    tabs = host_rotary_tables()
    x1full = {b: np.concatenate(
        [np.asarray(outB[b * SEQC + q]["x1T"]).T for q in range(SEQC)], 0)
        for b in range(B)}
    insC = [make_C_inputs(x1full[b], None, wqkv[0], qsc[0], ksc[0], wo2[0],
                          lng[0], lnb[0], w1[0], b1[0], w2[0], b2[0], q,
                          tabs) for b, q in cores]
    outC = _launch("C", insC)

    yfull = {b: np.concatenate(
        [np.asarray(outC[b * SEQC + q]["xoT"]).T for q in range(SEQC)], 0)
        for b in range(B)}
    insD = [make_C_inputs(yfull[b], x1full[b][q * TOK:(q + 1) * TOK],
                          wqkv[1], qsc[1], ksc[1], wo2[1], lng[1], lnb[1],
                          w1[1], b1[1], w2[1], b2[1], q, tabs)
            for b, q in cores]
    outD = _launch("D", insD)

    x2core = {c: np.asarray(outD[c]["xoT"]).T for c in range(NCORES)}
    insE = [make_E_inputs(x2core[b * SEQC + q], wqkv1, bqkv1, wg1, bg1,
                          lng1, lnb1) for b, q in cores]
    outE = _launch("E", insE)

    insF = []
    for b, q in cores:
        preS = np.zeros((3, D, D), np.float32)
        preK = np.zeros((3, 128, 8), np.float32)
        for j in range(q):
            preS[j] = np.asarray(outE[b * SEQC + j]["SD"])
            preK[j] = np.asarray(outE[b * SEQC + j]["kprD"])
        insF.append(make_F_inputs(x2core[b * SEQC + q], wqkv1, bqkv1, wg1,
                                  bg1, lng1, lnb1, wp1, bp1, FW1, Fb1, FW2,
                                  Fb2, preS, preK))
    outF = _launch("F", insF)

    out = np.stack([np.concatenate(
        [np.asarray(outF[b * SEQC + q]["xoT"]).T for q in range(SEQC)], 0)
        for b in range(B)])
    return np.ascontiguousarray(out.astype(np.float32))


# version 11 (final)
# speedup vs baseline: 1.1047x; 1.0008x over previous
import math
from contextlib import ExitStack

import numpy as np

import concourse.bass as bass
import concourse.mybir as mybir
from concourse.bass_types import AP
from concourse.bass_utils import run_bass_kernel_spmd

F32 = mybir.dt.float32
BF16 = mybir.dt.bfloat16
AF = mybir.ActivationFunctionType
ALU = mybir.AluOpType
AX = mybir.AxisListType
ds = bass.ds

B = 2
N = 8192
D = 1024
H = 16
DH = 64
BM = 32
TR = N // BM
TOPK = 51
NCORES = 8
TOK = N * B // NCORES
SEQC = 4

FH, FD = 8, 64
FI = FH * FD
WIN = 512
HID = int(D * 4 * 2 // 3)
QK_SCALE = 8.0

NEG = -1.0e30
LN_EPS = 1e-5
DEN_EPS = 1e-6
QB = 127.0
BIT_EPS = 1e-5
RMAGIC = 12582912.0


def _bf16(x):
    import ml_dtypes
    return np.ascontiguousarray(np.asarray(x).astype(ml_dtypes.bfloat16))


DMA_STREAMS = {"dsync": "sync", "dgps": "gpsimd"}
ENGS = ("tensor", "vector", "scalar", "gpsimd", "sync")


class Op:
    __slots__ = ("stream", "fn", "reads", "writes", "idx", "deps",
                 "inc_needed", "is_dma", "tag", "same_raw")

    def __init__(self, stream, fn, reads, writes, idx, is_dma, tag):
        self.stream, self.fn = stream, fn
        self.reads, self.writes = reads, writes
        self.idx = idx
        self.deps = {}
        self.inc_needed = False
        self.is_dma = is_dma
        self.tag = tag
        self.same_raw = -1


class Prog:
    def __init__(self, name):
        self.nc = bass.Bass("TRN2", target_bir_lowering=False, debug=False)
        self.es = ExitStack()
        self.name = name

    def dram_in(self, name, shape, dt=F32):
        return self.nc.dram_tensor(name, list(shape), dt, kind="ExternalInput").ap()

    def dram_out(self, name, shape, dt=F32):
        return self.nc.dram_tensor(name, list(shape), dt, kind="ExternalOutput").ap()

    def sb(self, name, shape, dt=F32):
        return self.es.enter_context(self.nc.sbuf_tensor(name, list(shape), dt))

    def ps(self, name, shape):
        return self.es.enter_context(self.nc.psum_tensor(name, list(shape), F32))

    def sem(self, name):
        return self.es.enter_context(self.nc.semaphore(name))


class Sched:
    def __init__(self, prog):
        self.p = prog
        self.ops = {s: [] for s in list(ENGS) + list(DMA_STREAMS)}
        self.last_w = {}
        self.readers = {}

    @staticmethod
    def _names(lst):
        out = []
        for x in lst or ():
            if x is None:
                continue
            if isinstance(x, AP):
                try:
                    ext = x.offset + 1
                    ok = True
                    for st, cnt in x.ap:
                        if st < 0:
                            ok = False
                            break
                        ext += (cnt - 1) * st
                    out.append((x.tensor.name,
                                (x.offset, ext) if ok else None))
                except Exception:
                    out.append((x.tensor.name, None))
            else:
                out.append((x.name if hasattr(x, "name") else str(x), None))
        return out

    @staticmethod
    def _ovl(a, b):
        if a is None or b is None:
            return True
        return a[0] < b[1] and b[0] < a[1]

    @staticmethod
    def _covers(a, b):
        if a is None or b is None:
            return False
        return a[0] <= b[0] and b[1] <= a[1]

    def add(self, stream, fn, reads=(), writes=(), tag=""):
        idx = len(self.ops[stream])
        op = Op(stream, fn, self._names(reads), self._names(writes), idx,
                stream in DMA_STREAMS, tag)
        deps = {}

        def _dep(s, i, raw=False):
            if s == stream and s not in DMA_STREAMS:
                if raw:
                    op.same_raw = max(op.same_raw, i)
                return
            key = (s, i % Sched.DMA_POOL) if s in DMA_STREAMS else (s, None)
            if deps.get(key, -1) < i:
                deps[key] = i

        for b, ivl in op.reads:
            for (s2, _sl, i, wivl) in self.last_w.get(b, ()):
                if Sched._ovl(ivl, wivl):
                    _dep(s2, i, raw=True)
        for b, ivl in op.writes:
            for (s2, _sl, i, wivl) in self.last_w.get(b, ()):
                if Sched._ovl(ivl, wivl):
                    _dep(s2, i, raw=True)
            for (rs, ri, rivl) in self.readers.get(b, ()):
                if Sched._ovl(ivl, rivl):
                    _dep(rs, ri, raw=True)
        op.deps = deps
        for (s2, _slot), i in deps.items():
            self.ops[s2][i].inc_needed = True
        slot = idx % Sched.DMA_POOL if op.is_dma else None
        for b, ivl in op.writes:
            old = self.last_w.get(b, ())
            kept = []
            for e in old:
                s2, sl2, i2, ivl2 = e
                if Sched._covers(ivl, ivl2) and not (
                        s2 == stream and op.is_dma and sl2 != slot):
                    continue
                kept.append(e)
            kept.append((stream, slot, idx, ivl))
            self.last_w[b] = kept
            rd = self.readers.get(b)
            if rd:
                self.readers[b] = [r for r in rd
                                   if not Sched._covers(ivl, r[2])]
        for b, ivl in op.reads:
            self.readers.setdefault(b, []).append((stream, idx, ivl))
        self.ops[stream].append(op)
        return op

    def mm(self, psum, lhsT, rhs, start, stop, extra_reads=(), tag="mm"):
        self.add("tensor",
                 lambda t: t.matmul(psum, lhsT, rhs, start=start, stop=stop),
                 reads=[lhsT, rhs] + list(extra_reads), writes=[psum], tag=tag)

    def transpose(self, psum, in_, ident, tag="tr"):
        self.add("tensor", lambda t: t.transpose(psum, in_, ident),
                 reads=[in_, ident], writes=[psum], tag=tag)

    def dve(self, meth, out, *ins, extra_reads=(), extra_writes=(), tag="", **kw):
        self.add("vector", lambda v: getattr(v, meth)(out, *ins, **kw),
                 reads=list(ins) + list(extra_reads),
                 writes=[out] + list(extra_writes), tag=tag or meth)

    def act(self, out, in_, func, bias=0.0, scale=1.0, accum=None, tag="act"):
        extra = [b for b in (bias, scale) if isinstance(b, AP)]

        def fn(s):
            kw = dict(bias=bias, scale=scale)
            if accum is not None:
                kw["accum_out"] = accum
            return s.activation(out, in_, func, **kw)
        self.add("scalar", fn, reads=[in_] + extra,
                 writes=[out] + ([accum] if accum is not None else []), tag=tag)

    def scopy(self, out, in_, tag="scopy"):
        self.add("scalar", lambda s: s.copy(out, in_), reads=[in_],
                 writes=[out], tag=tag)

    def dma(self, out, in_, q="dsync", tag="dma", slow=False):
        if slow:
            self.add(q, lambda e: e.dma_start(
                out=out, in_=in_, allow_slow_non_contiguous=True),
                reads=[in_], writes=[out], tag=tag)
        else:
            self.add(q, lambda e: e.dma_start(out=out, in_=in_),
                     reads=[in_], writes=[out], tag=tag)

    DMA_POOL = 8

    def emit(self):
        p = self.p
        nc = p.nc
        P = self.DMA_POOL
        live = {s: lst for s, lst in self.ops.items() if lst}
        sems = {}
        for s in live:
            if s in DMA_STREAMS:
                sems[s] = [p.sem(f"S_{s}_{j}") for j in range(P)]
            else:
                sems[s] = p.sem("S_" + s)

        prefix = {}
        for s, lst in live.items():
            if s in DMA_STREAMS:
                continue
            acc, pl = 0, []
            for op in lst:
                if op.inc_needed or op is lst[-1]:
                    acc += 1
                pl.append(acc)
            prefix[s] = pl

        def dep_wait(eng, s2, i, satisfied):
            if s2 in DMA_STREAMS:
                key = (s2, i % P)
                need = 16 * (i // P + 1)
                sem = sems[s2][i % P]
            else:
                key = s2
                need = prefix[s2][i]
                assert (self.ops[s2][i].inc_needed or
                        self.ops[s2][i] is live[s2][-1])
                sem = sems[s2]
            if satisfied.get(key, 0) < need:
                eng.wait_ge(sem, need)
                satisfied[key] = need

        with nc.Block() as blk:
            for s, lst in live.items():
                eng_name = DMA_STREAMS.get(s, s)
                deco = getattr(blk, eng_name)

                def make(s=s, lst=lst, eng_name=eng_name):
                    def body(eng):
                        satisfied = {}
                        dve_like = eng_name in ("vector", "scalar", "gpsimd")
                        is_dma_stream = s in DMA_STREAMS
                        flushed = 0
                        for op in lst:
                            for key in sorted(op.deps, key=str):
                                dep_wait(eng, key[0], op.deps[key], satisfied)
                            if is_dma_stream and op.idx >= P:
                                j = op.idx % P
                                eng.wait_ge(sems[s][j], 16 * (op.idx // P))
                            if dve_like and op.same_raw >= flushed:
                                eng.drain()
                                flushed = op.idx
                            inst = op.fn(eng)
                            if is_dma_stream:
                                inst.then_inc(sems[s][op.idx % P], 16)
                            else:
                                if op.inc_needed or op is lst[-1]:
                                    inst.then_inc(sems[s], 1)
                    return body

                deco(make())

            @blk.sync
            def _(sy):
                for s, lst in sorted(live.items()):
                    n = len(lst)
                    if s in DMA_STREAMS:
                        for j in range(min(n, P)):
                            cnt = (n - 1 - j) // P + 1
                            sy.wait_ge(sems[s][j], 16 * cnt)
                    else:
                        sy.wait_ge(sems[s], prefix[s][-1])

        return nc


def fm_view(dram_ap):
    return dram_ap.rearrange("(c p) t -> p c t", p=128)


def emit_split(s, hi, lo, src):
    s.dve("tensor_copy", hi, src, tag="split_hi")
    s.dve("tensor_sub", lo, src, hi, tag="split_lo")


def emit_proj3(s, psum, whi, wlo, xhi, xlo, nk, start=True, stop=True,
               tag="proj"):
    first = start
    for c in range(nk):
        s.mm(psum, whi(c), xhi(c), first, False, tag=tag)
        first = False
        s.mm(psum, whi(c), xlo(c), False, False, tag=tag)
        s.mm(psum, wlo(c), xhi(c), False, stop and (c == nk - 1), tag=tag)


def build_A():
    p = Prog("A")
    s = Sched(p)
    x0 = p.dram_in("x0T", [D, TOK])
    w_in = [p.dram_in(n, [D, D]) for n in ("wq", "wk", "wv")]
    qp = p.dram_out("qp", [128, H, TOK], BF16)
    kp = p.dram_out("kp", [128, H, TOK], BF16)
    vfull = p.dram_out("vfull", [TOK, H, DH])
    xpo = p.dram_out("xp", [D, TOK // BM])

    NT = 512
    ntt = TOK // NT
    xsc = [p.sb(f"xsc{i}", [128, TOK]) for i in range(2)]
    xhi = p.sb("xhi", [128, 8, TOK], BF16)
    xlo = p.sb("xlo", [128, 8, TOK], BF16)
    xpt = p.sb("xpt", [128, 8, TOK // BM])
    wbuf = [p.sb(f"wbuf{i}", [128, 8, 512]) for i in range(2)]
    whi = p.sb("whi", [128, 8, 1024], BF16)
    wlo = p.sb("wlo", [128, 8, 1024], BF16)
    evq = [p.sb(f"evq{i}", [128, NT]) for i in range(2)]
    evk = [p.sb(f"evk{i}", [128, NT]) for i in range(2)]
    evv = [p.sb(f"evv{i}", [128, NT]) for i in range(2)]
    hiq = [p.sb(f"hiq{i}", [128, NT], BF16) for i in range(2)]
    loq = [p.sb(f"loq{i}", [128, NT], BF16) for i in range(2)]
    hik = [p.sb(f"hik{i}", [128, NT], BF16) for i in range(2)]
    lok = [p.sb(f"lok{i}", [128, NT], BF16) for i in range(2)]
    hiv = [p.sb(f"hiv{i}", [128, NT], BF16) for i in range(2)]
    lov = [p.sb(f"lov{i}", [128, NT], BF16) for i in range(2)]
    pq = [p.ps(f"pq{i}", [128, NT]) for i in range(2)]
    pk = [p.ps(f"pk{i}", [128, NT]) for i in range(2)]
    pv = [p.ps(f"pv{i}", [128, NT]) for i in range(2)]

    for c in range(8):
        xb = xsc[c % 2]
        s.dma(xb[:], fm_view(x0)[:, c, :], tag=f"ldx{c}")
        emit_split(s, xhi[:, c, :], xlo[:, c, :], xb[:])
        s.dve("reduce_sum", xpt[:, c, :],
              xb[:].rearrange("p (r w) -> p r w", w=BM), axis=AX.X, tag="pool")
    s.dve("tensor_scalar_mul", xpt[:], xpt[:], 1.0 / BM, tag="poolscale")
    s.dma(fm_view(xpo), xpt[:], tag="stxp")

    for wi, wsrc in enumerate(w_in):
        for hf in range(2):
            wb = wbuf[hf]
            cols = slice(hf * 512, (hf + 1) * 512)
            s.dma(wb[:], wsrc[:, cols].rearrange("(c p) o -> p c o", p=128),
                  q="dgps", tag=f"ldw{wi}_{hf}")
            emit_split(s, whi[:, :, cols], wlo[:, :, cols], wb[:])

        if wi < 2:
            out_d = (qp, kp)
            ev = (evq, evk)[wi]
            hi_t = (hiq, hik)[wi]
            lo_t = (loq, lok)[wi]
            pp = (pq, pk)[wi]
            for oc in range(8):
                for tt in range(ntt):
                    bi = (oc * ntt + tt) % 2
                    sl = slice(tt * NT, (tt + 1) * NT)
                    ocs = slice(oc * 128, (oc + 1) * 128)
                    emit_proj3(
                        s, pp[bi][:],
                        lambda c: whi[:, c, ocs], lambda c: wlo[:, c, ocs],
                        lambda c: xhi[:, c, sl], lambda c: xlo[:, c, sl], 8,
                        tag=f"proj{wi}")
                    s.scopy(ev[bi][:], pp[bi][:], tag="evac")
                    emit_split(s, hi_t[bi][:], lo_t[bi][:], ev[bi][:])
                    for hh in range(2):
                        h_ = 2 * oc + hh
                        rows = slice(hh * 64, (hh + 1) * 64)
                        s.dma(out_d[wi][0:64, h_, sl], hi_t[bi][rows, :],
                              q="dgps", tag="stqk_hi")
                        s.dma(out_d[wi][64:128, h_, sl], lo_t[bi][rows, :],
                              q="dgps", tag="stqk_lo")
        else:
            for tt in range(TOK // 128):
                for half in range(2):
                    bi = (tt * 2 + half) % 2
                    ocs = slice(half * 512, (half + 1) * 512)
                    tsl = slice(tt * 128, (tt + 1) * 128)
                    first = True
                    for c in range(8):
                        s.mm(pv[bi][:], xhi[:, c, tsl], whi[:, c, ocs],
                             first, False, tag="projV")
                        first = False
                        s.mm(pv[bi][:], xhi[:, c, tsl], wlo[:, c, ocs],
                             False, False, tag="projV")
                        s.mm(pv[bi][:], xlo[:, c, tsl], whi[:, c, ocs],
                             False, c == 7, tag="projV")
                    s.scopy(evv[bi][:], pv[bi][:], tag="evacV")
                    h0 = half * 8
                    s.dma(vfull[tsl, h0:h0 + 8, :],
                          evv[bi][:].rearrange("t (h d) -> t h d", d=DH),
                          q="dgps", tag="stv")

    nc = s.emit()
    p.es.close()
    return nc


XT = 3584
MARG = 1536
KV0 = 512
Q0 = 1024
NQW = 5
HID_P = 2816
NHC = HID_P // 128


def build_C(block_ln, dbg=False):
    p = Prog("C")
    s = Sched(p)
    nc = p.nc
    xE = p.dram_in("xET", [D, XT])
    x1 = p.dram_in("x1T", [D, TOK])
    wqkv = p.dram_in("wqkv", [D, 3 * FI])
    wo = p.dram_in("wo", [FI, D])
    w1p = p.dram_in("w1p", [D, 2, HID_P])
    w2p = p.dram_in("w2p", [HID_P, D])
    qs128 = p.dram_in("qs128", [128, 1])
    ks128 = p.dram_in("ks128", [128, 1])
    lg128 = p.dram_in("lg128", [128, 8])
    lb128 = p.dram_in("lb128", [128, 8])
    b1a128 = p.dram_in("b1a128", [128, NHC])
    b1g128 = p.dram_in("b1g128", [128, NHC])
    b2c = p.dram_in("b2c", [128, 8])
    tabs = {n: p.dram_in(n, [128, XT]) for n in ("csq", "snq", "csk", "snk")}
    edgeneg = p.dram_in("edgeneg", [1, 1])
    edgezero = p.dram_in("edgezero", [128, 1])
    trimask = p.dram_in("trimask", [128, 128], BF16)
    expand2 = p.dram_in("expand2", [2, 128])
    ones2h = p.dram_in("ones2h", [128, 2])
    rotk = p.dram_in("rotk", [128, 128])
    idb = p.dram_in("idb", [128, 512], BF16)
    zid = p.dram_in("zid", [128, 640], BF16)
    tri2m = p.dram_in("tri2m", [128, 128], BF16)
    xo = p.dram_out("xoT", [D, TOK])

    q2D = nc.dram_tensor("q2D", [FH, 128, XT - Q0], BF16, kind="Internal").ap()
    qlD = nc.dram_tensor("qlD", [FH, 64, XT - Q0], BF16, kind="Internal").ap()
    kD = nc.dram_tensor("kD", [FH, 128, XT - KV0], BF16, kind="Internal").ap()
    vD = nc.dram_tensor("vD", [FH, 2, XT - KV0, FD], BF16, kind="Internal").ap()
    attD = nc.dram_tensor("attD", [128, FI // 128, 2, XT - Q0], BF16,
                          kind="Internal").ap()
    xmD = nc.dram_tensor("xmD", [D, TOK + 1], F32, kind="Internal").ap()
    gluD = nc.dram_tensor("gluD", [NHC, 2, 128, TOK], BF16,
                          kind="Internal").ap()
    lnhD = nc.dram_tensor("lnhD", [2, D, TOK], BF16, kind="Internal").ap()

    xsp = [p.sb(f"xsp{i}", [128, 513]) for i in range(2)]
    hhi = [p.sb(f"hhi{i}", [128, 8, 512], BF16) for i in range(2)]
    hlo = [p.sb(f"hlo{i}", [128, 8, 512], BF16) for i in range(2)]
    wfp = p.sb("wfp", [128, 8, 512])
    whi = p.sb("whi", [128, 8, 512], BF16)
    wlo = p.sb("wlo", [128, 8, 512], BF16)
    qs_s = p.sb("qs_s", [128, 1])
    ks_s = p.sb("ks_s", [128, 1])
    lg_s = p.sb("lg_s", [128, 8])
    lb_s = p.sb("lb_s", [128, 8])
    b1a_s = p.sb("b1a_s", [128, NHC])
    b1g_s = p.sb("b1g_s", [128, NHC])
    b2_s = p.sb("b2_s", [128, 8])
    tri_s = p.sb("tri_s", [128, 128], BF16)
    ex2_s = p.sb("ex2_s", [2, 128])
    on2_s = p.sb("on2_s", [128, 2])
    rot_s = p.sb("rot_s", [128, 128])
    idb_s = p.sb("idb_s", [128, 512], BF16)
    zid_s = p.sb("zid_s", [128, 640], BF16)
    tri2_s = p.sb("tri2_s", [128, 128], BF16)
    en_s = p.sb("en_s", [1, 1])
    ez_s = p.sb("ez_s", [128, 1])
    tab_s = {n: [p.sb(f"{n}{i}", [128, 512]) for i in range(2)]
             for n in ("csq", "snq", "csk", "snk")}
    ones_c = p.sb("ones_c", [128, 1])
    ones_r = p.sb("ones_r", [1, 128])
    ev = [p.sb(f"ev{i}", [128, 512]) for i in range(2)]
    ev2 = [p.sb(f"ev2{i}", [128, 512]) for i in range(2)]
    evh = [p.sb(f"evh{i}", [128, 512], BF16) for i in range(2)]
    evl = [p.sb(f"evl{i}", [128, 512], BF16) for i in range(2)]
    sqv = p.sb("sqv", [128, 512])
    n2 = p.sb("n2", [2, 512])
    kpx = [p.sb(f"kpx{i}", [128, XT - KV0], BF16) for i in range(2)]
    vstg = p.sb("vstg", [128, (XT - KV0) // 128, 2, 65], BF16)
    q2t = p.sb("q2t", [128, 512], BF16)
    qlt = p.sb("qlt", [64, 512], BF16)
    pex = [p.sb(f"pex{i}", [128, 512]) for i in range(2)]
    phh = [p.sb(f"phh{i}", [128, 512], BF16) for i in range(2)]
    pll = [p.sb(f"pll{i}", [128, 512], BF16) for i in range(2)]
    rden = p.sb("rden", [1, 512])
    rdex = p.sb("rdex", [64, 512])
    ahat = p.sb("ahat", [64, 512])
    ah_h = p.sb("ah_h", [64, 512], BF16)
    ah_l = p.sb("ah_l", [64, 512], BF16)
    ahld = [p.sb(f"ahld{i}", [128, 2, 512], BF16) for i in range(2)]
    lnhh = [p.sb(f"lnhh{i}", [128, 8, 512], BF16) for i in range(2)]
    lnhl = [p.sb(f"lnhl{i}", [128, 8, 512], BF16) for i in range(2)]
    stats = p.sb("stats", [1, 2, TOK])
    glh = p.sb("glh", [128, 512], BF16)
    gll = p.sb("gll", [128, 512], BF16)
    gel = p.sb("gel", [128, 512])
    asb = p.sb("asb", [128, 512])

    ps1024 = [p.ps(f"ps1024_{i}", [128, 1024]) for i in range(2)]
    ps512 = [p.ps(f"ps512_{i}", [128, 512]) for i in range(4)]

    for src_, dst in ((qs128, qs_s), (ks128, ks_s), (lg128, lg_s),
                      (lb128, lb_s), (b1a128, b1a_s), (b1g128, b1g_s),
                      (b2c, b2_s), (trimask, tri_s), (expand2, ex2_s),
                      (ones2h, on2_s), (edgeneg, en_s), (edgezero, ez_s),
                      (rotk, rot_s), (idb, idb_s), (zid, zid_s),
                      (tri2m, tri2_s)):
        s.dma(dst[:], src_[tuple([slice(None)] * len(src_.shape))], tag="ldc")
    s.dve("memset", ones_c[:], 1.0, tag="onesc")
    s.dve("memset", ones_r[:], 1.0, tag="onesr")
    s.dve("memset", vstg[:, :, 0, 64:65], 1.0, tag="vst1")
    s.dve("memset", vstg[:, :, 1, 64:65], 0.0, tag="vst0")

    NTT = (XT - KV0) // 512

    def load_x_tile(tt, tb):
        t0 = KV0 + tt * 512
        for c in range(8):
            s.dma(xsp[tb][:, 0:513],
                  xE.rearrange("(c p) t -> p c t", p=128)[:, c,
                                                          t0 - 1:t0 + 512],
                  tag="ldxc")
            off = 1 if c < 4 else 0
            emit_split(s, hhi[tb][:, c, :], hlo[tb][:, c, :],
                       xsp[tb][:, off:off + 512])

    p1iters = []
    for kind_ in range(3):
        for tt_ in range(NTT):
            if kind_ == 0 and KV0 + tt_ * 512 < Q0:
                continue
            p1iters.append((kind_, tt_))
    cur_kind = [-1]
    for it_i, (kind, tt) in enumerate(p1iters):
        t0 = KV0 + tt * 512
        tb = tt % 2
        if kind != cur_kind[0]:
            cols = slice(kind * 512, (kind + 1) * 512)
            s.dma(wfp[:], wqkv[:, cols].rearrange("(c p) o -> p c o", p=128),
                  q="dgps", tag=f"ldwqkv{kind}")
            emit_split(s, whi[:], wlo[:], wfp[:])
            cur_kind[0] = kind
        if it_i == 0:
            load_x_tile(tt, tb)
        if it_i + 1 < len(p1iters):
            ntt_ = p1iters[it_i + 1][1]
            load_x_tile(ntt_, ntt_ % 2)
        if True:
            if kind < 2:
                for n_ in (("csq", "snq") if kind == 0 else ("csk", "snk")):
                    s.dma(tab_s[n_][tb][:], tabs[n_][:, t0:t0 + 512],
                          q="dgps", tag="ldtab")
                def qk_chain(oc, kind, tb, t0):
                    pp = ps512[oc % 2][:, 0:512]
                    eb = ev[oc % 2]
                    sc_ap = (qs_s if kind == 0 else ks_s)[:]
                    s.act(eb[:], pp, AF.Square, tag="sq")
                    pn = ps1024[0][0:2, 0:512]
                    s.mm(pn, on2_s[:], eb[:], True, True, tag="n2mm")
                    s.act(n2[:], pn, AF.Sqrt, tag="nsqrt")
                    s.dve("tensor_scalar_max", n2[:], n2[:], 1e-12,
                          tag="nclip")
                    s.dve("reciprocal", n2[:], n2[:], tag="nrec")
                    pn2 = ps1024[0][0:128, 0:512]
                    s.mm(pn2, ex2_s[:], n2[:], True, True, tag="nexp")
                    s.dve("tensor_scalar_mul", eb[:], pp, sc_ap,
                          tag="qscale")
                    s.dve("tensor_mul", eb[:], eb[:], pn2, tag="qnorm")
                    prot = ps512[2 + oc % 2][:, 0:512]
                    s.mm(prot, rot_s[:], eb[:], True, True, tag="rotmm")
                    cs_t = tab_s["csq" if kind == 0 else "csk"][tb]
                    sn_t = tab_s["snq" if kind == 0 else "snk"][tb]
                    e2 = ev2[oc % 2]
                    s.dve("tensor_mul", e2[:], eb[:], cs_t[:], tag="qcs")
                    s.dve("tensor_mul", eb[:], prot, sn_t[:], tag="qsn")
                    s.dve("tensor_add", e2[:], e2[:], eb[:], tag="qfin")
                    emit_split(s, evh[oc % 2][:], evl[oc % 2][:], e2[:])
                    for hh_ in range(2):
                        h_ = 2 * oc + hh_
                        rows = slice(hh_ * 64, (hh_ + 1) * 64)
                        if kind == 0:
                            dsl = slice(t0 - Q0, t0 - Q0 + 512)
                            s.dma(q2D[h_, 0:64, dsl], evh[oc % 2][rows, :],
                                  q="dgps", tag="stq2a")
                            s.dma(q2D[h_, 64:128, dsl], evh[oc % 2][rows, :],
                                  q="dgps", tag="stq2b")
                            s.dma(qlD[h_, :, dsl], evl[oc % 2][rows, :],
                                  q="dgps", tag="stql")
                        else:
                            dsl = slice(t0 - KV0, t0 - KV0 + 512)
                            s.dma(kD[h_, 0:64, dsl], evh[oc % 2][rows, :],
                                  q="dgps", tag="stk_h")
                            s.dma(kD[h_, 64:128, dsl], evl[oc % 2][rows, :],
                                  q="dgps", tag="stk_l")

                pchain = [None]
                for oc in range(4):
                    pp = ps512[oc % 2][:, 0:512]
                    ocs = slice(oc * 128, (oc + 1) * 128)
                    emit_proj3(s, pp,
                               lambda c: whi[:, c, ocs],
                               lambda c: wlo[:, c, ocs],
                               lambda c: hhi[tb][:, c, :],
                               lambda c: hlo[tb][:, c, :], 8, tag="qkproj")
                    if pchain[0] is not None:
                        qk_chain(*pchain[0])
                    pchain[0] = (oc, kind, tb, t0)
                qk_chain(*pchain[0])
            else:
                for sub in range(4):
                    pp = ps512[sub % 2][:, 0:512]
                    tsl2 = slice(sub * 128, (sub + 1) * 128)
                    first = True
                    for c in range(8):
                        s.mm(pp, hhi[tb][:, c, tsl2], whi[:, c, :],
                             first, False, tag="vproj")
                        first = False
                        s.mm(pp, hhi[tb][:, c, tsl2], wlo[:, c, :],
                             False, False, tag="vproj")
                        s.mm(pp, hlo[tb][:, c, tsl2], whi[:, c, :],
                             False, c == 7, tag="vproj")
                    s.scopy(ev[sub % 2][:], pp, tag="vev")
                    emit_split(s, evh[sub % 2][:], evl[sub % 2][:],
                               ev[sub % 2][:])
                    vt0 = t0 - KV0 + sub * 128
                    for h_ in range(FH):
                        hc_ = slice(h_ * FD, (h_ + 1) * FD)
                        s.dma(vD[h_, 0, vt0:vt0 + 128, :],
                              evh[sub % 2][:, hc_], q="dgps", tag="stv_h")
                        s.dma(vD[h_, 1, vt0:vt0 + 128, :],
                              evl[sub % 2][:, hc_], q="dgps", tag="stv_l")

    if dbg:
        for nm, t_, sh in (("dbg_q2", q2D, [FH, 128, XT - Q0]),
                           ("dbg_k", kD, [FH, 128, XT - KV0]),
                           ("dbg_v", vD, [FH, 2, XT - KV0, FD])):
            o_ = p.dram_out(nm, sh, BF16)
            s.dma(o_[tuple(slice(None) for _ in sh)],
                  t_[tuple(slice(None) for _ in sh)], q="dgps", tag="dbg")
    NVC = (XT - KV0) // 128
    for h in range(FH):
        hb = h % 2
        s.dma(kpx[hb][:], kD[h, :, :], tag="ldkp")
        s.dma(vstg[:, :, 0, 0:64],
              vD[h, 0, :, :].rearrange("(kc pp) d -> pp kc d", pp=128),
              tag="ldvh")
        s.dma(vstg[:, :, 1, 0:64],
              vD[h, 1, :, :].rearrange("(kc pp) d -> pp kc d", pp=128),
              tag="ldvl")
        for w in range(2, 7):
            qb0 = w * 512
            qdsl = slice(qb0 - Q0, qb0 - Q0 + 512)
            s.dma(q2t[:], q2D[h, :, qdsl], q="dgps", tag="ldq2")
            s.dma(qlt[:], qlD[h, :, qdsl], q="dgps", tag="ldql2")
            pvp = ps512[0][0:65, 0:512]
            kb0 = (w - 1) * 512 - KV0
            order = [3, 2, 1, 0, 4, 5, 6, 7]
            prevs = []
            for oi, kci in enumerate(order):
                kb = oi % 2
                stp = ps1024[kb][:, 0:512]
                if kci < 4:
                    vld = 0
                    vend = kci * 128 + 128
                    z0 = 512 - kci * 128
                    s.mm(stp[:, 0:vend], tri2_s[:], zid_s[:, z0:z0 + vend],
                         True, False, tag="tri2mm")
                    if w == 3:
                        s.mm(stp[:, 0:vend], ones_r[:],
                             AP(en_s[:].tensor, 0, [[1, 1], [0, vend]]),
                             False, False, extra_reads=[en_s[:]],
                             tag="edgemm")
                else:
                    vld = (kci - 4) * 128
                    vend = 512
                    s.mm(stp[:, vld:512], tri_s[:], idb_s[:, 0:512 - vld],
                         True, False, tag="trimm")
                vsl = slice(vld, vend)
                ksl = slice(kb0 + kci * 128, kb0 + kci * 128 + 128)
                s.mm(stp[:, vsl], kpx[hb][:, ksl], q2t[:, vsl],
                     False, False, tag="fqkA")
                s.mm(stp[:, vsl], kpx[hb][0:64, ksl], qlt[:, vsl],
                     False, True, tag="fqkB")
                s.act(pex[kb][:, vsl], stp[:, vsl], AF.Exp, scale=QK_SCALE,
                      tag="fexp")
                s.act(phh[kb][:, vsl], stp[:, vsl], AF.Exp, scale=QK_SCALE,
                      tag="fexph")
                s.dve("tensor_sub", pll[kb][:, vsl], pex[kb][:, vsl],
                      phh[kb][:, vsl], tag="fplo")
                kcv = kb0 // 128 + kci
                prevs.append((oi, kb, vsl, kcv))
                emits = []
                if oi > 0:
                    emits.append(prevs.pop(0))
                if oi == 7:
                    emits.append(prevs.pop(0))
                for poi, pb_, pvsl, pkcv in emits:
                    s.mm(pvp[:, pvsl], vstg[:, pkcv, 0, :],
                         phh[pb_][:, pvsl], poi == 0, False, tag="fpvA")
                    s.mm(pvp[:, pvsl], vstg[:, pkcv, 0, :],
                         pll[pb_][:, pvsl], False, False, tag="fpvB")
                    s.mm(pvp[:, pvsl], vstg[:, pkcv, 1, :],
                         phh[pb_][:, pvsl], False, poi == 7, tag="fpvC")
            s.dve("reciprocal", rden[:], pvp[64:65, :], tag="frecip")
            pr = ps512[2][0:64, 0:512]
            s.mm(pr, ones_r[0:1, 0:64], rden[:], True, True, tag="frdex")
            s.scopy(rdex[:], pr, tag="frdev")
            s.dve("tensor_mul", ahat[:], pvp[0:64, :], rdex[:], tag="fahat")
            emit_split(s, ah_h[:], ah_l[:], ahat[:])
            hp = h // 2
            r0 = (h % 2) * 64
            s.dma(attD[r0:r0 + 64, hp, 0, qdsl], ah_h[:], q="dgps", tag="fsah")
            s.dma(attD[r0:r0 + 64, hp, 1, qdsl], ah_l[:], q="dgps", tag="fsal")

    if dbg:
        o_ = p.dram_out("dbg_att", [128, FI // 128, 2, XT - Q0], BF16)
        s.dma(o_[:, :, :, :], attD[:, :, :, :], q="dgps", tag="dbga")
    whf = whi[:].rearrange("p a b -> p (a b)")
    wlf = wlo[:].rearrange("p a b -> p (a b)")

    def wo_hi(ic, oc):
        return whf[:, ic * 1024 + oc * 128:ic * 1024 + (oc + 1) * 128]

    def wo_lo(ic, oc):
        return wlf[:, ic * 1024 + oc * 128:ic * 1024 + (oc + 1) * 128]

    wfp_f = wfp[:].rearrange("p a b -> p (a b)")
    for ic in range(4):
        s.dma(wfp_f[:, 0:1024], wo[ic * 128:(ic + 1) * 128, :], q="dgps",
              tag="ldwo")
        emit_split(s, whf[:, ic * 1024:(ic + 1) * 1024],
                   wlf[:, ic * 1024:(ic + 1) * 1024], wfp_f[:, 0:1024])

    def emit_wo(c0, length, tb):
        qsl = slice(c0 - Q0, c0 - Q0 + length)
        for ic in range(4):
            ab_ = ahld[ic % 2]
            s.dma(ab_[:, :, 0:length], attD[:, ic, :, qsl], tag="ldatt",
                  slow=(length == 1))
            for oc in range(8):
                pb = (ps512[oc][:, 0:length] if oc < 4 else
                      ps1024[(oc - 4) // 2][:, ((oc - 4) % 2) * 512:
                                            ((oc - 4) % 2) * 512 + length])
                s.mm(pb, wo_hi(ic, oc), ab_[:, 0, 0:length], ic == 0, False,
                     tag="woA")
                s.mm(pb, wo_hi(ic, oc), ab_[:, 1, 0:length], False, False,
                     tag="woB")
                s.mm(pb, wo_lo(ic, oc), ab_[:, 0, 0:length], False, ic == 3,
                     tag="woC")
        for oc in range(8):
            pb = (ps512[oc][:, 0:length] if oc < 4 else
                  ps1024[(oc - 4) // 2][:, ((oc - 4) % 2) * 512:
                                        ((oc - 4) % 2) * 512 + length])
            s.dma(ev[tb][:, 0:length],
                  xE.rearrange("(c p) t -> p c t", p=128)[:, oc,
                                                          c0:c0 + length],
                  tag="ldxr", slow=(length == 1))
            s.dve("tensor_add", ev2[tb][:, 0:length], pb, ev[tb][:, 0:length],
                  tag="xmadd")
            if c0 == 1535:
                s.dve("tensor_scalar_mul", ev2[tb][:, 0:length],
                      ev2[tb][:, 0:length], ez_s[:], tag="edgez")
            s.dma(xmD.rearrange("(c p) t -> p c t", p=128)[
                :, oc, c0 - 1535:c0 - 1535 + length], ev2[tb][:, 0:length],
                q="dgps", tag="stxm", slow=(length == 1))

    emit_wo(1535, 1, 0)
    for tt in range(4):
        emit_wo(1536 + tt * 512, 512, tt % 2)

    if dbg:
        o_ = p.dram_out("dbg_xm", [D, TOK + 1])
        s.dma(o_[:, :], xmD[:, :], q="dgps", tag="dbgxm")
    for tt in range(4):
        tb = tt % 2
        tsl = slice(tt * 512, (tt + 1) * 512)
        psum_s = ps512[2][0:1, 0:512]
        psum_q = ps512[2][32:33, 0:512]
        for oc in range(8):
            off = 1 if oc < 4 else 0
            s.dma(ev[tb][:], xmD.rearrange("(c p) t -> p c t", p=128)[
                :, oc, tt * 512 + off:tt * 512 + off + 512], tag="ldh2")
            s.act(sqv[:], ev[tb][:], AF.Square, tag="h2sq")
            s.mm(psum_s, ones_c[:], ev[tb][:], oc == 0, oc == 7, tag="h2sum")
            s.mm(psum_q, ones_c[:], sqv[:], oc == 0, oc == 7, tag="h2sq2")
        s.scopy(stats[:, 0, tsl], psum_s, tag="h2ev_s")
        s.scopy(stats[:, 1, tsl], psum_q, tag="h2ev_q")
    s.act(stats[:, 0, :], stats[:, 0, :], AF.Copy, scale=1.0 / D, tag="fmu")
    s.act(stats[:, 1, :], stats[:, 1, :], AF.Copy, scale=1.0 / D, tag="fexx")
    for hf in range(4):
        hsl = slice(hf * 512, (hf + 1) * 512)
        s.dve("tensor_mul", rden[:], stats[:, 0, hsl], stats[:, 0, hsl],
              tag="fmusq")
        s.dve("tensor_sub", stats[:, 1, hsl], stats[:, 1, hsl], rden[:],
              tag="fvar")
        s.dve("tensor_scalar_add", stats[:, 1, hsl], stats[:, 1, hsl],
              LN_EPS, tag="fvare")
        s.dve("reciprocal", rden[:], stats[:, 1, hsl], tag="fvrec")
        s.act(stats[:, 1, hsl], rden[:], AF.Sqrt, tag="frsq")
        s.dve("tensor_mul", stats[:, 0, hsl], stats[:, 0, hsl],
              stats[:, 1, hsl], tag="fmurs")
        s.dve("tensor_scalar_mul", stats[:, 0, hsl], stats[:, 0, hsl], -1.0,
              tag="fnegm")
    for tt in range(4):
        tb = tt % 2
        tsl = slice(tt * 512, (tt + 1) * 512)
        pa = ps512[0][:, 0:512]
        pbv = ps512[1][:, 0:512]
        s.mm(pa, ones_r[:], stats[:, 1, tsl], True, True, tag="faexp")
        s.mm(pbv, ones_r[:], stats[:, 0, tsl], True, True, tag="fbexp")
        for oc in range(8):
            off = 1 if oc < 4 else 0
            s.dma(ev[tb][:], xmD.rearrange("(c p) t -> p c t", p=128)[
                :, oc, tt * 512 + off:tt * 512 + off + 512], tag="ldh2b")
            s.dve("tensor_mul", ev2[tb][:], ev[tb][:], pa, tag="flnm")
            s.dve("tensor_add", ev2[tb][:], ev2[tb][:], pbv, tag="flna")
            s.dve("tensor_scalar", ev2[tb][:], ev2[tb][:], lg_s[:, oc:oc + 1],
                  scalar2=lb_s[:, oc:oc + 1], op0=ALU.mult, op1=ALU.add,
                  tag="flngb")
            emit_split(s, evh[tb][:], evl[tb][:], ev2[tb][:])
            s.dma(lnhD[0, oc * 128:(oc + 1) * 128, tsl], evh[tb][:],
                  q="dgps", tag="stlnh")
            s.dma(lnhD[1, oc * 128:(oc + 1) * 128, tsl], evl[tb][:],
                  q="dgps", tag="stlnl")

    if dbg:
        o_ = p.dram_out("dbg_lnh", [2, D, TOK], BF16)
        s.dma(o_[:, :, :], lnhD[:, :, :], q="dgps", tag="dbgln")
    def ld_lnh(tt_):
        tb_ = tt_ % 2
        tsl_ = slice(tt_ * 512, (tt_ + 1) * 512)
        s.dma(lnhh[tb_][:],
              lnhD[0, :, tsl_].rearrange("(c p) t -> p c t", p=128),
              tag="ldlh")
        s.dma(lnhl[tb_][:],
              lnhD[1, :, tsl_].rearrange("(c p) t -> p c t", p=128),
              tag="ldll")

    it4b = [(hc_, tt_) for hc_ in range(NHC) for tt_ in range(4)]
    for it_i, (hc, tt) in enumerate(it4b):
        if tt == 0:
            wfa = wfp_f[:, 0:1024].rearrange("p (c o) -> p c o", o=128)
            wfg = wfp_f[:, 1024:2048].rearrange("p (c o) -> p c o", o=128)
            s.dma(wfa, w1p[:, 0, hc * 128:(hc + 1) * 128].rearrange(
                "(c p) o -> p c o", p=128), q="dgps", tag="ldw1a")
            s.dma(wfg, w1p[:, 1, hc * 128:(hc + 1) * 128].rearrange(
                "(c p) o -> p c o", p=128), q="dgps", tag="ldw1g")
            wha = whf[:, 0:1024].rearrange("p (c o) -> p c o", o=128)
            wla = wlf[:, 0:1024].rearrange("p (c o) -> p c o", o=128)
            whg = whf[:, 1024:2048].rearrange("p (c o) -> p c o", o=128)
            wlg = wlf[:, 1024:2048].rearrange("p (c o) -> p c o", o=128)
            emit_split(s, wha, wla, wfa)
            emit_split(s, whg, wlg, wfg)
        tb = tt % 2
        tsl = slice(tt * 512, (tt + 1) * 512)
        if it_i == 0:
            ld_lnh(0)
        if it_i + 1 < len(it4b):
            ld_lnh(it4b[it_i + 1][1])
        if True:
            pa_ = ps512[tb][:, 0:512]
            pg_ = ps512[2 + tb][:, 0:512]
            emit_proj3(s, pa_, lambda c: wha[:, c, :], lambda c: wla[:, c, :],
                       lambda c: lnhh[tb][:, c, :],
                       lambda c: lnhl[tb][:, c, :], 8, tag="w1a")
            emit_proj3(s, pg_, lambda c: whg[:, c, :], lambda c: wlg[:, c, :],
                       lambda c: lnhh[tb][:, c, :],
                       lambda c: lnhl[tb][:, c, :], 8, tag="w1g")
            s.act(gel[:], pg_, AF.Gelu, bias=b1g_s[:, hc:hc + 1], tag="gelu")
            s.dve("tensor_scalar_add", asb[:], pa_, b1a_s[:, hc:hc + 1],
                  tag="aev")
            s.dve("tensor_mul", asb[:], asb[:], gel[:], tag="glu")
            emit_split(s, glh[:], gll[:], asb[:])
            s.dma(gluD[hc, 0, :, tsl], glh[:], q="dgps", tag="stgh")
            s.dma(gluD[hc, 1, :, tsl], gll[:], q="dgps", tag="stgl")

    gl2 = [p.sb(f"gl2_{i}", [128, 2, 512], BF16) for i in range(2)]
    for tt in range(4):
        tb = tt % 2
        tsl = slice(tt * 512, (tt + 1) * 512)
        for hc in range(NHC):
            w2h = whf[:, 2048:3072]
            w2l = wlf[:, 2048:3072]
            s.dma(wfp_f[:, 0:1024], w2p[hc * 128:(hc + 1) * 128, :],
                  q="dgps", tag="ldw2")
            emit_split(s, w2h, w2l, wfp_f[:, 0:1024])
            gb = gl2[hc % 2]
            s.dma(gb[:], gluD[hc].rearrange("hl p t -> p hl t")[:, :, tsl],
                  tag="ldglu")
            for oc in range(8):
                pb = (ps512[oc][:, 0:512] if oc < 4 else
                      ps1024[(oc - 4) // 2][:, ((oc - 4) % 2) * 512:
                                            ((oc - 4) % 2) * 512 + 512])
                ocs = slice(oc * 128, (oc + 1) * 128)
                s.mm(pb, w2h[:, ocs], gb[:, 0, :], hc == 0, False, tag="w2A")
                s.mm(pb, w2h[:, ocs], gb[:, 1, :], False, False, tag="w2B")
                s.mm(pb, w2l[:, ocs], gb[:, 0, :], False, hc == NHC - 1,
                     tag="w2C")
        for oc in range(8):
            pb = (ps512[oc][:, 0:512] if oc < 4 else
                  ps1024[(oc - 4) // 2][:, ((oc - 4) % 2) * 512:
                                        ((oc - 4) % 2) * 512 + 512])
            s.dve("tensor_scalar_add", ev[oc % 2][:], pb,
                  b2_s[:, oc:oc + 1], tag="b2ev")
            s.dma(ev2[oc % 2][:], xmD.rearrange("(c p) t -> p c t", p=128)[
                :, oc, tt * 512 + 1:tt * 512 + 513], tag="ldxm2")
            s.dve("tensor_add", ev[oc % 2][:], ev[oc % 2][:], ev2[oc % 2][:],
                  tag="ffnres")
            if block_ln:
                s.dma(ev2[oc % 2][:],
                      fm_view(x1)[:, oc, tsl], tag="ldx1r")
                s.dve("tensor_add", ev[oc % 2][:], ev[oc % 2][:],
                      ev2[oc % 2][:], tag="blkres")
                s.act(sqv[:], ev[oc % 2][:], AF.Square, tag="bsq")
                s.mm(ps512[0][0:1, 0:512], ones_c[:], ev[oc % 2][:],
                     oc == 0, oc == 7, tag="bsum")
                s.mm(ps512[0][32:33, 0:512], ones_c[:], sqv[:],
                     oc == 0, oc == 7, tag="bsq2")
                s.dma(fm_view(xo)[:, oc, tsl], ev[oc % 2][:], q="dgps",
                      tag="stpre")
            else:
                s.dma(fm_view(xo)[:, oc, tsl], ev[oc % 2][:], q="dgps",
                      tag="stxo")
        if block_ln:
            s.scopy(stats[:, 0, tsl], ps512[0][0:1, 0:512], tag="bev_s")
            s.scopy(stats[:, 1, tsl], ps512[0][32:33, 0:512], tag="bev_q")

    if block_ln:
        s.act(stats[:, 0, :], stats[:, 0, :], AF.Copy, scale=1.0 / D,
              tag="bmu")
        s.act(stats[:, 1, :], stats[:, 1, :], AF.Copy, scale=1.0 / D,
              tag="bexx")
        for hf in range(4):
            hsl = slice(hf * 512, (hf + 1) * 512)
            s.dve("tensor_mul", rden[:], stats[:, 0, hsl], stats[:, 0, hsl],
                  tag="bmusq")
            s.dve("tensor_sub", stats[:, 1, hsl], stats[:, 1, hsl], rden[:],
                  tag="bvar")
            s.dve("tensor_scalar_add", stats[:, 1, hsl], stats[:, 1, hsl],
                  LN_EPS, tag="bvare")
            s.dve("reciprocal", rden[:], stats[:, 1, hsl], tag="bvrec")
            s.act(stats[:, 1, hsl], rden[:], AF.Sqrt, tag="brsq")
            s.dve("tensor_mul", stats[:, 0, hsl], stats[:, 0, hsl],
                  stats[:, 1, hsl], tag="bmurs")
            s.dve("tensor_scalar_mul", stats[:, 0, hsl], stats[:, 0, hsl],
                  -1.0, tag="bnegm")
        for tt in range(4):
            tb = tt % 2
            tsl = slice(tt * 512, (tt + 1) * 512)
            pa = ps512[0][:, 0:512]
            pbv = ps512[1][:, 0:512]
            s.mm(pa, ones_r[:], stats[:, 1, tsl], True, True, tag="baexp")
            s.mm(pbv, ones_r[:], stats[:, 0, tsl], True, True, tag="bbexp")
            for oc in range(8):
                s.dma(ev[tb][:], fm_view(xo)[:, oc, tsl], tag="ldpre2")
                s.dve("tensor_mul", ev2[tb][:], ev[tb][:], pa, tag="blnm")
                s.dve("tensor_add", ev2[tb][:], ev2[tb][:], pbv, tag="blna")
                s.dma(fm_view(xo)[:, oc, tsl], ev2[tb][:], q="dgps",
                      tag="stxo2")

    nc_ = s.emit()
    p.es.close()
    return nc_


def _np_split_pair(w):
    import ml_dtypes
    hi = _bf16(w)
    lo = _bf16(np.asarray(w, np.float32) - np.asarray(hi, np.float32))
    return hi, lo


def host_rotary_tables():
    half = np.arange(0, FD, 2, dtype=np.float32)
    inv = (1.0 / 10000.0 ** (half / FD)).astype(np.float32)
    t = np.arange(-MARG, N, dtype=np.float32)
    fr = t[:, None] * inv[None, :]
    fr = np.concatenate([fr, fr], -1).astype(np.float32)
    base = ((half + 0.4 * FD) / (1.4 * FD)).astype(np.float32)
    power = ((t - N // 2) / float(WIN // 2)).astype(np.float32)
    sc = (base[None, :] ** power[:, None]).astype(np.float32)
    sc = np.concatenate([sc, sc], -1).astype(np.float32)
    cos, sin = np.cos(fr).astype(np.float32), np.sin(fr).astype(np.float32)
    csq = (cos * sc).T
    snq = (sin * sc).T
    csk = (cos / sc).T
    snk = (sin / sc).T
    dup = lambda a: np.ascontiguousarray(np.concatenate([a, a], 0))
    return dup(csq), dup(snq), dup(csk), dup(snk)


def host_rotk():
    R = np.zeros((128, 128), np.float32)
    for blk in range(2):
        b0 = blk * 64
        for i in range(32):
            R[b0 + 32 + i, b0 + i] = -1.0
            R[b0 + i, b0 + 32 + i] = 1.0
    return R


def host_tri2T():
    t = np.full((128, 128), NEG, np.float32)
    t[np.triu_indices(128)] = 0.0
    return _bf16(t)


def host_trimaskT():
    t = np.full((128, 128), NEG, np.float32)
    t[np.tril_indices(128)] = 0.0
    return _bf16(t)


def make_C_inputs(x_layer_full, x1_core, wqkv, qs, ksc, wo, lg, lb, w1, b1,
                  w2, b2, core_q, tabs):
    s0 = core_q * TOK
    xe = np.zeros((XT, D), np.float32)
    lo = s0 - MARG
    src0 = max(0, lo)
    xe[src0 - lo:XT] = x_layer_full[src0:s0 + TOK]
    w1pad = np.zeros((D, 2, HID_P), np.float32)
    w1pad[:, 0, :HID] = w1[:, :HID]
    w1pad[:, 1, :HID] = w1[:, HID:]
    w2pad = np.zeros((HID_P, D), np.float32)
    w2pad[:HID] = w2
    b1a = np.zeros((NHC * 128,), np.float32)
    b1g = np.zeros((NHC * 128,), np.float32)
    b1a[:HID] = b1[:HID]
    b1g[:HID] = b1[HID:]
    csq, snq, csk, snk = tabs
    c0 = s0 - MARG + MARG
    tsl = slice(s0, s0 + XT)
    first = core_q == 0
    ins = {
        "xET": np.ascontiguousarray(xe.T),
        "x1T": (np.ascontiguousarray(x1_core.T) if x1_core is not None
                else np.zeros((D, TOK), np.float32)),
        "wqkv": np.ascontiguousarray(wqkv),
        "wo": np.ascontiguousarray(wo),
        "w1p": w1pad, "w2p": w2pad,
        "qs128": np.ascontiguousarray(np.tile(qs, 2)[:, None]),
        "ks128": np.ascontiguousarray(np.tile(ksc, 2)[:, None]),
        "lg128": np.ascontiguousarray(lg.reshape(8, 128).T),
        "lb128": np.ascontiguousarray(lb.reshape(8, 128).T),
        "b1a128": np.ascontiguousarray(b1a.reshape(NHC, 128).T),
        "b1g128": np.ascontiguousarray(b1g.reshape(NHC, 128).T),
        "b2c": np.ascontiguousarray(b2.reshape(8, 128).T),
        "csq": np.ascontiguousarray(csq[:, tsl]),
        "snq": np.ascontiguousarray(snq[:, tsl]),
        "csk": np.ascontiguousarray(csk[:, tsl]),
        "snk": np.ascontiguousarray(snk[:, tsl]),
        "edgeneg": np.full((1, 1), NEG if first else 0.0, np.float32),
        "edgezero": np.full((128, 1), 0.0 if first else 1.0, np.float32),
        "trimask": host_trimaskT(),
        "expand2": np.ascontiguousarray(
            np.kron(np.eye(2, dtype=np.float32), np.ones((1, 64), np.float32))),
        "ones2h": np.ascontiguousarray(
            np.kron(np.eye(2, dtype=np.float32), np.ones((64, 1), np.float32))),
        "rotk": host_rotk(),
        "idb": _bf16(np.concatenate(
            [np.eye(128, dtype=np.float32),
             np.zeros((128, 384), np.float32)], 1)),
        "zid": _bf16(np.concatenate(
            [np.zeros((128, 512), np.float32),
             np.eye(128, dtype=np.float32)], 1)),
        "tri2m": host_tri2T(),
    }
    return ins


def _np_split_pair(w):
    hi = _bf16(w)
    lo = _bf16(np.asarray(w, np.float32) - np.asarray(hi, np.float32))
    return hi, lo


def build_B(dbg=False):
    p = Prog("B")
    s = Sched(p)
    nc = p.nc
    x0 = p.dram_in("x0T", [D, TOK])
    qp2 = p.dram_in("qp2", [128, H, TOK], BF16)
    qlo = p.dram_in("qlo", [64, H, TOK], BF16)
    kpB = p.dram_in("kpB", [128, H, N], BF16)
    vfB = p.dram_in("vfB", [H, N, DH])
    xpg = p.dram_in("xpg", [B, D, TR])
    wq = p.dram_in("wq", [D, D])
    wk = p.dram_in("wk", [D, D])
    wo = p.dram_in("wo", [D, D])
    kron4 = p.dram_in("kron4", [4, 128], BF16)
    esel = p.dram_in("esel", [TR, TOK // BM], BF16)
    x1o = p.dram_out("x1T", [D, TOK])
    x1pre_d = nc.dram_tensor("x1pre", [D, TOK], F32, kind="Internal").ap()
    ah_d = nc.dram_tensor("ah_d", [128, 8, 2, TOK], BF16, kind="Internal").ap()

    MYROWS = TOK // BM
    QH = 1024
    NKC = N // 128

    wfp = p.sb("wfp", [128, 8, 512])
    whi = p.sb("whi", [128, 8, 1024], BF16)
    wlo = p.sb("wlo", [128, 8, 1024], BF16)
    xps = [p.sb(f"xps{i}", [128, B * TR]) for i in range(2)]
    xphi = p.sb("xphi", [128, 8, B * TR], BF16)
    xplo = p.sb("xplo", [128, 8, B * TR], BF16)
    pev = p.sb("pev", [128, B * TR])
    pevh = p.sb("pevh", [128, B * TR], BF16)
    pevl = p.sb("pevl", [128, B * TR], BF16)
    prq = [p.sb(f"prq{i}", [128, TR], BF16) for i in range(2)]
    prka = [p.sb(f"prka{i}", [128, TR], BF16) for i in range(2)]
    prkb = [p.sb(f"prkb{i}", [64, TR], BF16) for i in range(2)]
    avg = p.sb("avg", [128, 2, TR])
    m8 = p.sb("m8", [128, 8])
    amk = p.sb("amk", [128, 2, TR], BF16)
    kronsb = p.sb("kronsb", [4, 128], BF16)
    eselsb = p.sb("eselsb", [128, 2, MYROWS], BF16)
    btl = p.sb("btl", [4, MYROWS], BF16)
    binrep = p.sb("binrep", [128, NKC, MYROWS], BF16)
    kst = [p.sb(f"kst{i}", [128, N], BF16) for i in range(2)]
    vsth = p.sb("vsth", [128, NKC, 65], BF16)
    vstl = p.sb("vstl", [128, NKC, 65], BF16)
    qst = [p.sb(f"qst{i}", [128, QH], BF16) for i in range(2)]
    qlst = [p.sb(f"qlst{i}", [64, QH], BF16) for i in range(2)]
    pexp = [p.sb(f"pexp{i}", [128, QH]) for i in range(2)]
    phi = [p.sb(f"phi{i}", [128, QH], BF16) for i in range(2)]
    plo = [p.sb(f"plo{i}", [128, QH], BF16) for i in range(2)]
    rden = p.sb("rden", [1, QH])
    rdex = p.sb("rdex", [64, QH])
    ahat = p.sb("ahat", [64, QH])
    ah_h = p.sb("ah_h", [64, QH], BF16)
    ah_l = p.sb("ah_l", [64, QH], BF16)
    ahld = [p.sb(f"ahld{i}", [128, 2, 512], BF16) for i in range(2)]
    x0c = [p.sb(f"x0c{i}", [128, 512]) for i in range(2)]
    x1p = [p.sb(f"x1p{i}", [128, 512]) for i in range(2)]
    sq = p.sb("sq", [128, 512])
    stats = p.sb("stats", [1, 2, TOK])
    rden2 = None
    ones_c = p.sb("ones_c", [128, 1])
    ones_r = p.sb("ones_r", [1, 128])

    ps1024 = [p.ps(f"ps1024_{i}", [128, 1024]) for i in range(2)]
    ps512 = [p.ps(f"ps512_{i}", [128, 512]) for i in range(4)]

    s.dma(kronsb[:], kron4[:, :], tag="ldkron")
    s.dma(eselsb[:], esel.rearrange("(it p) m -> p it m", p=128), tag="ldesel")
    s.dve("memset", ones_c[:], 1.0, tag="onesc")
    s.dve("memset", ones_r[:], 1.0, tag="onesr")
    s.dve("memset", vsth[:, :, 64:65], 1.0, tag="vstones")
    s.dve("memset", vstl[:, :, 64:65], 0.0, tag="vstzer")

    for c in range(8):
        xb = xps[c % 2]
        s.dma(xb[:].rearrange("p (b r) -> p b r", b=B),
              xpg.rearrange("b (c p) r -> p c b r", p=128)[:, c, :, :],
              tag=f"ldxp{c}")
        emit_split(s, xphi[:, c, :], xplo[:, c, :], xb[:])
    avg_first = [True]
    for wi, wsrc in enumerate((wq, wk)):
        for hf in range(2):
            cols = slice(hf * 512, (hf + 1) * 512)
            s.dma(wfp[:], wsrc[:, cols].rearrange("(c p) o -> p c o", p=128),
                  q="dgps", tag=f"ldwqk{wi}{hf}")
            emit_split(s, whi[:, :, cols], wlo[:, :, cols], wfp[:])
        if wi == 0:
            qph_d = nc.dram_tensor("qph_d", [128, 8, B * TR], BF16,
                                   kind="Internal").ap()
            qpl_d = nc.dram_tensor("qpl_d", [128, 8, B * TR], BF16,
                                   kind="Internal").ap()
        for oc in range(8):
            pp = ps512[oc % 2][:, 0:B * TR]
            ocs = slice(oc * 128, (oc + 1) * 128)
            emit_proj3(s, pp,
                       lambda c: whi[:, c, ocs], lambda c: wlo[:, c, ocs],
                       lambda c: xphi[:, c, :], lambda c: xplo[:, c, :], 8,
                       tag="poolproj")
            s.scopy(pev[:], pp, tag="poolev")
            emit_split(s, pevh[:], pevl[:], pev[:])
            if wi == 0:
                s.dma(qph_d[:, oc, :], pevh[:], q="dgps", tag="stqp_h")
                s.dma(qpl_d[:, oc, :], pevl[:], q="dgps", tag="stqp_l")
            else:
                for hh in range(2):
                    h = 2 * oc + hh
                    ri = hh % 2
                    rs_ = slice(hh * 64, (hh + 1) * 64)
                    s.dma(prka[ri][0:64, :], pevh[rs_, 0:TR], q="dgps",
                          tag="prka1")
                    s.dma(prka[ri][64:128, :], pevh[rs_, 0:TR], q="dgps",
                          tag="prka2")
                    s.dma(prkb[ri][0:64, :], pevl[rs_, 0:TR], q="dgps",
                          tag="prkb")
                    s.dma(prq[ri][0:64, :], qph_d[rs_, oc, 0:TR], q="dgps",
                          tag="prq_h")
                    s.dma(prq[ri][64:128, :], qpl_d[rs_, oc, 0:TR], q="dgps",
                          tag="prq_l")
                    for it in range(2):
                        isl = slice(it * 128, (it + 1) * 128)
                        pp2 = ps1024[it][:, 0:TR]
                        s.mm(pp2, prq[ri][:, isl], prka[ri][:, :],
                             avg_first[0], False, tag="avgA")
                        s.mm(pp2, prq[ri][0:64, isl], prkb[ri][:, :],
                             False, False, tag="avgB")
                    avg_first[0] = False
                    bsl = slice(TR, 2 * TR)
                    s.dma(prka[ri][0:64, :], pevh[rs_, bsl], q="dgps",
                          tag="prka1b")
                    s.dma(prka[ri][64:128, :], pevh[rs_, bsl], q="dgps",
                          tag="prka2b")
                    s.dma(prkb[ri][0:64, :], pevl[rs_, bsl], q="dgps",
                          tag="prkbb")
                    s.dma(prq[ri][0:64, :], qph_d[rs_, oc, bsl], q="dgps",
                          tag="prq_hb")
                    s.dma(prq[ri][64:128, :], qpl_d[rs_, oc, bsl], q="dgps",
                          tag="prq_lb")
                    last = oc == 7 and hh == 1
                    for it in range(2):
                        isl = slice(it * 128, (it + 1) * 128)
                        pp2 = ps1024[it][:, 0:TR]
                        s.mm(pp2, prq[ri][:, isl], prka[ri][:, :],
                             False, False, tag="avgAb")
                        s.mm(pp2, prq[ri][0:64, isl], prkb[ri][:, :],
                             False, last, tag="avgBb")
    for it in range(2):
        s.act(avg[:, it, :], ps1024[it][:, 0:TR], AF.Copy,
              scale=1.0 / (B * H), tag="avgev")
    for it in range(2):
        w2 = avg[:, it, :]
        for r in range(7):
            s.dve("max", m8[:], w2, tag="max8")
            if r == 6:
                s.dve("memset", m8[:, 3:8], -1e38, tag="m8pad")
            s.dve("match_replace", w2, m8[:], w2, imm_value=NEG, tag="mrep")
    s.dve("tensor_scalar", amk[:], avg[:], NEG * 0.5, scalar2=NEG,
          op0=ALU.is_gt, op1=ALU.mult, tag="is_sel")
    if dbg:
        damk = p.dram_out("dbg_amk", [128, 2, TR], BF16)
        s.dma(damk[:, :, :], amk[:], q="dgps", tag="dbgamk")
        davg = p.dram_out("dbg_avg", [128, 2, TR])
        s.dma(davg[:, :, :], avg[:], q="dgps", tag="dbgavg")
    for kc in range(NKC):
        pp = ps512[2][0:4, 0:MYROWS]
        jsl = slice(kc * 4, (kc + 1) * 4)
        for it in range(2):
            s.mm(pp, amk[:, it, jsl], eselsb[:, it, :], it == 0, it == 1,
                 tag="eselmm")
        s.scopy(btl[:], pp, tag="eselev")
        pp2 = ps512[3][:, 0:MYROWS]
        s.mm(pp2, kronsb[:], btl[:], True, True, tag="binexp")
        s.scopy(binrep[:, kc, :], pp2, tag="binev")

    for hf in range(2):
        cols = slice(hf * 512, (hf + 1) * 512)
        s.dma(wfp[:], wo[:, cols].rearrange("(c p) o -> p c o", p=128),
              q="dgps", tag=f"ldwo{hf}")
        emit_split(s, whi[:, :, cols], wlo[:, :, cols], wfp[:])

    SCALE = 1.0 / math.sqrt(DH)
    for h in range(H):
        hb = h % 2
        s.dma(kst[hb][:], kpB[:, h, :], tag="ldk")
        for vq in range(4):
            vqs = slice(vq * 16, (vq + 1) * 16)
            vstage = pexp[0][:].rearrange("p (kc d) -> p kc d", d=64)[
                :, 0:16, :]
            s.dma(vstage, vfB[h, vq * 2048:(vq + 1) * 2048, :].rearrange(
                "(kc pp) d -> pp kc d", pp=128), tag="ldvf")
            emit_split(s, vsth[:, vqs, 0:64], vstl[:, vqs, 0:64], vstage)
        for qh in range(2):
            qsl = slice(qh * QH, (qh + 1) * QH)
            qb = qh % 2
            s.dma(qst[qb][:], qp2[:, h, qsl], q="dgps", tag="ldq")
            s.dma(qlst[qb][:], qlo[:, h, qsl], q="dgps", tag="ldql")
            pvp = [ps512[0][0:65, :], ps512[1][0:65, :]]
            for kc in range(NKC):
                kb = kc % 2
                stp = ps1024[kb]
                ksl = slice(kc * 128, (kc + 1) * 128)
                for half in range(2):
                    psl = slice(half * 512, (half + 1) * 512)
                    s.mm(stp[:, psl], kst[hb][:, ksl], qst[qb][:, psl],
                         True, False, tag="qkA")
                    s.mm(stp[:, psl], kst[hb][0:64, ksl], qlst[qb][:, psl],
                         False, True, tag="qkB")
                mrow = NKC * MYROWS
                mv = AP(binrep[:].tensor, kc * MYROWS + qh * 32,
                        [[mrow, 128], [1, 32], [0, 32]])
                s.dve("tensor_tensor",
                      stp[:].rearrange("p (r j) -> p r j", j=32),
                      stp[:].rearrange("p (r j) -> p r j", j=32),
                      mv, op=ALU.add, extra_reads=[binrep[:]], tag="maskadd")
                s.act(pexp[kb][:], stp[:], AF.Exp, scale=SCALE, tag="exp")
                s.act(phi[kb][:], stp[:], AF.Exp, scale=SCALE, tag="exph")
                s.dve("tensor_sub", plo[kb][:], pexp[kb][:], phi[kb][:],
                      tag="plosub")
                for pkc in ([kc - 1] if kc > 0 else []) + (
                        [kc] if kc == NKC - 1 else []):
                    pb_ = pkc % 2
                    for half in range(2):
                        psl = slice(half * 512, (half + 1) * 512)
                        pv = pvp[half]
                        s.mm(pv, vsth[:, pkc, :], phi[pb_][:, psl],
                             pkc == 0, False, tag="pvA")
                        s.mm(pv, vsth[:, pkc, :], plo[pb_][:, psl], False,
                             False, tag="pvB")
                        s.mm(pv, vstl[:, pkc, :], phi[pb_][:, psl], False,
                             pkc == NKC - 1, tag="pvC")
            for half in range(2):
                psl = slice(half * 512, (half + 1) * 512)
                s.dve("reciprocal", rden[:, psl], pvp[half][64:65, :],
                      tag="recip")
            pp = ps1024[0][0:64, :]
            for half in range(2):
                psl = slice(half * 512, (half + 1) * 512)
                s.mm(pp[:, psl], ones_r[0:1, 0:64], rden[:, psl],
                     True, True, tag="rdex")
            s.scopy(rdex[:], pp, tag="rdexev")
            for half in range(2):
                psl = slice(half * 512, (half + 1) * 512)
                s.dve("tensor_mul", ahat[:, psl], pvp[half][0:64, :],
                      rdex[:, psl], tag="ahat")
            emit_split(s, ah_h[:], ah_l[:], ahat[:])
            hp = h // 2
            r0 = (h % 2) * 64
            s.dma(ah_d[r0:r0 + 64, hp, 0, qsl], ah_h[:], q="dgps", tag="stah")
            s.dma(ah_d[r0:r0 + 64, hp, 1, qsl], ah_l[:], q="dgps", tag="stal")

    if dbg:
        dah = p.dram_out("dbg_ah", [128, 8, 2, TOK], BF16)
        for hp_ in range(8):
            s.dma(dah[:, hp_, :, :], ah_d[:, hp_, :, :], q="dgps", tag="dbgah")
    def octile(oc):
        if oc < 4:
            return ps512[oc][:, 0:512]
        i = oc - 4
        return ps1024[i // 2][:, (i % 2) * 512:(i % 2) * 512 + 512]

    for tt in range(4):
        tb = tt % 2
        tsl = slice(tt * 512, (tt + 1) * 512)
        for hp in range(8):
            ab_ = ahld[hp % 2]
            s.dma(ab_[:], ah_d[:, hp, :, tsl], tag="ldah")
            for oc in range(8):
                pb = octile(oc)
                ocs = slice(oc * 128, (oc + 1) * 128)
                s.mm(pb, whi[:, hp, ocs], ab_[:, 0, :], hp == 0, False,
                     tag="opA")
                s.mm(pb, whi[:, hp, ocs], ab_[:, 1, :], False, False,
                     tag="opB")
                s.mm(pb, wlo[:, hp, ocs], ab_[:, 0, :], False, hp == 7,
                     tag="opC")
        psum_s = ps512[0][0:1, 0:512]
        psum_q = ps512[0][32:33, 0:512]
        for oc in range(8):
            pb = octile(oc)
            s.dma(x0c[tb][:], fm_view(x0)[:, oc, tsl], tag="ldx0")
            s.dve("tensor_add", x1p[oc % 2][:], pb, x0c[tb][:], tag="res")
            s.dma(fm_view(x1pre_d)[:, oc, tsl], x1p[oc % 2][:], q="dgps",
                  tag="stx1p")
            s.act(sq[:], x1p[oc % 2][:], AF.Square, tag="sqr")
            s.mm(psum_s, ones_c[:], x1p[oc % 2][:], oc == 0, oc == 7,
                 tag="ssum")
            s.mm(psum_q, ones_c[:], sq[:], oc == 0, oc == 7, tag="sqsum")
        s.scopy(stats[:, 0, tsl], psum_s, tag="stev_s")
        s.scopy(stats[:, 1, tsl], psum_q, tag="stev_q")

    s.act(stats[:, 0, :], stats[:, 0, :], AF.Copy, scale=1.0 / D, tag="mu")
    s.act(stats[:, 1, :], stats[:, 1, :], AF.Copy, scale=1.0 / D, tag="exx")
    for hf in range(2):
        hsl = slice(hf * QH, (hf + 1) * QH)
        s.dve("tensor_mul", rden[:], stats[:, 0, hsl], stats[:, 0, hsl],
              tag="musq")
        s.dve("tensor_sub", stats[:, 1, hsl], stats[:, 1, hsl], rden[:],
              tag="var")
    for hf in range(2):
        hsl = slice(hf * QH, (hf + 1) * QH)
        s.dve("tensor_scalar_add", stats[:, 1, hsl], stats[:, 1, hsl],
              LN_EPS, tag="vare")
        s.dve("reciprocal", rden[:], stats[:, 1, hsl], tag="vrec")
        s.act(stats[:, 1, hsl], rden[:], AF.Sqrt, tag="rsq")
    s.dve("tensor_mul", stats[:, 0, :], stats[:, 0, :], stats[:, 1, :],
          tag="murs")
    s.dve("tensor_scalar_mul", stats[:, 0, :], stats[:, 0, :], -1.0,
          tag="negm")
    for tt in range(4):
        tb = tt % 2
        tsl = slice(tt * 512, (tt + 1) * 512)
        pa = ps512[0][:, 0:512]
        pbv = ps512[1][:, 0:512]
        s.mm(pa, ones_r[:], stats[:, 1, tsl], True, True, tag="aexp")
        s.mm(pbv, ones_r[:], stats[:, 0, tsl], True, True, tag="bexp")
        for oc in range(8):
            s.dma(x1p[tb][:], fm_view(x1pre_d)[:, oc, tsl], tag="ldx1p")
            s.dve("tensor_mul", x0c[tb][:], x1p[tb][:], pa, tag="lnmul")
            s.dve("tensor_add", x0c[tb][:], x0c[tb][:], pbv, tag="lnadd")
            s.dma(fm_view(x1o)[:, oc, tsl], x0c[tb][:], q="dgps", tag="stx1")

    nc_ = s.emit()
    p.es.close()
    return nc_


TOKF = 2048
CH = 512
NCK = TOKF // CH
HD4 = 4 * D
W1N = float(HD4 * D)


def build_EF(is_F, dbg=False):
    p = Prog("F" if is_F else "E")
    s = Sched(p)
    nc = p.nc
    x2 = p.dram_in("x2T", [D, TOKF])
    wqkvg = p.dram_in("wqkvg", [D, 3 * D])
    wvp = p.dram_in("wv", [D, D])
    bqg = p.dram_in("bqg", [1, 3 * D])
    bvp = p.dram_in("bv", [1, D])
    lgA = p.dram_in("lgA", [128, 8])
    lbA = p.dram_in("lbA", [128, 8])
    idf = p.dram_in("idf", [128, 128])
    if is_F:
        preS = p.dram_in("preS", [3, D, D])
        preK = p.dram_in("preK", [3, 128, 8])
        wpj = p.dram_in("wpj", [D, D])
        bpj = p.dram_in("bpj", [1, D])
        t1m = p.dram_in("t1m", [128, 896])
        w1t = p.dram_in("w1t", [D, HD4])
        b1c = p.dram_in("b1c", [128, 32])
        w2t = p.dram_in("w2t", [HD4, D])
        b2c8 = p.dram_in("b2c8", [128, 8])
        xout = p.dram_out("xoT", [D, TOKF])
        x3D = nc.dram_tensor("x3D", [D, TOKF], F32, kind="Internal").ap()
        w1qD = nc.dram_tensor("w1qD", [D, HD4], BF16, kind="Internal").ap()
        w2qD = nc.dram_tensor("w2qD", [HD4, D], BF16, kind="Internal").ap()
        rawD = nc.dram_tensor("rawD", [2, 4, 512], F32, kind="Internal").ap()
    else:
        SD = p.dram_out("SD", [D, D])
        kprD = p.dram_out("kprD", [128, 8])

    x2c = p.sb("x2c", [128, 8, CH])
    xnh = p.sb("xnh", [128, 8, CH], BF16)
    xnl = p.sb("xnl", [128, 8, CH], BF16)
    kfm = p.sb("kfm", [128, 8, CH])
    gat = p.sb("gat", [128, 8, CH])
    vtm = p.sb("vtm", [128, 4, D])
    ktm = p.sb("ktm", [128, 4, D])
    Ssb = p.sb("Ssb", [128, 8, D])
    kpre = p.sb("kpre", [128, 8])
    kt8 = p.sb("kt8", [128, 8])
    wsh = [p.sb(f"wsh{i}", [128, 8, 128], BF16) for i in range(2)]
    wsl = [p.sb(f"wsl{i}", [128, 8, 128], BF16) for i in range(2)]
    wvh = p.sb("wvh", [128, 8, 256], BF16)
    wvl = p.sb("wvl", [128, 8, 256], BF16)
    ev = [p.sb(f"ev{i}", [128, CH]) for i in range(2)]
    ev2 = p.sb("ev2", [128, CH])
    sqv = p.sb("sqv", [128, CH])
    uA = p.sb("uA", [128, CH])
    strow = p.sb("strow", [1, 2, CH])
    r1 = p.sb("r1", [1, CH])
    r2 = p.sb("r2", [1, CH])
    rowq = p.sb("rowq", [1, CH])
    rowsc = p.sb("rowsc", [1, CH])
    brow = [p.sb(f"brow{i}", [1, 128]) for i in range(2)]
    bvrow = p.sb("bvrow", [1, 256])
    xst = p.sb("xst", [128, 4])
    xr4 = p.sb("xr4", [4, 128])
    thr = p.sb("thr", [128, 8])
    wsum = p.sb("wsum", [128, 1])
    acol = p.sb("acol", [128, 1])
    bneg = p.sb("bneg", [128, 1])
    lg_s = p.sb("lg_s", [128, 8])
    lb_s = p.sb("lb_s", [128, 8])
    idf_s = p.sb("idf_s", [128, 128])
    ones_c = p.sb("ones_c", [128, 1])
    ones_r = p.sb("ones_r", [1, 128])
    ones_t = p.sb("ones_t", [1, 512])
    if is_F:
        qfm = p.sb("qfm", [128, 8, CH])
        Pt = p.sb("Pt", [128, 4, CH])
        t1_s = p.sb("t1_s", [128, 896])
        b1_s = p.sb("b1_s", [128, 32])
        b2_s = p.sb("b2_s", [128, 8])
        sc2s = p.sb("sc2s", [128, CH])

    ps = [p.ps(f"ps{i}", [128, 512]) for i in range(8)]

    s.dma(lg_s[:], lgA[:, :], tag="ldc")
    s.dma(lb_s[:], lbA[:, :], tag="ldc")
    s.dma(idf_s[:], idf[:, :], tag="ldc")
    if is_F:
        s.dma(t1_s[:], t1m[:, :], tag="ldc")
        s.dma(b1_s[:], b1c[:, :], tag="ldc")
        s.dma(b2_s[:], b2c8[:, :], tag="ldc")
    s.dve("memset", ones_c[:], 1.0, tag="onesc")
    s.dve("memset", ones_r[:], 1.0, tag="onesr")
    s.dve("memset", bneg[:], -0.5, tag="bneg")
    s.dve("memset", ones_t[:], 1.0, tag="onest")

    Sv = SD.rearrange("(c p) v -> p c v", p=128) if not is_F else None
    if is_F:
        pv = preS.rearrange("s (c p) v -> s p c v", p=128)
        s.dma(Ssb[:], pv[0], tag="ldS0")
        for j in (1, 2):
            s.dma(vtm[:], pv[j][:, 0:4, :], tag="ldSa")
            s.dma(ktm[:], pv[j][:, 4:8, :], tag="ldSb")
            for c in range(4):
                s.dve("tensor_add", Ssb[:, c, :], Ssb[:, c, :], vtm[:, c, :],
                      tag="Sacc")
                s.dve("tensor_add", Ssb[:, 4 + c, :], Ssb[:, 4 + c, :],
                      ktm[:, c, :], tag="Sacc")
        s.dma(kpre[:], preK[0, :, :], tag="ldK0")
        for j in (1, 2):
            s.dma(kt8[:], preK[j, :, :], tag="ldKj")
            s.dve("tensor_add", kpre[:], kpre[:], kt8[:], tag="Kacc")
    else:
        s.dve("memset", Ssb[:], 0.0, tag="zS")
        s.dve("memset", kpre[:], 0.0, tag="zK")

    def emit_elu1(dst, src_psum, tmp_r, tmp_e):
        s.act(tmp_r, src_psum, AF.Relu, tag="elur")
        s.dve("tensor_sub", tmp_e, src_psum, tmp_r, tag="elus")
        s.act(tmp_e, tmp_e, AF.Exp, tag="elue")
        s.dve("tensor_add", dst, tmp_e, tmp_r, tag="elua")

    for ck in range(NCK):
        tsl = slice(ck * CH, (ck + 1) * CH)
        psum_s = ps[0][0:1, 0:512]
        psum_q = ps[0][32:33, 0:512]
        for oc in range(8):
            s.dma(x2c[:, oc, :], fm_view(x2)[:, oc, tsl], tag="ldx2")
            s.act(sqv[:], x2c[:, oc, :], AF.Square, tag="lnsq")
            s.mm(psum_s, ones_c[:], x2c[:, oc, :], oc == 0, oc == 7,
                 tag="lnsum")
            s.mm(psum_q, ones_c[:], sqv[:], oc == 0, oc == 7, tag="lnsq2")
        s.scopy(strow[:, 0, :], psum_s, tag="lnev_s")
        s.scopy(strow[:, 1, :], psum_q, tag="lnev_q")
        s.act(strow[:, 0, :], strow[:, 0, :], AF.Copy, scale=1.0 / D,
              tag="lnmu")
        s.act(strow[:, 1, :], strow[:, 1, :], AF.Copy, scale=1.0 / D,
              tag="lnex")
        s.dve("tensor_mul", r1[:], strow[:, 0, :], strow[:, 0, :], tag="lnm2")
        s.dve("tensor_sub", strow[:, 1, :], strow[:, 1, :], r1[:], tag="lnvr")
        s.dve("tensor_scalar_add", strow[:, 1, :], strow[:, 1, :], LN_EPS,
              tag="lnve")
        s.dve("reciprocal", r1[:], strow[:, 1, :], tag="lnrc")
        s.act(strow[:, 1, :], r1[:], AF.Sqrt, tag="lnrs")
        s.dve("tensor_mul", strow[:, 0, :], strow[:, 0, :], strow[:, 1, :],
              tag="lnms")
        s.dve("tensor_scalar_mul", strow[:, 0, :], strow[:, 0, :], -1.0,
              tag="lnng")
        pa = ps[1][:, 0:512]
        pb = ps[2][:, 0:512]
        s.mm(pa, ones_r[:], strow[:, 1, :], True, True, tag="lnae")
        s.mm(pb, ones_r[:], strow[:, 0, :], True, True, tag="lnbe")
        for oc in range(8):
            s.dve("tensor_mul", kfm[:, oc, :], x2c[:, oc, :], pa, tag="lnap")
        for oc in range(8):
            s.dve("tensor_add", kfm[:, oc, :], kfm[:, oc, :], pb, tag="lnad")
        for oc in range(8):
            s.dve("tensor_scalar", kfm[:, oc, :], kfm[:, oc, :],
                  lg_s[:, oc:oc + 1], scalar2=lb_s[:, oc:oc + 1],
                  op0=ALU.mult, op1=ALU.add, tag="lngb")
        for oc in range(8):
            s.dve("tensor_copy", xnh[:, oc, :], kfm[:, oc, :],
                  tag="split_hi")
        for oc in range(8):
            s.dve("tensor_sub", xnl[:, oc, :], kfm[:, oc, :], xnh[:, oc, :],
                  tag="split_lo")

        kinds = ([(0, "q"), (1, "g"), (2, "k")] if is_F
                 else [(1, "g"), (2, "k")])
        fiters = [(kd, knm, oc8) for kd, knm in kinds for oc8 in range(8)]

        def fm_ldw(i):
            kd, knm, oc8 = fiters[i]
            oc = kd * 8 + oc8
            bi = oc % 2
            wb = x2c[:, :, 256 + bi * 128:256 + (bi + 1) * 128]
            s.dma(wb, wqkvg[:, oc * 128:(oc + 1) * 128].rearrange(
                "(c p) o -> p c o", p=128), q="dgps", tag="ldwq")
            s.dma(brow[bi][:], bqg[:, oc * 128:(oc + 1) * 128],
                  q="dgps", tag="ldbq")
            emit_split(s, wsh[bi][:], wsl[bi][:], wb)

        def fm_evac(i):
            kd, knm, oc8 = fiters[i]
            bi = (kd * 8 + oc8) % 2
            pp = ps[3 + bi][:, 0:512]
            if knm == "q":
                emit_elu1(qfm[:, oc8, :], pp, sqv[:], ev[0][:])
            elif knm == "g":
                s.act(gat[:, oc8, :], pp, AF.Sigmoid, tag="sigm")
            else:
                s.dve("tensor_mul", ev[1][:], pp, gat[:, oc8, :],
                      tag="kgate")
                emit_elu1(kfm[:, oc8, :], ev[1][:], sqv[:], ev[0][:])

        for fi in range(len(fiters)):
            kd, knm, oc8 = fiters[fi]
            bi = (kd * 8 + oc8) % 2
            if fi == 0:
                fm_ldw(0)
            pp = ps[3 + bi][:, 0:512]
            emit_proj3(s, pp,
                       lambda c: wsh[bi][:, c, :],
                       lambda c: wsl[bi][:, c, :],
                       lambda c: xnh[:, c, :], lambda c: xnl[:, c, :],
                       8, start=True, stop=False, tag="prj" + knm)
            s.mm(pp, brow[bi][:], ones_t[:], False, True, tag="bfold")
            if fi + 1 < len(fiters):
                fm_ldw(fi + 1)
            if fi > 0:
                fm_evac(fi - 1)
        fm_evac(len(fiters) - 1)

        for dvq in range(4):
            vsl = slice(dvq * 256, (dvq + 1) * 256)
            s.dma(x2c[:, :, 0:256], wvp[:, vsl].rearrange(
                "(c p) o -> p c o", p=128), q="dgps", tag="ldwv")
            s.dma(bvrow[:], bvp[:, vsl], q="dgps", tag="ldbv")
            emit_split(s, wvh[:], wvl[:], x2c[:, :, 0:256])
            for jb in range(4):
                jsl = slice(jb * 128, (jb + 1) * 128)
                pp = ps[3 + jb % 2][:, 0:256]
                first = True
                for c in range(8):
                    s.mm(pp, xnh[:, c, jsl], wvh[:, c, :], first, False,
                         tag="vprj")
                    first = False
                    s.mm(pp, xnh[:, c, jsl], wvl[:, c, :], False, False,
                         tag="vprj")
                    s.mm(pp, xnl[:, c, jsl], wvh[:, c, :], False, False,
                         tag="vprj")
                s.mm(pp, ones_r[:], bvrow[:], False, True, tag="vbias")
                s.scopy(vtm[:, jb, vsl], pp, tag="vev")

        for c in range(8):
            for jb in range(4):
                pp = ps[5 + jb % 2][:, 0:128]
                s.transpose(pp, kfm[:, c, jb * 128:(jb + 1) * 128], idf_s[:])
                s.scopy(ktm[:, jb, c * 128:(c + 1) * 128], pp, tag="ktev")

        if is_F:
            for jb in range(4):
                pp = ps[3 + jb % 2][:, 0:512]
                for c in range(8):
                    s.mm(pp, kfm[:, c, jb * 128:(jb + 1) * 128],
                         qfm[:, c, :], c == 0, c == 7, tag="atmm")
                z0 = 384 - 128 * jb
                s.dve("tensor_mul", Pt[:, jb, :], pp,
                      t1_s[:, z0:z0 + 512], tag="atmask")
            pd = ps[7][0:1, 0:512]
            for jb in range(4):
                s.mm(pd, ones_c[:], Pt[:, jb, :], jb == 0, False, tag="denA")
            for c in range(8):
                s.mm(pd, kpre[:, c:c + 1], qfm[:, c, :], False, c == 7,
                     tag="denB")
            s.scopy(r1[:], pd, tag="denev")
            s.dve("tensor_scalar_add", r1[:], r1[:], DEN_EPS, tag="deneps")
            s.dve("reciprocal", r1[:], r1[:], tag="denrc")
            pdb = ps[7][:, 0:512]
            s.mm(pdb, ones_r[:], r1[:], True, True, tag="denbc")
            s.scopy(uA[:], pdb, tag="denbev")
            for dc in range(8):
                pn = ps[dc % 3][:, 0:512]
                dsl = slice(dc * 128, (dc + 1) * 128)
                for jb in range(4):
                    s.mm(pn, vtm[:, jb, dsl], Pt[:, jb, :], jb == 0, False,
                         tag="numA")
                for c in range(8):
                    s.mm(pn, Ssb[:, c, dsl], qfm[:, c, :], False, c == 7,
                         tag="numB")
                s.dve("tensor_mul", gat[:, dc, :], pn, uA[:], tag="numev")

        for c in range(8):
            for dvh in range(2):
                pp = ps[5 + dvh][:, 0:512]
                hsl = slice(dvh * 512, (dvh + 1) * 512)
                for jb in range(4):
                    s.mm(pp, ktm[:, jb, c * 128:(c + 1) * 128],
                         vtm[:, jb, hsl], jb == 0, jb == 3, tag="supd")
                s.dve("tensor_add", Ssb[:, c, hsl], Ssb[:, c, hsl], pp,
                      tag="sacc")
        s.dve("reduce_sum", kt8[:], kfm[:], axis=AX.X, tag="kinc")
        s.dve("tensor_add", kpre[:], kpre[:], kt8[:], tag="kprea")

        if is_F:
            emit_split(s, xnh[:], xnl[:], gat[:])
            psum_s2 = ps[0][0:1, 0:512]
            psum_q2 = ps[0][32:33, 0:512]
            pre3 = vtm[:].rearrange("p a b -> p (a b)")
            for oc in range(8):
                bi = oc % 2
                wb = x2c[:, :, 256 + bi * 128:256 + (bi + 1) * 128]
                s.dma(wb, wpj[:, oc * 128:(oc + 1) * 128].rearrange(
                    "(c p) o -> p c o", p=128), q="dgps", tag="ldwp")
                s.dma(brow[bi][:], bpj[:, oc * 128:(oc + 1) * 128],
                      q="dgps", tag="ldbp")
                emit_split(s, wsh[bi][:], wsl[bi][:], wb)
                pp = ps[3 + bi][:, 0:512]
                emit_proj3(s, pp,
                           lambda c: wsh[bi][:, c, :],
                           lambda c: wsl[bi][:, c, :],
                           lambda c: xnh[:, c, :], lambda c: xnl[:, c, :],
                           8, start=True, stop=False, tag="prjo")
                s.mm(pp, brow[bi][:], ones_t[:], False, True, tag="bfoldo")
                s.dma(ev2[:], fm_view(x2)[:, oc, tsl], tag="ldx2r")
                pslc = pre3[:, oc * 512:(oc + 1) * 512]
                s.dve("tensor_add", pslc, pp, ev2[:], tag="ores")
                s.act(sqv[:], pslc, AF.Square, tag="osq")
                s.mm(psum_s2, ones_c[:], pslc, oc == 0, oc == 7, tag="osum")
                s.mm(psum_q2, ones_c[:], sqv[:], oc == 0, oc == 7, tag="osq2")
            s.scopy(strow[:, 0, :], psum_s2, tag="oev_s")
            s.scopy(strow[:, 1, :], psum_q2, tag="oev_q")
            s.act(strow[:, 0, :], strow[:, 0, :], AF.Copy, scale=1.0 / D,
                  tag="omu")
            s.act(strow[:, 1, :], strow[:, 1, :], AF.Copy, scale=1.0 / D,
                  tag="oex")
            s.dve("tensor_mul", r1[:], strow[:, 0, :], strow[:, 0, :],
                  tag="om2")
            s.dve("tensor_sub", strow[:, 1, :], strow[:, 1, :], r1[:],
                  tag="ovr")
            s.dve("tensor_scalar_add", strow[:, 1, :], strow[:, 1, :],
                  LN_EPS, tag="ove")
            s.dve("reciprocal", r1[:], strow[:, 1, :], tag="orc")
            s.act(strow[:, 1, :], r1[:], AF.Sqrt, tag="ors")
            s.dve("tensor_mul", strow[:, 0, :], strow[:, 0, :],
                  strow[:, 1, :], tag="oms")
            s.dve("tensor_scalar_mul", strow[:, 0, :], strow[:, 0, :], -1.0,
                  tag="ong")
            pa2 = ps[1][:, 0:512]
            pb2 = ps[2][:, 0:512]
            s.mm(pa2, ones_r[:], strow[:, 1, :], True, True, tag="oae")
            s.mm(pb2, ones_r[:], strow[:, 0, :], True, True, tag="obe")
            for oc in range(8):
                s.dve("tensor_mul", qfm[:, oc, :],
                      pre3[:, oc * 512:(oc + 1) * 512], pa2, tag="olm")
            for oc in range(8):
                s.dve("tensor_add", qfm[:, oc, :], qfm[:, oc, :], pb2,
                      tag="ola")
            for oc in range(8):
                s.dma(fm_view(x3D)[:, oc, tsl], qfm[:, oc, :], q="dgps",
                      tag="stx3")

    if not is_F:
        for c in range(8):
            s.dma(Sv[:, c, :], Ssb[:, c, :], q="dgps", tag="stS")
        s.dma(kprD[:, :], kpre[:], q="dgps", tag="stkp")
        if dbg:
            pass
        nc_ = s.emit()
        p.es.close()
        return nc_

    if dbg:
        dx3 = p.dram_out("dbg_x3", [D, TOKF])
        for c in range(8):
            s.dma(fm_view(dx3)[:, c, :], fm_view(x3D)[:, c, :], q="dgps",
                  tag="dbgx3")

    x2f = x2c[:].rearrange("p a b -> p (a b)")
    kf_f = kfm[:].rearrange("p a b -> p (a b)")
    ga_f = gat[:].rearrange("p a b -> p (a b)")
    xh_f = xnh[:].rearrange("p a b -> p (a b)")
    Ss_f = Ssb[:].rearrange("p a b -> p (a b)")
    qf_f = qfm[:].rearrange("p a b -> p (a b)")
    w1v = w1t.rearrange("(c p) h -> p c h", p=128)
    w2v = w2t.rearrange("(c p) d -> p c d", p=128)
    w1qv = w1qD.rearrange("(c p) h -> p c h", p=128)
    w2qv = w2qD.rearrange("(c p) d -> p c d", p=128)

    for wi, (wview, nhg, wtot) in enumerate(((w1v, 8, W1N), (w2v, 32, W1N))):
        stage = (x2c[:] if wi == 0 else
                 x2f.rearrange("p (a b) -> p a b", a=4))
        s.dve("memset", wsum[:], 0.0, tag="zws")
        for g in range(8):
            if wi == 0:
                src = wview[:, :, g * 512:(g + 1) * 512]
            else:
                src = wview[:, g * 4:(g + 1) * 4, :]
            s.dma(stage, src, tag="ldwg")
            s.act(x2f[:, 0:4096], x2f[:, 0:4096], AF.Abs, accum=acol[:],
                  tag="wabs")
            s.dve("tensor_add", wsum[:], wsum[:], acol[:], tag="wsacc")
        pt = ps[0][0:1, 0:1]
        s.mm(pt, ones_c[:], wsum[:], True, True, tag="wtot")
        s.scopy(r1[0:1, 0:1], pt, tag="wtev")
        s.act(r1[0:1, 0:1], r1[0:1, 0:1], AF.Copy, scale=1.0 / wtot,
              tag="wmean")
        s.dve("tensor_scalar_max", r1[0:1, 0:1], r1[0:1, 0:1], BIT_EPS,
              tag="wclip")
        pb = ps[1][:, 0:1]
        s.mm(pb, ones_r[:], r1[0:1, 0:1], True, True, tag="wbc")
        c0 = wi * 3
        s.dve("tensor_scalar_mul", thr[:, c0:c0 + 1], pb, 0.5, tag="thrp")
        s.dve("tensor_scalar_mul", thr[:, c0 + 1:c0 + 2], pb, -0.5,
              tag="thrn")
        s.dve("tensor_scalar_mul", thr[:, c0 + 2:c0 + 3], pb, 1.0 / QB,
              tag="thrq")
        qstage = (xnh[:] if wi == 0 else
                  xh_f.rearrange("p (a b) -> p a b", a=4))
        for g in range(8):
            if wi == 0:
                src = wview[:, :, g * 512:(g + 1) * 512]
                dst = w1qv[:, :, g * 512:(g + 1) * 512]
            else:
                src = wview[:, g * 4:(g + 1) * 4, :]
                dst = w2qv[:, g * 4:(g + 1) * 4, :]
            s.dma(stage, src, tag="ldwg2")
            s.dve("tensor_scalar", kf_f[:, 0:4096], x2f[:, 0:4096],
                  thr[:, c0:c0 + 1], scalar2=None, op0=ALU.is_ge, tag="tern1")
            s.dve("tensor_scalar", ga_f[:, 0:4096], x2f[:, 0:4096],
                  thr[:, c0 + 1:c0 + 2], scalar2=None, op0=ALU.is_le,
                  tag="tern2")
            s.dve("tensor_sub", xh_f[:, 0:4096], kf_f[:, 0:4096],
                  ga_f[:, 0:4096], tag="tern3")
            s.dma(dst, qstage, q="dgps", tag="stwq")

    RCONST = [128.0, 64.0, 32.0, 16.0, 8.0, 4.0, 2.0, 1.0]

    def emit_round_group(srcs, qpsum, dsts, ybufs, ubufs, sqb):
        n = len(srcs)
        for g in range(n):
            s.dve("tensor_mul", ybufs[g], srcs[g], qpsum, tag="rny")
        for g in range(n):
            s.dve("tensor_mul", ubufs[g], srcs[g], qpsum, tag="rnu")
        for ckst in RCONST:
            for g in range(n):
                s.act(sqb[g % len(sqb)], ubufs[g], AF.Sign,
                      bias=bneg[:, 0:1], tag="rsgn")
                s.dve("scalar_tensor_tensor", ubufs[g], sqb[g % len(sqb)],
                      -ckst / 2.0, ubufs[g], op0=ALU.mult, op1=ALU.add,
                      tag="rupd")
        for g in range(n):
            s.dve("scalar_tensor_tensor", ybufs[g], ubufs[g], -1.0,
                  ybufs[g], op0=ALU.mult, op1=ALU.add, tag="rfin")
        for g in range(n):
            s.dve("tensor_scalar_add", dsts[g], ybufs[g], 0.5, tag="rcast")

    def hbuf(hc):
        if hc < 16:
            return Ss_f[:, hc * 512:(hc + 1) * 512]
        if hc < 24:
            return qf_f[:, (hc - 16) * 512:(hc - 15) * 512]
        return kf_f[:, (hc - 24) * 512:(hc - 23) * 512]

    def emit_rows(par, col0, thr_col):
        s.dve("tensor_scalar_max", xst[:], xst[:], BIT_EPS, tag="xsc")
        pr = ps[6][0:4, 0:128]
        s.mm(pr, xst[:], idf_s[:], True, True, tag="xsrow")
        s.dve("reciprocal", xr4[:], pr, tag="xsrc")
        s.dve("tensor_scalar_mul", xr4[:], xr4[:], QB, tag="xsq")
        s.dma(rawD[par, :, col0:col0 + 128], xr4[:], q="dgps", tag="strow")
        s.dma(rowq[:].rearrange("a (b c) -> a b c", b=4),
              rawD[par:par + 1, :, col0:col0 + 128], tag="ldrow")
        s.dve("tensor_scalar_mul", xr4[:], pr, thr_col, tag="xssc")
        s.dma(rawD[par, :, col0 + 128:col0 + 256], xr4[:], q="dgps",
              tag="strow2")
        s.dma(rowsc[:].rearrange("a (b c) -> a b c", b=4),
              rawD[par:par + 1, :, col0 + 128:col0 + 256], tag="ldrow2")

    w2pieces = [xnh[:, :, 0:256], xnh[:, :, 256:512], wvh[:], wvl[:]]
    for tt in range(4):
        par = tt % 2
        tsl = slice(tt * CH, (tt + 1) * CH)
        for oc in range(8):
            s.dma(x2c[:, oc, :], fm_view(x3D)[:, oc, tsl], tag="ldx3")
        s.dve("tensor_reduce", ev[0][:],
              x2c[:].rearrange("p c t -> p t c"), axis=AX.X, op=ALU.max,
              apply_absolute_value=True, tag="absr")
        for sub in range(4):
            pp = ps[5][:, 0:128]
            s.transpose(pp, ev[0][:, sub * 128:(sub + 1) * 128], idf_s[:])
            s.dve("reduce_max", xst[:, sub:sub + 1], pp, axis=AX.X,
                  tag="xsr")
        emit_rows(par, 0, thr[0:4, 2:3])
        pqb = ps[2][:, 0:512]
        psc = ps[1][:, 0:512]
        s.mm(pqb, ones_r[:], rowq[:], True, True, tag="qbc")
        s.mm(psc, ones_r[:], rowsc[:], True, True, tag="scbc")
        s.scopy(sc2s[:], psc, tag="sc1ev")
        sqb4 = [ev[0][:], ev[1][:], sqv[:], uA[:]]
        emit_round_group([x2c[:, c, :] for c in range(8)], pqb,
                         [xnh[:, c, :] for c in range(8)],
                         [Ss_f[:, g * 512:(g + 1) * 512] for g in range(8)],
                         [Ss_f[:, (8 + g) * 512:(9 + g) * 512]
                          for g in range(8)], sqb4)
        vt_f = vtm[:].rearrange("p a b -> p (a b)")
        kt_f = ktm[:].rearrange("p a b -> p (a b)")
        for hc in range(32):
            bi = hc % 2
            s.dma(wsh[bi][:], w1qv[:, :, hc * 128:(hc + 1) * 128],
                  tag="ldw1q")
            pp = ps[3 + bi][:, 0:512]
            for c in range(8):
                s.mm(pp, wsh[bi][:, c, :], xnh[:, c, :], c == 0, c == 7,
                     tag="bl1")
            s.dve("tensor_mul", ev[0][:], pp, sc2s[:], tag="dq1")
            s.act(hbuf(hc), ev[0][:], AF.Gelu, bias=b1_s[:, hc:hc + 1],
                  tag="gelu1")
        s.dve("tensor_reduce", ev[0][:],
              Ssb[:].rearrange("p c v -> p (c v)").rearrange(
                  "p (a t) -> p t a", t=512), axis=AX.X, op=ALU.max,
              apply_absolute_value=True, tag="absr2a")
        s.dve("tensor_reduce", ev2[:],
              qfm[:].rearrange("p c t -> p t c"), axis=AX.X, op=ALU.max,
              apply_absolute_value=True, tag="absr2b")
        s.dve("tensor_max", ev[0][:], ev[0][:], ev2[:], tag="absm1")
        s.dve("tensor_reduce", ev2[:],
              kfm[:].rearrange("p c t -> p t c"), axis=AX.X, op=ALU.max,
              apply_absolute_value=True, tag="absr2c")
        s.dve("tensor_max", ev[0][:], ev[0][:], ev2[:], tag="absm2")
        for sub in range(4):
            pp = ps[5][:, 0:128]
            s.transpose(pp, ev[0][:, sub * 128:(sub + 1) * 128], idf_s[:])
            s.dve("reduce_max", xst[:, sub:sub + 1], pp, axis=AX.X,
                  tag="xsr2")
        emit_rows(par, 256, thr[0:4, 5:6])
        s.mm(pqb, ones_r[:], rowq[:], True, True, tag="qbc2")
        s.mm(psc, ones_r[:], rowsc[:], True, True, tag="scbc2")
        s.scopy(sc2s[:], psc, tag="sc2ev")
        qb2s = gat[:, 0, :]
        s.scopy(qb2s, pqb, tag="qb2ev")
        for G in range(8):
            gb0 = (G % 2) * 4
            hqb = xnl[:, gb0:gb0 + 4, :]
            for pc in range(4):
                s.dma(w2pieces[pc][:, 0:4, :],
                      w2qv[:, G * 4:(G + 1) * 4, pc * 256:(pc + 1) * 256],
                      q="dgps", tag="ldw2q")
            emit_round_group([hbuf(G * 4 + g) for g in range(4)], qb2s,
                             [hqb[:, g, :] for g in range(4)],
                             [vt_f[:, g * 512:(g + 1) * 512]
                              for g in range(4)],
                             [kt_f[:, g * 512:(g + 1) * 512]
                              for g in range(4)], sqb4)
            for oc in range(8):
                pp = ps[oc][:, 0:512]
                rem = (oc % 2) * 128
                for hc8 in range(4):
                    wap = w2pieces[oc // 2][:, hc8, rem:rem + 128]
                    s.mm(pp, wap, hqb[:, hc8, :], G == 0 and hc8 == 0,
                         G == 7 and hc8 == 3, tag="bl2")
        pre4 = ktm[:].rearrange("p a b -> p (a b)")
        for oc in range(8):
            s.dve("tensor_mul", vt_f[:, oc * 512:(oc + 1) * 512],
                  ps[oc][:, 0:512], sc2s[:], tag="dq2")
        for oc in range(8):
            s.dve("tensor_scalar_add", vt_f[:, oc * 512:(oc + 1) * 512],
                  vt_f[:, oc * 512:(oc + 1) * 512], b2_s[:, oc:oc + 1],
                  tag="b2a")
        for oc in range(8):
            s.dve("tensor_add", pre4[:, oc * 512:(oc + 1) * 512],
                  vt_f[:, oc * 512:(oc + 1) * 512], x2c[:, oc, :],
                  tag="fres")
        psum_s3 = ps[0][0:1, 0:512]
        psum_q3 = ps[0][32:33, 0:512]
        for oc in range(8):
            pslc = pre4[:, oc * 512:(oc + 1) * 512]
            s.act(ev2[:], pslc, AF.Square, tag="fsq")
            s.mm(psum_s3, ones_c[:], pslc, oc == 0, oc == 7, tag="fsum")
            s.mm(psum_q3, ones_c[:], ev2[:], oc == 0, oc == 7, tag="fsq2")
        s.scopy(strow[:, 0, :], psum_s3, tag="fev_s")
        s.scopy(strow[:, 1, :], psum_q3, tag="fev_q")
        s.act(strow[:, 0, :], strow[:, 0, :], AF.Copy, scale=1.0 / D,
              tag="fmu2")
        s.act(strow[:, 1, :], strow[:, 1, :], AF.Copy, scale=1.0 / D,
              tag="fex2")
        s.dve("tensor_mul", r1[:], strow[:, 0, :], strow[:, 0, :], tag="fm22")
        s.dve("tensor_sub", strow[:, 1, :], strow[:, 1, :], r1[:], tag="fvr2")
        s.dve("tensor_scalar_add", strow[:, 1, :], strow[:, 1, :], LN_EPS,
              tag="fve2")
        s.dve("reciprocal", r1[:], strow[:, 1, :], tag="frc2")
        s.act(strow[:, 1, :], r1[:], AF.Sqrt, tag="frs2")
        s.dve("tensor_mul", strow[:, 0, :], strow[:, 0, :], strow[:, 1, :],
              tag="fms2")
        s.dve("tensor_scalar_mul", strow[:, 0, :], strow[:, 0, :], -1.0,
              tag="fng2")
        pa3 = ps[1][:, 0:512]
        pb3 = ps[2][:, 0:512]
        s.mm(pa3, ones_r[:], strow[:, 1, :], True, True, tag="fae")
        s.mm(pb3, ones_r[:], strow[:, 0, :], True, True, tag="fbe")
        for oc in range(8):
            s.dve("tensor_mul", vt_f[:, oc * 512:(oc + 1) * 512],
                  pre4[:, oc * 512:(oc + 1) * 512], pa3, tag="flm2")
        for oc in range(8):
            s.dve("tensor_add", vt_f[:, oc * 512:(oc + 1) * 512],
                  vt_f[:, oc * 512:(oc + 1) * 512], pb3, tag="fla2")
        for oc in range(8):
            s.dma(fm_view(xout)[:, oc, tsl],
                  vt_f[:, oc * 512:(oc + 1) * 512], q="dgps", tag="stxf")

    nc_ = s.emit()
    p.es.close()
    return nc_


def host_t1m():
    pq = np.arange(128)[:, None]
    ii = np.arange(896)[None, :]
    return np.ascontiguousarray((ii >= pq + 384).astype(np.float32))


def _l1_common(x2_core, w_qkv, b_qkv, w_gate, b_gate, ln_g, ln_b):
    wqkvg = np.concatenate([w_qkv[:, 0:D], w_gate, w_qkv[:, D:2 * D]], 1)
    bqg = np.concatenate([b_qkv[0:D], b_gate, b_qkv[D:2 * D]])[None]
    return {
        "x2T": np.ascontiguousarray(np.asarray(x2_core, np.float32).T),
        "wqkvg": np.ascontiguousarray(wqkvg, dtype=np.float32),
        "wv": np.ascontiguousarray(w_qkv[:, 2 * D:3 * D], dtype=np.float32),
        "bqg": np.ascontiguousarray(bqg, dtype=np.float32),
        "bv": np.ascontiguousarray(b_qkv[2 * D:3 * D][None],
                                   dtype=np.float32),
        "lgA": np.ascontiguousarray(np.asarray(ln_g, np.float32
                                               ).reshape(8, 128).T),
        "lbA": np.ascontiguousarray(np.asarray(ln_b, np.float32
                                               ).reshape(8, 128).T),
        "idf": np.eye(128, dtype=np.float32),
    }


def make_E_inputs(x2_core, w_qkv, b_qkv, w_gate, b_gate, ln_g, ln_b):
    return _l1_common(x2_core, w_qkv, b_qkv, w_gate, b_gate, ln_g, ln_b)


def make_F_inputs(x2_core, w_qkv, b_qkv, w_gate, b_gate, ln_g, ln_b,
                  w_proj, b_proj, W1, b1, W2, b2, preS, preK):
    ins = _l1_common(x2_core, w_qkv, b_qkv, w_gate, b_gate, ln_g, ln_b)
    ins.update({
        "preS": np.ascontiguousarray(preS, dtype=np.float32),
        "preK": np.ascontiguousarray(preK, dtype=np.float32),
        "wpj": np.ascontiguousarray(w_proj, dtype=np.float32),
        "bpj": np.ascontiguousarray(np.asarray(b_proj, np.float32)[None]),
        "t1m": host_t1m(),
        "w1t": np.ascontiguousarray(np.asarray(W1, np.float32).T),
        "b1c": np.ascontiguousarray(np.asarray(b1, np.float32
                                               ).reshape(32, 128).T),
        "w2t": np.ascontiguousarray(np.asarray(W2, np.float32).T),
        "b2c8": np.ascontiguousarray(np.asarray(b2, np.float32
                                                ).reshape(8, 128).T),
    })
    return ins


_NC_CACHE = {}


def _get_nc(name):
    if name not in _NC_CACHE:
        if name == "A":
            _NC_CACHE[name] = build_A()
        elif name == "B":
            _NC_CACHE[name] = build_B()
        elif name == "C":
            _NC_CACHE[name] = build_C(block_ln=False)
        elif name == "D":
            _NC_CACHE[name] = build_C(block_ln=True)
        elif name == "E":
            _NC_CACHE[name] = build_EF(False)
        elif name == "F":
            _NC_CACHE[name] = build_EF(True)
    return _NC_CACHE[name]


def _launch(name, in_maps):
    res = run_bass_kernel_spmd(_get_nc(name), in_maps,
                               core_ids=list(range(NCORES)))
    return res.results


def kernel(**inputs):
    f32c = lambda a: np.ascontiguousarray(np.asarray(a, np.float32))
    x0 = f32c(inputs["L0_attn_x"])
    WQ = f32c(inputs["L0_attn_W_Q"])
    WK = f32c(inputs["L0_attn_W_K"])
    WV = f32c(inputs["L0_attn_W_V"])
    WO = f32c(inputs["L0_attn_W_O"])
    wqkv = f32c(inputs["L0_ffn_wqkv"])
    qsc = f32c(inputs["L0_ffn_q_scale"])
    ksc = f32c(inputs["L0_ffn_k_scale"])
    wo2 = f32c(inputs["L0_ffn_wo"])
    lng = f32c(inputs["L0_ffn_ln_g"])
    lnb = f32c(inputs["L0_ffn_ln_b"])
    w1 = f32c(inputs["L0_ffn_w1"])
    b1 = f32c(inputs["L0_ffn_b1"])
    w2 = f32c(inputs["L0_ffn_w2"])
    b2 = f32c(inputs["L0_ffn_b2"])
    wqkv1 = f32c(inputs["L1_attn_w_qkv"])
    bqkv1 = f32c(inputs["L1_attn_b_qkv"])
    wg1 = f32c(inputs["L1_attn_w_gate"])
    bg1 = f32c(inputs["L1_attn_b_gate"])
    wp1 = f32c(inputs["L1_attn_w_proj"])
    bp1 = f32c(inputs["L1_attn_b_proj"])
    lng1 = f32c(inputs["L1_attn_ln_g"])
    lnb1 = f32c(inputs["L1_attn_ln_b"])
    FW1 = f32c(inputs["L1_ffn_W1"])
    Fb1 = f32c(inputs["L1_ffn_b1"])
    FW2 = f32c(inputs["L1_ffn_W2"])
    Fb2 = f32c(inputs["L1_ffn_b2"])

    cores = [(c // SEQC, c % SEQC) for c in range(NCORES)]

    insA = [{"x0T": np.ascontiguousarray(x0[b, q * TOK:(q + 1) * TOK].T),
             "wq": WQ, "wk": WK, "wv": WV} for b, q in cores]
    outA = _launch("A", insA)

    kron4 = _bf16(np.kron(np.eye(4, dtype=np.float32),
                          np.ones((1, 32), np.float32)))
    kpB = {}
    vhiB = {}
    vloB = {}
    for b in range(B):
        kpB[b] = np.ascontiguousarray(np.concatenate(
            [np.asarray(outA[b * SEQC + q]["kp"]) for q in range(SEQC)], 2))
        vf = np.concatenate(
            [np.asarray(outA[b * SEQC + q]["vfull"]) for q in range(SEQC)],
            0)
        vhiB[b] = np.ascontiguousarray(vf.transpose(1, 0, 2))
    xpg = np.stack([np.concatenate(
        [np.asarray(outA[b * SEQC + q]["xp"]) for q in range(SEQC)], 1)
        for b in range(B)]).astype(np.float32)
    insB = []
    for b, q in cores:
        qp = np.asarray(outA[b * SEQC + q]["qp"])
        es = np.zeros((TR, TOK // BM), np.float32)
        for mcol in range(TOK // BM):
            es[q * (TOK // BM) + mcol, mcol] = 1.0
        insB.append({
            "x0T": np.ascontiguousarray(x0[b, q * TOK:(q + 1) * TOK].T),
            "qp2": np.ascontiguousarray(
                np.concatenate([qp[0:64], qp[0:64]], 0)),
            "qlo": np.ascontiguousarray(qp[64:128]),
            "kpB": kpB[b], "vfB": vhiB[b],
            "xpg": xpg, "wq": WQ, "wk": WK, "wo": WO,
            "kron4": kron4, "esel": _bf16(es),
        })
    outB = _launch("B", insB)

    tabs = host_rotary_tables()
    x1full = {b: np.concatenate(
        [np.asarray(outB[b * SEQC + q]["x1T"]).T for q in range(SEQC)], 0)
        for b in range(B)}
    insC = [make_C_inputs(x1full[b], None, wqkv[0], qsc[0], ksc[0], wo2[0],
                          lng[0], lnb[0], w1[0], b1[0], w2[0], b2[0], q,
                          tabs) for b, q in cores]
    outC = _launch("C", insC)

    yfull = {b: np.concatenate(
        [np.asarray(outC[b * SEQC + q]["xoT"]).T for q in range(SEQC)], 0)
        for b in range(B)}
    insD = [make_C_inputs(yfull[b], x1full[b][q * TOK:(q + 1) * TOK],
                          wqkv[1], qsc[1], ksc[1], wo2[1], lng[1], lnb[1],
                          w1[1], b1[1], w2[1], b2[1], q, tabs)
            for b, q in cores]
    outD = _launch("D", insD)

    x2core = {c: np.asarray(outD[c]["xoT"]).T for c in range(NCORES)}
    insE = [make_E_inputs(x2core[b * SEQC + q], wqkv1, bqkv1, wg1, bg1,
                          lng1, lnb1) for b, q in cores]
    outE = _launch("E", insE)

    insF = []
    for b, q in cores:
        preS = np.zeros((3, D, D), np.float32)
        preK = np.zeros((3, 128, 8), np.float32)
        for j in range(q):
            preS[j] = np.asarray(outE[b * SEQC + j]["SD"])
            preK[j] = np.asarray(outE[b * SEQC + j]["kprD"])
        insF.append(make_F_inputs(x2core[b * SEQC + q], wqkv1, bqkv1, wg1,
                                  bg1, lng1, lnb1, wp1, bp1, FW1, Fb1, FW2,
                                  Fb2, preS, preK))
    outF = _launch("F", insF)

    out = np.stack([np.concatenate(
        [np.asarray(outF[b * SEQC + q]["xoT"]).T for q in range(SEQC)], 0)
        for b in range(B)])
    return np.ascontiguousarray(out.astype(np.float32))
